# version 16
# baseline (speedup 1.0000x reference)
import contextlib
import numpy as np
import ml_dtypes
import concourse.bass as bass
import concourse.mybir as mybir
from concourse.bass_utils import run_bass_kernel_spmd

F32 = mybir.dt.float32
BF16 = mybir.dt.bfloat16
AF = mybir.ActivationFunctionType
ALU = mybir.AluOpType
AX = mybir.AxisListType

P = 128
D = 2048
EH = 2048
L = 2048
ET = 16
DT = 16
NTT = 16
LC = 256
NCH = 8
NST = 16
SEG = LC + 1
TOKH = 1024
TTH = 8
II = 8192
EPS = 1e-5
PAIRS = [[0, 1], [2, 3], [4, 5], [6, 7]]
ENGS = ("sync", "tensor", "scalar", "vector", "gpsimd")


class Chan:
    __slots__ = ("name", "n", "h")
    def __init__(self, name):
        self.name, self.n, self.h = name, 0, None


class Builder:
    def __init__(self):
        self.ops = {e: [] for e in ENGS}
        self.chans = []
        self.pending = {e: [] for e in ENGS}

    def chan(self, name):
        c = Chan(name)
        self.chans.append(c)
        return c

    def barrier_into(self, toks):
        for e in ENGS:
            self.pending[e].extend(toks)

    def emit(self, eng, fn, waits=(), inc=None):
        if inc is not None:
            inc[0].n += inc[1]
        tok = None if inc is None else (inc[0], inc[0].n)
        waits = tuple(self.pending[eng]) + tuple(w for w in waits if w is not None)
        self.pending[eng] = []

        def f(h, fn=fn, waits=waits, inc=inc):
            seen = {}
            for c, v in waits:
                if seen.get(id(c), -1) < v:
                    seen[id(c)] = v
                    h.wait_ge(c.h, v)
            inst = fn(h)
            if inc is not None and inst is not None:
                inst.then_inc(inc[0].h, inc[1])
        self.ops[eng].append(f)
        return tok


def build_nc(stop_after=None, debug_taps=False):
    nc = bass.Bass("TRN2", target_bir_lowering=False, debug=False)

    def dram_in(name, shape, dtype=F32):
        return nc.dram_tensor(name, list(shape), dtype, kind="ExternalInput").ap()

    resid = dram_in("resid", (L, D))
    resid_ffn = dram_in("resid_ffn", (TOKH, D))
    in_wT = dram_in("in_wT", (D, EH))
    skip_wT = dram_in("skip_wT", (D, EH))
    wd1T = dram_in("wd1T", (EH, P))
    wd2T = dram_in("wd2T", (P, EH))
    wbcT = dram_in("wbcT", (EH, 32))
    convw = dram_in("convw", (P, ET * 4))
    convb = dram_in("convb", (P, ET))
    wd2b = dram_in("wd2b", (P, ET))
    alog = dram_in("alog", (P, ET * NST))
    wdvec = dram_in("wdvec", (P, ET))
    normw = dram_in("normw", (P, DT))
    out_wT = dram_in("out_wT", (2 * EH, D))
    gate_w = dram_in("gate_w", (D, 8))
    up_w = dram_in("up_w", (D, II))
    down_w = dram_in("down_w", (II, D))
    exp_w = dram_in("exp_w", (D, D))
    ln_g = dram_in("ln_g", (1, D))
    ln_b = dram_in("ln_b", (1, D))
    hsel = dram_in("hsel", (P, 2))

    out = nc.dram_tensor("out", [TOKH, D], F32, kind="ExternalOutput").ap()

    x_spill = nc.dram_tensor("x_spill", [ET, P, L], F32).ap()
    ogb_spill = nc.dram_tensor("ogb_spill", [32, P, D], BF16).ap()
    silu_spill = nc.dram_tensor("silu_spill", [ET, P, L], BF16).ap()
    xn_spill = nc.dram_tensor("xn_spill", [TTH, P, D], F32).ap()
    cc_t_in = nc.dram_tensor("cc_t_in", [160, L], F32).ap()
    cc_t_out = nc.dram_tensor("cc_t_out", [160, L], F32).ap()
    yg_in = [nc.dram_tensor(f"yg_in{c}", [EH, LC], BF16).ap() for c in range(NCH)]
    yg_out = [nc.dram_tensor(f"yg_out{c}", [2 * EH, LC], BF16).ap() for c in range(NCH)]

    taps = {}
    if debug_taps:
        def tap(name, shape, dtype=F32):
            taps[name] = nc.dram_tensor(name, list(shape), dtype, kind="ExternalOutput").ap()
        tap("tap_rnT", (DT, P, L), BF16)
        tap("tap_t", (P, L))
        tap("tap_bc", (32, L))
        tap("tap_x", (ET, P, L))
        tap("tap_delta", (NCH, P, ET * LC))
        tap("tap_yg", (NCH, P, ET * LC), BF16)
        tap("tap_ffn", (TTH, P, D))

    def sbg(name, shape, dtype):
        return nc.alloc_sbuf_tensor(name, list(shape), dtype)

    t_bf = sbg("t_bf", (P, L), BF16)
    wd1T_bf = sbg("wd1T_bf", (P, ET * P), BF16)
    wd2T_bf = sbg("wd2T_bf", (P, ET * P), BF16)
    wbcT_bf = sbg("wbcT_bf", (P, ET * 32), BF16)
    a_neg = sbg("a_neg", (P, ET * NST), F32)
    convw_sb = sbg("convw_sb", (P, ET * 4), F32)
    convb_sb = sbg("convb_sb", (P, ET), F32)
    wd2b_sb = sbg("wd2b_sb", (P, ET), F32)
    wdvec_sb = sbg("wdvec_sb", (P, ET), F32)
    normw_sb = sbg("normw_sb", (P, DT), F32)
    msel_sb = sbg("msel_sb", (P, 2), F32)
    carries = sbg("carries", (P, ET * NST), F32)
    t_acc = sbg("t_acc", (P, L), F32)
    bc_acc = sbg("bc_acc", (32, L), F32)
    wstage = [sbg(f"wstage{i}", (P, D), F32) for i in range(2)]
    wbf = [sbg(f"wbf{i}", (P, D), BF16) for i in range(4)]
    id_f32 = sbg("id_f32", (P, P), F32)
    id_bf = sbg("id_bf", (P, P), BF16)
    cb = [sbg(f"cb{i}", (P, 1), F32) for i in range(8)]

    ARENA = (nc.sbuf_bytes_remaining - 64) & ~31
    arena = nc.alloc_sbuf_tensor("arena", [P, ARENA], mybir.dt.uint8)
    arena_base = nc.lookup_mloc(arena).addr

    def place(name, shape, dtype, offset):
        per_part = int(np.prod(shape[1:])) * mybir.dt.size(dtype)
        assert offset % 32 == 0 and offset + per_part <= ARENA, (name, offset, per_part)
        return nc.alloc_sbuf_tensor_at(name, list(shape), dtype,
                                       offset=arena_base + offset)

    class Bump:
        def __init__(self, start=0): self.o = start
        def take(self, nbytes):
            r = self.o
            self.o = (self.o + nbytes + 31) & ~31
            return r

    b1 = Bump()
    rnT = place("rnT", (P, DT, L), BF16, b1.take(DT * L * 2))
    rtoff = [b1.take(D * 4) for _ in range(2)]
    resid_t = [place(f"resid_t{i}", (P, D), F32, rtoff[i]) for i in range(2)]
    xn_f32 = place("xn_f32", (P, D), F32, b1.take(D * 4))
    rnpre = place("rnpre", (P, D), BF16, b1.take(D * 2))
    xin_st = place("xin_st", (P, D + 8), F32, b1.take((D + 8) * 4))
    convacc = place("convacc", (P, D), F32, b1.take(D * 4))
    x_f32 = place("x_f32", (P, D), F32, b1.take(D * 4))
    xb_bf = place("xb_bf", (P, D), BF16, b1.take(D * 2))
    silu_st = place("silu_st", (P, D), BF16, b1.take(D * 2))
    junk = place("junk", (P, D), F32, b1.take(D * 4))
    assert b1.o <= ARENA
    castb = [place(f"castb{i}", (P, D), BF16, roff) for i, roff in enumerate(rtoff)]

    b2 = Bump()
    delta3 = place("delta3", (P, ET, LC), F32, b2.take(ET * LC * 4))
    dx3 = place("dx3", (P, ET, LC), F32, b2.take(ET * LC * 4))
    x3 = place("x3", (P, ET, LC), F32, b2.take(ET * LC * 4))
    yacc3 = place("yacc3", (P, ET, LC), F32, b2.take(ET * LC * 4))
    a_big = [place(f"a_big{i}", (P, ET * SEG), F32, b2.take(ET * SEG * 4)) for i in range(2)]
    u_big = [place(f"u_big{i}", (P, ET * SEG), F32, b2.take(ET * SEG * 4)) for i in range(2)]
    bcn = [place(f"bcn{i}", (P, 2, LC), F32, b2.take(2 * LC * 4)) for i in range(2)]
    silu_ch = place("silu_ch", (P, ET, LC), BF16, b2.take(ET * LC * 2))
    sp1_off = b2.take(LC * 4)
    sp3_off = b2.take(LC * 4)
    sp3 = [place("sp3a", (P, LC), F32, sp1_off),
           place("sp3b", (P, LC), F32, sp3_off)]
    ygf = place("ygf", (P, LC), F32, sp3_off)
    yg1 = [place(f"yg1{i}", (P, LC), BF16, sp1_off + i * (LC * 2)) for i in range(2)]
    assert b2.o <= ARENA

    b3 = Bump()
    yffn3 = place("yffn3", (P, TTH, D), F32, b3.take(TTH * D * 4))
    h2T_off = b3.take(16 * TOKH * 2)
    h2T = place("h2T", (P, 16, TOKH), BF16, h2T_off)
    xnT3 = place("xnT3", (P, DT, TOKH), BF16, b3.take(DT * TOKH * 2))
    relu_st = place("relu_st", (P, 512), F32, b3.take(512 * 4))
    moeT_st = place("moeT_st", (P, TOKH), F32, b3.take(TOKH * 4))
    logits = place("logits", (P, TTH * 8), F32, b3.take(TTH * 8 * 4))
    pair28 = place("pair28", (P, 28), F32, b3.take(28 * 4))
    scale_sb = place("scale_sb", (P, TTH), F32, b3.take(TTH * 4))
    gate_f = place("gate_f", (P, DT * 8), F32, b3.take(DT * 8 * 4))
    gate_b = place("gate_b", (P, DT * 8), BF16, b3.take(DT * 8 * 2))
    assert b3.o <= ARENA
    bh = Bump(h2T_off)
    p3res = place("p3res", (P, D), F32, bh.take(D * 4))
    p3xn = place("p3xn", (P, D), F32, bh.take(D * 4))
    p3xnb = place("p3xnb", (P, D), BF16, bh.take(D * 2))
    junk3 = place("junk3", (P, D), F32, bh.take(D * 4))
    assert bh.o <= h2T_off + 16 * TOKH * 2

    b4 = Bump(TTH * D * 4)
    yfullA = place("yfullA", (P, 32, LC), BF16, b4.take(32 * LC * 2))
    yfullB = place("yfullB", (P, 32, LC), BF16, b4.take(32 * LC * 2))
    ysel = [place(f"ysel{i}", (P, 32, LC), BF16, b4.take(32 * LC * 2)) for i in range(2)]
    attnT = place("attnT", (P, 2, LC), F32, b4.take(2 * LC * 4))
    assert b4.o <= ARENA

    pb = [nc.alloc_psum_tensor(f"pb{i}", [P, 512], F32) for i in range(4)]
    pD = nc.alloc_psum_tensor("pD", [P, 2048], F32)
    pbv = [x.ap() for x in pb]
    pbbf = [x.ap().bitcast(BF16) for x in pb]
    pDv = pD.ap()

    B = Builder()
    SY = B.chan("SY"); GPD = B.chan("GPD"); SCD = B.chan("SCD")
    PE = B.chan("PE"); SC = B.chan("SC"); VE = B.chan("VE"); GP = B.chan("GP")
    CCT = B.chan("CCT")
    CCY = [B.chan(f"CCY{c}") for c in range(NCH)]

    def dma(out_ap, in_ap, waits=(), eng="sync"):
        ch = {"gpsimd": GPD, "scalar": SCD}.get(eng, SY)
        return B.emit(eng, lambda h, o=out_ap, i=in_ap: h.dma_start(out=o, in_=i),
                      waits=waits, inc=(ch, 16))

    def pe(fn, waits=(), inc=True):
        return B.emit("tensor", fn, waits=waits, inc=(PE, 1) if inc else None)

    def sc(fn, waits=()):
        return B.emit("scalar", fn, waits=waits, inc=(SC, 1))

    def ve(fn, waits=()):
        return B.emit("vector", fn, waits=waits, inc=(VE, 1))

    def gp(fn, waits=()):
        return B.emit("gpsimd", fn, waits=waits, inc=(GP, 1))

    def now(ch):
        return (ch, ch.n)

    def snapshot():
        return [now(PE), now(SC), now(VE), now(GP), now(SY), now(GPD)]

    dma(convw_sb.ap(), convw)
    dma(convb_sb.ap(), convb)
    dma(wd2b_sb.ap(), wd2b)
    dma(wdvec_sb.ap(), wdvec)
    dma(normw_sb.ap(), normw)
    dma(msel_sb.ap(), hsel)
    dma(a_neg.ap(), alog)
    dma(wstage[0].ap()[:, 0:ET * P].rearrange("p (e r) -> p e r", e=ET),
        wd1T.rearrange("(e p) r -> p e r", p=P))
    dma(wstage[1].ap()[:, 0:ET * P], wd2T)
    t_small = now(SY)

    t_mz1 = gp(lambda h: h.memset(id_f32.ap(), 0.0))
    t_af1 = gp(lambda h: h.affine_select(out=id_f32.ap(), in_=id_f32.ap(),
                                         compare_op=ALU.not_equal, fill=1.0, base=0,
                                         pattern=[[-1, P]], channel_multiplier=1),
               waits=[t_mz1])
    t_mz2 = gp(lambda h: h.memset(id_bf.ap(), 0.0), waits=[t_af1])
    tok_id = gp(lambda h: h.affine_select(out=id_bf.ap(), in_=id_bf.ap(),
                                          compare_op=ALU.not_equal, fill=1.0, base=0,
                                          pattern=[[-1, P]], channel_multiplier=1),
                waits=[t_mz2])
    tok_wd1c = sc(lambda h: h.activation(out=wd1T_bf.ap(), in_=wstage[0].ap()[:, 0:ET * P],
                                         func=AF.Copy), waits=[t_small])
    tok_wd2c = sc(lambda h: h.activation(out=wd2T_bf.ap(), in_=wstage[1].ap()[:, 0:ET * P],
                                         func=AF.Copy), waits=[t_small])
    t_lbc = dma(wstage[0].ap()[:, 0:ET * 32].rearrange("p (e m) -> p e m", e=ET),
                wbcT.rearrange("(e p) m -> p e m", p=P),
                waits=[tok_wd1c])
    tok_wbcc = sc(lambda h: h.activation(out=wbcT_bf.ap(), in_=wstage[0].ap()[:, 0:ET * 32],
                                         func=AF.Copy), waits=[t_lbc])
    tok_aexp = sc(lambda h: h.activation(out=a_neg.ap(), in_=a_neg.ap(), func=AF.Exp),
                  waits=[t_small])
    ve(lambda h: h.tensor_single_scalar(a_neg.ap(), a_neg.ap(), -1.0, ALU.mult),
       waits=[tok_aexp])
    tok_cz = ve(lambda h: h.memset(carries.ap(), 0.0))
    tok_tz = ve(lambda h: h.memset(t_acc.ap(), 0.0))
    tok_bz = ve(lambda h: h.memset(bc_acc.ap(), 0.0))

    inv_d = 1.0 / D
    ev_bank = {}
    t_rn_prev = None
    t_tr_prev = None
    for tt in range(NTT):
        par = tt % 2
        w = [now(VE), now(SC)] if tt >= 2 else []
        t_ld = dma(resid_t[par].ap(), resid[tt * P:(tt + 1) * P, :], waits=w)
        t_sum = ve(lambda h, par=par: h.reduce_sum(cb[0].ap(), resid_t[par].ap(), axis=AX.X),
                   waits=[t_ld])
        t_nm = ve(lambda h: h.tensor_single_scalar(cb[1].ap(), cb[0].ap(), -inv_d, ALU.mult),
                  waits=[t_sum])
        t_var = sc(lambda h, par=par: h.activation(out=junk.ap(), in_=resid_t[par].ap(),
                                                   func=AF.Square, bias=cb[1].ap(),
                                                   accum_out=cb[2].ap()),
                   waits=[t_nm])
        t_v = ve(lambda h: h.tensor_scalar(cb[3].ap(), cb[2].ap(), inv_d, EPS, ALU.mult, ALU.add),
                 waits=[t_var])
        t_l1 = sc(lambda h: h.activation(out=cb[4].ap(), in_=cb[3].ap(), func=AF.Ln), waits=[t_v])
        t_rstd = sc(lambda h: h.activation(out=cb[4].ap(), in_=cb[4].ap(), func=AF.Exp, scale=-0.5),
                    waits=[t_l1])
        t_rc = ve(lambda h: h.reciprocal(cb[5].ap(), cb[3].ap()), waits=[t_l1])
        t_vv = ve(lambda h: h.tensor_single_scalar(cb[6].ap(), cb[2].ap(), inv_d, ALU.mult),
                  waits=[t_rc])
        t_msq = ve(lambda h: h.tensor_tensor(cb[6].ap(), cb[6].ap(), cb[5].ap(), ALU.mult),
                   waits=[t_vv])
        t_t2 = ve(lambda h: h.tensor_single_scalar(cb[6].ap(), cb[6].ap(), EPS, ALU.add),
                  waits=[t_msq])
        t_l2 = sc(lambda h: h.activation(out=cb[7].ap(), in_=cb[6].ap(), func=AF.Ln), waits=[t_t2])
        t_s = sc(lambda h: h.activation(out=cb[7].ap(), in_=cb[7].ap(), func=AF.Exp, scale=-0.5),
                 waits=[t_l2])
        t_xn = ve(lambda h, par=par: h.tensor_scalar(xn_f32.ap(), resid_t[par].ap(),
                                                     cb[1].ap(), cb[4].ap(), ALU.add, ALU.mult),
                  waits=[t_rstd, t_rn_prev])
        t_rn = ve(lambda h: h.tensor_single_scalar(rnpre.ap(), xn_f32.ap(), cb[7].ap(), ALU.mult),
                  waits=[t_xn, t_s, t_tr_prev])
        t_rn_prev = t_rn
        for dt in range(DT):
            bank = dt % 2
            t_tr = pe(lambda h, dt=dt, bank=bank: h.transpose(
                          pbbf[bank][:, 0:P], rnpre.ap()[:, dt * P:(dt + 1) * P], id_bf.ap()),
                      waits=[t_rn, tok_id, ev_bank.get(bank)])
            ev_bank[bank] = sc(lambda h, dt=dt, tt=tt, bank=bank: h.activation(
                                   out=rnT.ap()[:, dt, tt * P:(tt + 1) * P],
                                   in_=pbbf[bank][:, 0:P],
                                   func=AF.Copy, scale=normw_sb.ap()[:, dt:dt + 1]),
                               waits=[t_tr])
            if dt == DT - 1:
                t_tr_prev = t_tr
    tok_p1a = now(SC)
    if debug_taps:
        for dt in range(DT):
            dma(taps["tap_rnT"][dt], rnT.ap()[:, dt, :], waits=[tok_p1a])

    castf = [junk, xn_f32]
    ps_state = {"cst": [None, None], "st": [None, None], "j": 0}

    def emit_prespill(njobs):
        for _ in range(njobs):
            j = ps_state["j"]
            if j >= 32:
                return
            ps_state["j"] = j + 1
            pj = j % 2
            wlf = [ps_state["cst"][pj]] if ps_state["cst"][pj] else [now(SC), now(VE), now(PE)]
            t_lf = dma(castf[pj].ap(), out_wT[j * P:(j + 1) * P, :], waits=wlf)
            t_cst = ve(lambda h, fv=castf[pj].ap(), bv=castb[pj].ap(): h.tensor_copy(bv, fv),
                       waits=[t_lf, ps_state["st"][pj]])
            ps_state["cst"][pj] = t_cst
            ps_state["st"][pj] = dma(ogb_spill[j], castb[pj].ap(), waits=[t_cst], eng="scalar")

    t_csk = None
    t_cin = None
    g1last = [None, None]
    g2last = [None, None]
    prev_mm5 = None
    t_e3 = None
    t_e5 = None
    t_spill_prev = None
    t_xsp_prev = None
    t_c = None
    t_zp = None
    for e in range(ET):
        par = e % 2
        t_lin = dma(wstage[1].ap().rearrange("p (a b) -> p a b", a=DT),
                    in_wT.rearrange("(dt p) e -> p dt e", p=P)[:, :, e * P:(e + 1) * P],
                    waits=[t_cin if e > 0 else tok_wd2c])
        t_cin = sc(lambda h, par=par: h.activation(out=wbf[2 + par].ap(), in_=wstage[1].ap(),
                                                   func=AF.Copy),
                   waits=[t_lin, g2last[par]])
        for tc in range(4):
            bank = 2 + tc % 2
            wmm = [t_cin, tok_p1a, ev_bank.get(bank)]
            for dt in range(DT):
                t_mm = pe(lambda h, dt=dt, bank=bank, tc=tc, par=par: h.matmul(
                              pbv[bank], wbf[2 + par].ap()[:, dt * P:(dt + 1) * P],
                              rnT.ap()[:, dt, tc * 512:(tc + 1) * 512],
                              start=(dt == 0), stop=(dt == DT - 1)),
                          waits=wmm if dt == 0 else (), inc=(dt == DT - 1))
            ev_bank[bank] = sc(lambda h, tc=tc, bank=bank: h.activation(
                                   out=xin_st.ap()[:, 8 + tc * 512:8 + (tc + 1) * 512],
                                   in_=pbv[bank], func=AF.Copy),
                               waits=[t_mm, t_c if tc <= 1 else None])
        g2last[par] = t_mm
        if e == 0:
            t_zp = ve(lambda h: h.memset(xin_st.ap()[:, 0:8], 0.0))
        t_c = ve(lambda h, e=e: h.tensor_scalar(convacc.ap(), xin_st.ap()[:, 5:5 + D],
                                                convw_sb.ap()[:, 4 * e:4 * e + 1],
                                                convb_sb.ap()[:, e:e + 1], ALU.mult, ALU.add),
                 waits=[ev_bank[2], ev_bank[3], t_zp, t_xsp_prev])
        for k in range(1, 4):
            t_c = ve(lambda h, e=e, k=k: h.scalar_tensor_tensor(
                         convacc.ap(), xin_st.ap()[:, 5 + k:5 + k + D],
                         convw_sb.ap()[:, 4 * e + k:4 * e + k + 1], convacc.ap(),
                         ALU.mult, ALU.add),
                     waits=[t_c])
        t_x = sc(lambda h: h.activation(out=x_f32.ap(), in_=convacc.ap(), func=AF.Silu),
                 waits=[t_c, t_xsp_prev])
        t_xsp = dma(x_spill[e], x_f32.ap(), waits=[t_x])
        t_xb = sc(lambda h: h.activation(out=xb_bf.ap(), in_=x_f32.ap(), func=AF.Copy),
                  waits=[t_x, prev_mm5])
        t_xsp_prev = t_xsp
        wg3 = [t_xb, tok_wd1c, t_e5 or tok_tz]
        t_mm3 = None
        for tc in range(4):
            t_mm3 = pe(lambda h, e=e, tc=tc: h.matmul(
                           pDv[:, tc * 512:(tc + 1) * 512],
                           wd1T_bf.ap()[:, e * P:(e + 1) * P],
                           xb_bf.ap()[:, tc * 512:(tc + 1) * 512],
                           start=True, stop=True),
                       waits=wg3 if tc == 0 else (), inc=(tc == 3))
        t_e3 = ve(lambda h: h.tensor_tensor(t_acc.ap(), pDv, t_acc.ap(), ALU.add),
                  waits=[t_mm3, tok_tz])
        t_mm5 = None
        for tc in range(4):
            t_mm5 = pe(lambda h, e=e, tc=tc: h.matmul(
                           pDv[0:32, tc * 512:(tc + 1) * 512],
                           wbcT_bf.ap()[:, e * 32:(e + 1) * 32],
                           xb_bf.ap()[:, tc * 512:(tc + 1) * 512],
                           start=True, stop=True),
                       waits=[t_e3, tok_wbcc] if tc == 0 else (), inc=(tc == 3))
        t_e5 = ve(lambda h: h.tensor_tensor(bc_acc.ap(), pDv[0:32, :], bc_acc.ap(), ALU.add),
                  waits=[t_mm5, tok_bz])
        prev_mm5 = t_mm5
        emit_prespill(1)
    t_d1 = dma(cc_t_in[0:128, :], t_acc.ap(), waits=[t_e3, t_e5])
    t_d2 = dma(cc_t_in[128:160, :], bc_acc.ap())
    tok_cct = B.emit("gpsimd",
                     lambda h: h.collective_compute("AllReduce", ALU.add, replica_groups=PAIRS,
                                                    ins=[cc_t_in.opt()], outs=[cc_t_out.opt()]),
                     waits=[t_d1, t_d2], inc=(CCT, 1))
    for e in range(ET):
        par = e % 2
        t_lsk = dma(wstage[0].ap().rearrange("p (a b) -> p a b", a=DT),
                    skip_wT.rearrange("(dt p) e -> p dt e", p=P)[:, :, e * P:(e + 1) * P],
                    waits=[t_csk if e > 0 else tok_wbcc])
        t_csk = sc(lambda h, par=par: h.activation(out=wbf[par].ap(), in_=wstage[0].ap(),
                                                   func=AF.Copy),
                   waits=[t_lsk, g1last[par]])
        for tc in range(4):
            bank = tc % 2
            wmm = [t_csk, tok_p1a, ev_bank.get(bank)]
            for dt in range(DT):
                t_mm = pe(lambda h, dt=dt, bank=bank, tc=tc, par=par: h.matmul(
                              pbv[bank], wbf[par].ap()[:, dt * P:(dt + 1) * P],
                              rnT.ap()[:, dt, tc * 512:(tc + 1) * 512],
                              start=(dt == 0), stop=(dt == DT - 1)),
                          waits=wmm if dt == 0 else (), inc=(dt == DT - 1))
            ev_bank[bank] = sc(lambda h, tc=tc, bank=bank: h.activation(
                                   out=silu_st.ap()[:, tc * 512:(tc + 1) * 512],
                                   in_=pbv[bank], func=AF.Silu),
                               waits=[t_mm, t_spill_prev if tc <= 1 else None])
        g1last[par] = t_mm
        t_spill_prev = dma(silu_spill[e], silu_st.ap(), waits=[ev_bank[0], ev_bank[1]])
        emit_prespill(1)

    emit_prespill(64)

    t_tb = dma(wstage[1].ap(), cc_t_out[0:128, :], waits=[tok_cct, t_cin])
    tok_tcast = sc(lambda h: h.activation(out=t_bf.ap(), in_=wstage[1].ap(), func=AF.Copy),
                   waits=[t_tb])
    if debug_taps:
        dma(taps["tap_t"], cc_t_out[0:128, :], waits=[tok_cct])
        dma(taps["tap_bc"], cc_t_out[128:160, :], waits=[tok_cct])

    if stop_after == "p1":
        return _finalize(nc, B)

    B.barrier_into(snapshot())
    tok_az = ve(lambda h: (h.memset(a_big[0].ap(), 0.0), h.memset(a_big[1].ap(), 0.0))[1])
    tok_cz2 = ve(lambda h: h.memset(carries.ap(), 0.0))

    def bc_src(n, c0, c1):
        two = cc_t_out[128 + n:145 + n:16, c0:c1]
        return two.unsqueeze(0).broadcast_to([P, 2, c1 - c0])

    u3 = [u.ap().rearrange("p (e s) -> p e s", s=SEG) for u in u_big]
    a3 = [a.ap().rearrange("p (e s) -> p e s", s=SEG) for a in a_big]
    car3 = carries.ap().rearrange("p (e m) -> p e m", m=NST)
    t_gat = None
    t_dx = None
    hist_u = [None, None]
    hist_p = [None, None]
    hist_yadd = [None, None]
    hist_exp = [None, None]
    hist_tts = [None, None]
    t_yadd = None
    t_u = None
    t_p = None
    prev_ygd0 = prev_ygd1 = None
    for c in range(NCH):
        t_delp = [None, None]
        c0, c1 = c * LC, (c + 1) * LC
        wfree = [t_gat] if t_gat else []
        t_lx = dma(x3.ap(), x_spill.rearrange("e p l -> p e l")[:, :, c0:c1], waits=wfree)
        t_ls = dma(silu_ch.ap(), silu_spill.rearrange("e p l -> p e l")[:, :, c0:c1], waits=wfree)
        t_del = None
        for e in range(ET):
            bank = e % 2
            wg4 = [tok_tcast, tok_wd2c, ev_bank.get(bank)]
            if e < 2 and t_dx:
                wg4.append(t_dx)
            t_g4 = pe(lambda h, e=e, bank=bank, c0=c0, c1=c1: h.matmul(
                          pbv[bank][:, 0:LC], wd2T_bf.ap()[:, e * P:(e + 1) * P],
                          t_bf.ap()[:, c0:c1], start=True, stop=True),
                      waits=wg4, inc=True)
            sb3 = sp3[e % 2]
            wdel_guard = ([t_delp[e % 2]] if t_delp[e % 2]
                          else [t_gat, prev_ygd0, prev_ygd1])
            t_s1 = sc(lambda h, e=e, bank=bank: h.activation(
                          out=delta3.ap()[:, e, :], in_=pbv[bank][:, 0:LC], func=AF.Relu,
                          bias=wd2b_sb.ap()[:, e:e + 1]),
                      waits=[t_g4] + ([t_gat, prev_ygd0, prev_ygd1, t_u, now(SC)]
                                      if e == 0 else []))
            t_s2 = sc(lambda h, e=e, bank=bank, sb3=sb3: h.activation(
                          out=sb3.ap(), in_=pbv[bank][:, 0:LC], func=AF.Abs,
                          bias=wd2b_sb.ap()[:, e:e + 1]),
                      waits=[t_s1] + wdel_guard)
            ev_bank[bank] = t_s2
            t_s3 = sc(lambda h, sb3=sb3: h.activation(out=sb3.ap(), in_=sb3.ap(), func=AF.Exp,
                                                      scale=-1.0),
                      waits=[t_s2])
            t_s4 = sc(lambda h, sb3=sb3: h.activation(out=sb3.ap(), in_=sb3.ap(), func=AF.Ln,
                                                      bias=1.0),
                      waits=[t_s3])
            t_del = ve(lambda h, e=e, sb3=sb3: h.tensor_tensor(delta3.ap()[:, e, :],
                                                               delta3.ap()[:, e, :], sb3.ap(),
                                                               ALU.add),
                       waits=[t_s4])
            t_delp[e % 2] = t_del
        t_dly = t_del
        t_dx = ve(lambda h: h.tensor_tensor(
                      dx3.ap().rearrange("p e l -> p (e l)"),
                      delta3.ap().rearrange("p e l -> p (e l)"),
                      x3.ap().rearrange("p e l -> p (e l)"), ALU.mult),
                  waits=[t_dly, t_lx, t_u])
        t_yz = ve(lambda h: h.memset(yacc3.ap(), 0.0), waits=[t_gat])
        bc_tok = {0: dma(bcn[0].ap(), bc_src(0, c0, c1),
                         waits=[tok_cct, hist_u[0], hist_p[0]])}
        for n in range(NST):
            pn = n % 2
            if n + 1 < NST:
                bc_tok[n + 1] = dma(bcn[(n + 1) % 2].ap(), bc_src(n + 1, c0, c1),
                                    waits=[hist_u[(n + 1) % 2], hist_p[(n + 1) % 2]])
            t_a = None
            for e in range(ET):
                wexp = [t_dly]
                if e == 0:
                    wexp.append(hist_p[pn] or tok_az)
                t_a = sc(lambda h, e=e, n=n, pn=pn: h.activation(
                             out=a_big[pn].ap()[:, e * SEG + 1:e * SEG + 1 + LC],
                             in_=delta3.ap()[:, e, :], func=AF.Exp,
                             scale=a_neg.ap()[:, e * NST + n:e * NST + n + 1]),
                         waits=wexp)
            hist_exp[pn] = t_a
            t_u = ve(lambda h, pn=pn: h.tensor_tensor(
                         u3[pn][:, :, 1:SEG], dx3.ap(),
                         bcn[pn].ap()[:, 0:1, :].broadcast_to([P, ET, LC]), ALU.mult),
                     waits=[t_dx, bc_tok[n], hist_yadd[pn]])
            hist_u[pn] = t_u
            t_cc = ve(lambda h, n=n, pn=pn: h.tensor_copy(u3[pn][:, :, 0:1], car3[:, :, n:n + 1]),
                      waits=[t_u, tok_cz2])
            t_h = ve(lambda h, pn=pn: h.tensor_tensor_scan(a_big[pn].ap(), a_big[pn].ap(),
                                                           u_big[pn].ap(), 0.0,
                                                           ALU.mult, ALU.add),
                     waits=[t_cc, t_a, tok_az])
            hist_tts[pn] = t_h
            t_nc = ve(lambda h, n=n, pn=pn: h.tensor_copy(car3[:, :, n:n + 1],
                                                          a3[pn][:, :, SEG - 1:SEG]),
                      waits=[t_h])
            t_p = ve(lambda h, pn=pn: h.tensor_tensor(
                         u3[pn][:, :, 1:SEG], a3[pn][:, :, 1:SEG],
                         bcn[pn].ap()[:, 1:2, :].broadcast_to([P, ET, LC]), ALU.mult),
                     waits=[t_nc])
            hist_p[pn] = t_p
            t_yadd = ve(lambda h, pn=pn: h.tensor_tensor(yacc3.ap(), yacc3.ap(),
                                                         u3[pn][:, :, 1:SEG], ALU.add),
                        waits=[t_p, t_yz])
            hist_yadd[pn] = t_yadd
        t_g2 = [None, None]
        t_ygd = [None, None]
        for e in range(ET):
            pg = e % 2
            t_g1 = ve(lambda h, e=e: h.scalar_tensor_tensor(
                          ygf.ap(), x3.ap()[:, e, :], wdvec_sb.ap()[:, e:e + 1],
                          yacc3.ap()[:, e, :], ALU.mult, ALU.add),
                      waits=([t_ls, t_yadd] if e == 0 else []) + [t_ygd[pg]])
            t_g2[pg] = ve(lambda h, e=e, pg=pg: h.tensor_tensor(yg1[pg].ap(), ygf.ap(),
                                                                silu_ch.ap()[:, e, :], ALU.mult),
                          waits=[t_g1])
            t_ygd[pg] = dma(yg_in[c][e * P:(e + 1) * P, :], yg1[pg].ap(), waits=[t_g2[pg]])
        t_gat = t_g2[1]
        prev_ygd0, prev_ygd1 = t_ygd[0], t_ygd[1]
        B.emit("gpsimd",
               lambda h, c=c: h.collective_compute("AllGather", ALU.bypass, replica_groups=PAIRS,
                                                   ins=[yg_in[c].opt()], outs=[yg_out[c].opt()]),
               waits=[t_ygd[0], t_ygd[1]], inc=(CCY[c], 1))
        if debug_taps:
            dma(taps["tap_delta"][c], delta3.ap().rearrange("p e l -> p (e l)"), waits=[t_dx])

    if stop_after == "p2":
        return _finalize(nc, B)

    B.barrier_into(snapshot())
    ev_bank = {}
    ev2 = {}
    t_prev = None
    t_xb = None
    t_spx = None
    for tt in range(TTH):
        t_ld = dma(p3res.ap(), resid_ffn[tt * P:(tt + 1) * P, :],
                   waits=[t_prev, now(SC), now(VE)] if t_prev else [])
        t_sum = ve(lambda h: h.reduce_sum(cb[0].ap(), p3res.ap(), axis=AX.X), waits=[t_ld])
        t_nm = ve(lambda h: h.tensor_single_scalar(cb[1].ap(), cb[0].ap(), -inv_d, ALU.mult),
                  waits=[t_sum])
        t_var = sc(lambda h: h.activation(out=junk3.ap(), in_=p3res.ap(), func=AF.Square,
                                          bias=cb[1].ap(), accum_out=cb[2].ap()),
                   waits=[t_nm])
        t_v = ve(lambda h: h.tensor_scalar(cb[3].ap(), cb[2].ap(), inv_d, EPS, ALU.mult, ALU.add),
                 waits=[t_var])
        t_l1 = sc(lambda h: h.activation(out=cb[4].ap(), in_=cb[3].ap(), func=AF.Ln), waits=[t_v])
        t_rstd = sc(lambda h: h.activation(out=cb[4].ap(), in_=cb[4].ap(), func=AF.Exp, scale=-0.5),
                    waits=[t_l1])
        t_xn = ve(lambda h: h.tensor_scalar(p3xn.ap(), p3res.ap(), cb[1].ap(), cb[4].ap(),
                                            ALU.add, ALU.mult),
                  waits=[t_rstd, t_xb, t_spx])
        t_spx = dma(xn_spill[tt], p3xn.ap(), waits=[t_xn])
        t_xb = sc(lambda h: h.activation(out=p3xnb.ap(), in_=p3xn.ap(), func=AF.Copy),
                  waits=[t_xn])
        for dt in range(DT):
            bank = dt % 2
            t_tr = pe(lambda h, dt=dt, bank=bank: h.transpose(
                          pbbf[bank][:, 0:P], p3xnb.ap()[:, dt * P:(dt + 1) * P], id_bf.ap()),
                      waits=[t_xb, tok_id, ev_bank.get(bank)])
            ev_bank[bank] = sc(lambda h, dt=dt, tt=tt, bank=bank: h.activation(
                                   out=xnT3.ap()[:, dt, tt * P:(tt + 1) * P],
                                   in_=pbbf[bank][:, 0:P], func=AF.Copy),
                               waits=[t_tr])
        t_prev = ev_bank[1]
    tok_xnT = now(SC)

    t_lg = dma(gate_f.ap().rearrange("p (a b) -> p a b", a=DT),
               gate_w.rearrange("(dt p) m -> p dt m", p=P))
    tok_gc = sc(lambda h: h.activation(out=gate_b.ap(), in_=gate_f.ap(), func=AF.Copy),
                waits=[t_lg])
    ev0 = None
    t_sc = None
    for tt in range(TTH):
        t_mmg = None
        for dt in range(DT):
            t_mmg = pe(lambda h, dt=dt, tt=tt: h.matmul(
                           pbv[0][:, 0:8], xnT3.ap()[:, dt, tt * P:(tt + 1) * P],
                           gate_b.ap()[:, dt * 8:(dt + 1) * 8],
                           start=(dt == 0), stop=(dt == DT - 1)),
                       waits=[tok_gc, tok_xnT, ev0, ev_bank.get(0)] if dt == 0 else (),
                       inc=(dt == DT - 1))
        ev0 = ve(lambda h, tt=tt: h.tensor_copy(logits.ap()[:, tt * 8:(tt + 1) * 8],
                                                pbv[0][:, 0:8]),
                 waits=[t_mmg])
        tpair = ev0
        off = 0
        for k in range(1, 8):
            tpair = ve(lambda h, tt=tt, k=k, off=off: h.tensor_tensor(
                           pair28.ap()[:, off:off + 8 - k],
                           logits.ap()[:, tt * 8:tt * 8 + 8 - k],
                           logits.ap()[:, tt * 8 + k:tt * 8 + 8], ALU.add),
                       waits=[tpair])
            off += 8 - k
        t_sc = ve(lambda h, tt=tt: h.reduce_max(scale_sb.ap()[:, tt:tt + 1], pair28.ap(),
                                                axis=AX.X),
                  waits=[tpair])
    tok_scale = t_sc

    wcast = [None, None]
    ev_pD = None
    exmm = [None, None]
    tr_last = None
    for ot in range(DT):
        par = ot % 2
        t_lw = dma(wstage[par].ap().rearrange("p (a b) -> p a b", a=DT),
                   exp_w.rearrange("(dt p) o -> p dt o", p=P)[:, :, ot * P:(ot + 1) * P],
                   waits=[wcast[par]])
        t_wc = sc(lambda h, par=par: h.activation(out=wbf[par].ap(), in_=wstage[par].ap(),
                                                  func=AF.Copy),
                  waits=[t_lw, exmm[par]])
        wcast[par] = t_wc
        t_mme = None
        for tc in range(2):
            for dt in range(DT):
                t_mme = pe(lambda h, dt=dt, tc=tc, par=par: h.matmul(
                               pDv[:, tc * 512:(tc + 1) * 512],
                               wbf[par].ap()[:, dt * P:(dt + 1) * P],
                               xnT3.ap()[:, dt, tc * 512:(tc + 1) * 512],
                               start=(dt == 0), stop=(dt == DT - 1)),
                           waits=[t_wc, ev_pD] if (dt == 0 and tc == 0) else (),
                           inc=(dt == DT - 1))
        exmm[par] = t_mme
        ev_pD = sc(lambda h: h.activation(out=moeT_st.ap(), in_=pDv[:, 0:1024], func=AF.Copy),
                   waits=[t_mme, tr_last])
        for tt in range(TTH):
            bank = 2 + tt % 2
            t_tr = pe(lambda h, tt=tt, bank=bank: h.transpose(
                          pbv[bank][:, 0:P], moeT_st.ap()[:, tt * P:(tt + 1) * P], id_f32.ap()),
                      waits=[ev_pD, tok_id, ev2.get(bank)])
            ev2[bank] = sc(lambda h, tt=tt, ot=ot, bank=bank: h.activation(
                               out=yffn3.ap()[:, tt, ot * P:(ot + 1) * P],
                               in_=pbv[bank][:, 0:P], func=AF.Copy,
                               scale=scale_sb.ap()[:, tt:tt + 1]),
                           waits=[t_tr, tok_scale])
            tr_last = t_tr
    tok_moe = [now(SC), now(VE)]

    ev_dn = None
    t_h2 = None
    upmm = [None, None]
    dnmm = [None, None]
    for q in range(4):
        for icq in range(16):
            ic = q * 16 + icq
            par = ic % 2
            t_lu = dma(wstage[par].ap().rearrange("p (a b) -> p a b", a=DT),
                       up_w.rearrange("(dt p) i -> p dt i", p=P)[:, :, ic * P:(ic + 1) * P],
                       waits=[wcast[par]])
            t_wc = sc(lambda h, par=par: h.activation(out=wbf[par].ap(), in_=wstage[par].ap(),
                                                      func=AF.Copy),
                      waits=[t_lu, upmm[par] if upmm[par] else exmm[par]])
            wcast[par] = t_wc
            for tc in range(2):
                bank = tc
                wmm = [t_wc, ev_bank.get(bank)]
                if q == 0 and icq == 0 and tc == 0:
                    wmm += tok_moe
                t_mmu = None
                for dt in range(DT):
                    t_mmu = pe(lambda h, dt=dt, tc=tc, par=par, bank=bank: h.matmul(
                                   pbv[bank], wbf[par].ap()[:, dt * P:(dt + 1) * P],
                                   xnT3.ap()[:, dt, tc * 512:(tc + 1) * 512],
                                   start=(dt == 0), stop=(dt == DT - 1)),
                               waits=wmm if dt == 0 else (), inc=(dt == DT - 1))
                upmm[par] = t_mmu
                t_r = sc(lambda h, bank=bank: h.activation(out=relu_st.ap(), in_=pbv[bank],
                                                           func=AF.Relu),
                         waits=[t_mmu, t_h2])
                ev_bank[bank] = t_r
                t_h2 = sc(lambda h, icq=icq, tc=tc: h.activation(
                              out=h2T.ap()[:, icq, tc * 512:(tc + 1) * 512],
                              in_=relu_st.ap(), func=AF.Square),
                          waits=[t_r, ev_dn if (q > 0 and icq == 0 and tc == 0) else None])
        tok_up = t_h2
        for dto in range(DT):
            par = dto % 2
            t_ldw = dma(wstage[par].ap().rearrange("p (a b) -> p a b", a=16),
                        down_w.rearrange("(i p) d -> p i d", p=P)[:, q * 16:(q + 1) * 16,
                                                                 dto * P:(dto + 1) * P],
                        waits=[wcast[par]])
            t_dwc = sc(lambda h, par=par: h.activation(out=wbf[2 + par].ap(), in_=wstage[par].ap(),
                                                       func=AF.Copy),
                       waits=[t_ldw, dnmm[par]])
            wcast[par] = t_dwc
            t_mmd = None
            for icq in range(16):
                for tc in range(2):
                    t_mmd = pe(lambda h, icq=icq, tc=tc, par=par: h.matmul(
                                   pDv[:, tc * 512:(tc + 1) * 512],
                                   wbf[2 + par].ap()[:, icq * P:(icq + 1) * P],
                                   h2T.ap()[:, icq, tc * 512:(tc + 1) * 512],
                                   start=(icq == 0), stop=(icq == 15)),
                               waits=[t_dwc, tok_up, ev_dn] if (icq == 0 and tc == 0) else (),
                               inc=(icq == 15))
            dnmm[par] = t_mmd
            ev_dn = sc(lambda h: h.activation(out=moeT_st.ap(), in_=pDv[:, 0:1024], func=AF.Copy),
                       waits=[t_mmd, tr_last])
            for tt in range(TTH):
                bank = 2 + tt % 2
                t_tr = pe(lambda h, tt=tt, bank=bank: h.transpose(
                              pbv[bank][:, 0:P], moeT_st.ap()[:, tt * P:(tt + 1) * P],
                              id_f32.ap()),
                          waits=[ev_dn, ev2.get(bank)])
                ev2[bank] = ve(lambda h, tt=tt, dto=dto, bank=bank: h.tensor_tensor(
                                   yffn3.ap()[:, tt, dto * P:(dto + 1) * P],
                                   pbv[bank][:, 0:P],
                                   yffn3.ap()[:, tt, dto * P:(dto + 1) * P], ALU.add),
                               waits=[t_tr])
                tr_last = t_tr
    tok_down = [now(VE), now(SC), now(PE)]

    t_lgm = dma(wstage[0].ap(), ln_g.broadcast_to([P, D]), waits=[wcast[0], wcast[1]])
    t_lbt = dma(wstage[1].ap(), ln_b.broadcast_to([P, D]), waits=[wcast[0], wcast[1]])
    t_fin = None
    t_lr = None
    t_lxn = None
    for tt in range(TTH):
        t_sum = ve(lambda h, tt=tt: h.reduce_sum(cb[0].ap(), yffn3.ap()[:, tt, :], axis=AX.X),
                   waits=tok_down if tt == 0 else [])
        t_nm = ve(lambda h: h.tensor_single_scalar(cb[1].ap(), cb[0].ap(), -inv_d, ALU.mult),
                  waits=[t_sum])
        t_var = sc(lambda h, tt=tt: h.activation(out=junk3.ap(), in_=yffn3.ap()[:, tt, :],
                                                 func=AF.Square, bias=cb[1].ap(),
                                                 accum_out=cb[2].ap()),
                   waits=[t_nm] + (tok_down if tt == 0 else []))
        t_v = ve(lambda h: h.tensor_scalar(cb[3].ap(), cb[2].ap(), inv_d, EPS, ALU.mult, ALU.add),
                 waits=[t_var])
        t_l1 = sc(lambda h: h.activation(out=cb[4].ap(), in_=cb[3].ap(), func=AF.Ln), waits=[t_v])
        t_rstd = sc(lambda h: h.activation(out=cb[4].ap(), in_=cb[4].ap(), func=AF.Exp, scale=-0.5),
                    waits=[t_l1])
        t_app = ve(lambda h, tt=tt: h.tensor_scalar(yffn3.ap()[:, tt, :], yffn3.ap()[:, tt, :],
                                                    cb[1].ap(), cb[4].ap(), ALU.add, ALU.mult),
                   waits=[t_rstd])
        t_gm = ve(lambda h, tt=tt: h.tensor_tensor(yffn3.ap()[:, tt, :], yffn3.ap()[:, tt, :],
                                                   wstage[0].ap(), ALU.mult),
                  waits=[t_app, t_lgm])
        t_bt = ve(lambda h, tt=tt: h.tensor_tensor(yffn3.ap()[:, tt, :], yffn3.ap()[:, tt, :],
                                                   wstage[1].ap(), ALU.mult_op1 if False else ALU.add),
                  waits=[t_gm, t_lbt])
        t_lr = dma(p3res.ap(), resid_ffn[tt * P:(tt + 1) * P, :],
                   waits=[t_fin] + (tok_down if tt == 0 else []))
        t_lxn = dma(p3xn.ap(), xn_spill[tt], waits=[t_fin] if t_fin else [])
        t_a1 = ve(lambda h, tt=tt: h.tensor_tensor(yffn3.ap()[:, tt, :], yffn3.ap()[:, tt, :],
                                                   p3res.ap(), ALU.add),
                  waits=[t_bt, t_lr])
        t_fin = ve(lambda h, tt=tt: h.tensor_tensor(yffn3.ap()[:, tt, :], yffn3.ap()[:, tt, :],
                                                    p3xn.ap(), ALU.add),
                   waits=[t_a1, t_lxn])
    tok_base = t_fin
    if debug_taps:
        for tt in range(TTH):
            dma(taps["tap_ffn"][tt], yffn3.ap()[:, tt, :], waits=[tok_base])

    if stop_after == "p3":
        return _finalize(nc, B)

    B.barrier_into(snapshot())
    t_selp = [None, None]
    t_cast4 = [None, None]
    g6mm = [None, None]
    ev_pb0 = {}
    tr4_last = None
    t_add_last = None
    for ps in range(2):
        sel_tok = []
        for ci in range(2):
            ch = ps * 2 + ci
            t_la = dma(yfullA.ap(), yg_out[ch].rearrange("(e p) l -> p e l", p=P),
                       waits=[(CCY[ch], 1), t_selp[0], t_selp[1]])
            t_lb = dma(yfullB.ap(), yg_out[ch + 4].rearrange("(e p) l -> p e l", p=P),
                       waits=[(CCY[ch + 4], 1)])
            t_s1 = ve(lambda h, ci=ci: h.tensor_single_scalar(
                          ysel[ci].ap().rearrange("p e l -> p (e l)"),
                          yfullA.ap().rearrange("p e l -> p (e l)"),
                          msel_sb.ap()[:, 0:1], ALU.mult),
                      waits=[t_la, now(PE)])
            t_sel = ve(lambda h, ci=ci: h.scalar_tensor_tensor(
                           ysel[ci].ap().rearrange("p e l -> p (e l)"),
                           yfullB.ap().rearrange("p e l -> p (e l)"),
                           msel_sb.ap()[:, 1:2],
                           ysel[ci].ap().rearrange("p e l -> p (e l)"),
                           ALU.mult, ALU.add),
                       waits=[t_s1, t_lb])
            t_selp[ci] = t_sel
            sel_tok.append(t_sel)
        for dt in range(DT):
            for hb in range(2):
                par = (dt * 2 + hb) % 2
                t_lo = dma(wbf[par].ap().rearrange("p (a b) -> p a b", a=DT),
                           ogb_spill.rearrange("g p d -> p g d")[:, hb * 16:(hb + 1) * 16,
                                                                dt * P:(dt + 1) * P],
                           waits=[g6mm[par]], eng=("sync" if par == 0 else "scalar"))
                for ci in range(2):
                    for eti in range(16):
                        et = hb * 16 + eti
                        t_mmo = pe(lambda h, eti=eti, et=et, par=par, ci=ci: h.matmul(
                                       pbv[ci][:, 0:LC], wbf[par].ap()[:, eti * P:(eti + 1) * P],
                                       ysel[ci].ap()[:, et, :],
                                       start=(et == 0), stop=(et == 31)),
                                   waits=([t_lo, sel_tok[ci], ev_pb0.get(ci)]
                                          if eti == 0 else ()),
                                   inc=(eti == 15))
                    g6mm[par] = t_mmo
            for ci in range(2):
                ch = ps * 2 + ci
                ev_pb0[ci] = sc(lambda h, ci=ci: h.activation(
                                    out=attnT.ap()[:, ci, :], in_=pbv[ci][:, 0:LC], func=AF.Copy),
                                waits=[g6mm[(dt * 2 + 1) % 2], g6mm[dt * 2 % 2], tr4_last])
                for k in range(2):
                    bank = 2 + k
                    t_tr = pe(lambda h, ci=ci, k=k, bank=bank: h.transpose(
                                  pbv[bank][:, 0:P], attnT.ap()[:, ci, k * P:(k + 1) * P],
                                  id_f32.ap()),
                              waits=[ev_pb0[ci], ev2.get(bank)])
                    ev2[bank] = ve(lambda h, ch=ch, k=k, dt=dt, bank=bank: h.tensor_tensor(
                                       yffn3.ap()[:, ch * 2 + k, dt * P:(dt + 1) * P],
                                       pbv[bank][:, 0:P],
                                       yffn3.ap()[:, ch * 2 + k, dt * P:(dt + 1) * P], ALU.add),
                                   waits=[t_tr, tok_base])
                    tr4_last = t_tr
        for ci in range(2):
            ch = ps * 2 + ci
            for k in range(2):
                dma(out[(ch * 256 + k * P):(ch * 256 + (k + 1) * P), :],
                    yffn3.ap()[:, ch * 2 + k, :], waits=[ev2[2], ev2[3]])

    return _finalize(nc, B)


def _finalize(nc, B):
    B.emit("sync", lambda h: None, waits=[(c, c.n) for c in B.chans if c.n > 0])
    with contextlib.ExitStack() as st:
        for c in B.chans:
            c.h = st.enter_context(nc.semaphore(c.name))
        block = st.enter_context(nc.Block())
        for eng in ENGS:
            def mk(eng):
                def emit_all(h):
                    for f in B.ops[eng]:
                        f(h)
                return emit_all
            getattr(block, eng)(mk(eng))
    return nc


def _prep_core(inputs, b, hf):
    asc = np.ascontiguousarray
    eh = slice(hf * EH, (hf + 1) * EH)
    f32 = np.float32
    resid = np.asarray(inputs["L0_attn_resid"], f32)
    pmaj = lambda v: asc(np.asarray(v, f32).reshape(-1, P).T)
    m = {
        "resid": asc(resid[b]),
        "resid_ffn": asc(resid[b, hf * TOKH:(hf + 1) * TOKH]),
        "in_wT": asc(np.asarray(inputs["L0_attn_in_w"], f32)[eh].T),
        "skip_wT": asc(np.asarray(inputs["L0_attn_skip_w"], f32)[eh].T),
        "wd1T": asc(np.asarray(inputs["L0_attn_wd1"], f32)[:, eh].T),
        "wd2T": asc(np.asarray(inputs["L0_attn_wd2"], f32)[eh].T),
        "wbcT": asc(np.concatenate([np.asarray(inputs["L0_attn_wB"], f32)[:, eh],
                                    np.asarray(inputs["L0_attn_wC"], f32)[:, eh]], 0).T),
        "convw": asc(np.asarray(inputs["L0_attn_conv_w"], f32)[eh]
                     .reshape(ET, P, 4).transpose(1, 0, 2).reshape(P, ET * 4)),
        "convb": pmaj(np.asarray(inputs["L0_attn_conv_b"], f32)[eh]),
        "wd2b": pmaj(np.asarray(inputs["L0_attn_wd2_b"], f32)[eh]),
        "alog": asc(np.asarray(inputs["L0_attn_A_log"], f32)[eh]
                    .reshape(ET, P, NST).transpose(1, 0, 2).reshape(P, ET * NST)),
        "wdvec": pmaj(np.asarray(inputs["L0_attn_W_D"], f32)[eh]),
        "normw": pmaj(np.asarray(inputs["L0_attn_norm_w"], f32)),
        "out_wT": asc(np.asarray(inputs["L0_attn_out_w"], f32).T),
        "gate_w": asc(np.asarray(inputs["L0_ffn_gate_w"], f32)),
        "up_w": asc(np.asarray(inputs["L0_ffn_up_w"], f32)),
        "down_w": asc(np.asarray(inputs["L0_ffn_down_w"], f32)),
        "exp_w": asc(np.asarray(inputs["L0_ffn_expert_weight"], f32)),
        "ln_g": asc(np.asarray(inputs["L0_ffn_ln_gamma"], f32).reshape(1, D)),
        "ln_b": asc(np.asarray(inputs["L0_ffn_ln_beta"], f32).reshape(1, D)),
        "hsel": asc(np.broadcast_to(
            np.array([1.0, 0.0] if hf == 0 else [0.0, 1.0], f32), (P, 2)).copy()),
    }
    return m


def kernel(**inputs):
    nc = build_nc()
    in_maps = [_prep_core(inputs, c // 2, c % 2) for c in range(8)]
    res = run_bass_kernel_spmd(nc, in_maps, core_ids=list(range(8)))
    outp = np.zeros((4, L, D), np.float32)
    for c in range(8):
        b, hf = c // 2, c % 2
        outp[b, hf * TOKH:(hf + 1) * TOKH, :] = np.asarray(res.results[c]["out"])
    return outp


# version 17
# speedup vs baseline: 1.0333x; 1.0333x over previous
import contextlib
import numpy as np
import ml_dtypes
import concourse.bass as bass
import concourse.mybir as mybir
from concourse.bass_utils import run_bass_kernel_spmd

F32 = mybir.dt.float32
BF16 = mybir.dt.bfloat16
AF = mybir.ActivationFunctionType
ALU = mybir.AluOpType
AX = mybir.AxisListType

P = 128
D = 2048
EH = 2048
L = 2048
ET = 16
DT = 16
NTT = 16
LC = 256
NCH = 8
NST = 16
SEG = LC + 1
TOKH = 1024
TTH = 8
II = 8192
EPS = 1e-5
PAIRS = [[0, 1], [2, 3], [4, 5], [6, 7]]
ENGS = ("sync", "tensor", "scalar", "vector", "gpsimd")


class Chan:
    __slots__ = ("name", "n", "h")
    def __init__(self, name):
        self.name, self.n, self.h = name, 0, None


class Builder:
    def __init__(self):
        self.ops = {e: [] for e in ENGS}
        self.chans = []
        self.pending = {e: [] for e in ENGS}

    def chan(self, name):
        c = Chan(name)
        self.chans.append(c)
        return c

    def barrier_into(self, toks):
        for e in ENGS:
            self.pending[e].extend(toks)

    def emit(self, eng, fn, waits=(), inc=None):
        if inc is not None:
            inc[0].n += inc[1]
        tok = None if inc is None else (inc[0], inc[0].n)
        waits = tuple(self.pending[eng]) + tuple(w for w in waits if w is not None)
        self.pending[eng] = []

        def f(h, fn=fn, waits=waits, inc=inc):
            seen = {}
            for c, v in waits:
                if seen.get(id(c), -1) < v:
                    seen[id(c)] = v
                    h.wait_ge(c.h, v)
            inst = fn(h)
            if inc is not None and inst is not None:
                inst.then_inc(inc[0].h, inc[1])
        self.ops[eng].append(f)
        return tok


def build_nc(stop_after=None, debug_taps=False):
    nc = bass.Bass("TRN2", target_bir_lowering=False, debug=False)

    def dram_in(name, shape, dtype=F32):
        return nc.dram_tensor(name, list(shape), dtype, kind="ExternalInput").ap()

    resid = dram_in("resid", (L, D))
    resid_ffn = dram_in("resid_ffn", (TOKH, D))
    in_wT = dram_in("in_wT", (D, EH))
    skip_wT = dram_in("skip_wT", (D, EH))
    wd1T = dram_in("wd1T", (EH, P))
    wd2T = dram_in("wd2T", (P, EH))
    wbcT = dram_in("wbcT", (EH, 32))
    convw = dram_in("convw", (P, ET * 4))
    convb = dram_in("convb", (P, ET))
    wd2b = dram_in("wd2b", (P, ET))
    alog = dram_in("alog", (P, ET * NST))
    wdvec = dram_in("wdvec", (P, ET))
    normw = dram_in("normw", (P, DT))
    out_wT = dram_in("out_wT", (2 * EH, D))
    gate_w = dram_in("gate_w", (D, 8))
    up_w = dram_in("up_w", (D, II))
    down_w = dram_in("down_w", (II, D))
    exp_w = dram_in("exp_w", (D, D))
    ln_g = dram_in("ln_g", (1, D))
    ln_b = dram_in("ln_b", (1, D))
    hsel = dram_in("hsel", (P, 2))

    out = nc.dram_tensor("out", [TOKH, D], F32, kind="ExternalOutput").ap()

    x_spill = nc.dram_tensor("x_spill", [ET, P, L], F32).ap()
    ogb_spill = nc.dram_tensor("ogb_spill", [32, P, D], BF16).ap()
    silu_spill = nc.dram_tensor("silu_spill", [ET, P, L], BF16).ap()
    xn_spill = nc.dram_tensor("xn_spill", [TTH, P, D], F32).ap()
    cc_t_in = nc.dram_tensor("cc_t_in", [160, L], F32).ap()
    cc_t_out = nc.dram_tensor("cc_t_out", [160, L], F32).ap()
    yg_in = [nc.dram_tensor(f"yg_in{c}", [EH, LC], BF16).ap() for c in range(NCH)]
    yg_out = [nc.dram_tensor(f"yg_out{c}", [2 * EH, LC], BF16).ap() for c in range(NCH)]

    taps = {}
    if debug_taps:
        def tap(name, shape, dtype=F32):
            taps[name] = nc.dram_tensor(name, list(shape), dtype, kind="ExternalOutput").ap()
        tap("tap_rnT", (DT, P, L), BF16)
        tap("tap_t", (P, L))
        tap("tap_bc", (32, L))
        tap("tap_x", (ET, P, L))
        tap("tap_delta", (NCH, P, ET * LC))
        tap("tap_yg", (NCH, P, ET * LC), BF16)
        tap("tap_ffn", (TTH, P, D))

    def sbg(name, shape, dtype):
        return nc.alloc_sbuf_tensor(name, list(shape), dtype)

    t_bf = sbg("t_bf", (P, L), BF16)
    wd1T_bf = sbg("wd1T_bf", (P, ET * P), BF16)
    wd2T_bf = sbg("wd2T_bf", (P, ET * P), BF16)
    wbcT_bf = sbg("wbcT_bf", (P, ET * 32), BF16)
    a_neg = sbg("a_neg", (P, ET * NST), F32)
    convw_sb = sbg("convw_sb", (P, ET * 4), F32)
    convb_sb = sbg("convb_sb", (P, ET), F32)
    wd2b_sb = sbg("wd2b_sb", (P, ET), F32)
    wdvec_sb = sbg("wdvec_sb", (P, ET), F32)
    normw_sb = sbg("normw_sb", (P, DT), F32)
    msel_sb = sbg("msel_sb", (P, 2), F32)
    carries = sbg("carries", (P, ET * NST), F32)
    t_acc = sbg("t_acc", (P, L), F32)
    bc_acc = sbg("bc_acc", (32, L), F32)
    wstage = [sbg(f"wstage{i}", (P, D), F32) for i in range(2)]
    wbf = [sbg(f"wbf{i}", (P, D), BF16) for i in range(4)]
    id_f32 = sbg("id_f32", (P, P), F32)
    id_bf = sbg("id_bf", (P, P), BF16)
    cb = [sbg(f"cb{i}", (P, 1), F32) for i in range(8)]

    ARENA = (nc.sbuf_bytes_remaining - 64) & ~31
    arena = nc.alloc_sbuf_tensor("arena", [P, ARENA], mybir.dt.uint8)
    arena_base = nc.lookup_mloc(arena).addr

    def place(name, shape, dtype, offset):
        per_part = int(np.prod(shape[1:])) * mybir.dt.size(dtype)
        assert offset % 32 == 0 and offset + per_part <= ARENA, (name, offset, per_part)
        return nc.alloc_sbuf_tensor_at(name, list(shape), dtype,
                                       offset=arena_base + offset)

    class Bump:
        def __init__(self, start=0): self.o = start
        def take(self, nbytes):
            r = self.o
            self.o = (self.o + nbytes + 31) & ~31
            return r

    b1 = Bump()
    rnT = place("rnT", (P, DT, L), BF16, b1.take(DT * L * 2))
    rtoff = [b1.take(D * 4) for _ in range(2)]
    resid_t = [place(f"resid_t{i}", (P, D), F32, rtoff[i]) for i in range(2)]
    xn_f32 = place("xn_f32", (P, D), F32, b1.take(D * 4))
    rnpre = place("rnpre", (P, D), BF16, b1.take(D * 2))
    xin_st = place("xin_st", (P, D + 8), F32, b1.take((D + 8) * 4))
    convacc = place("convacc", (P, D), F32, b1.take(D * 4))
    x_f32 = place("x_f32", (P, D), F32, b1.take(D * 4))
    xb_bf = place("xb_bf", (P, D), BF16, b1.take(D * 2))
    silu_st = place("silu_st", (P, D), BF16, b1.take(D * 2))
    junk = place("junk", (P, D), F32, b1.take(D * 4))
    assert b1.o <= ARENA
    castb = [place(f"castb{i}", (P, D), BF16, roff) for i, roff in enumerate(rtoff)]

    b2 = Bump()
    delta3_off = b2.take(ET * LC * 4)
    delta3 = place("delta3", (P, ET, LC), F32, delta3_off)
    dx3 = place("dx3", (P, ET, LC), F32, b2.take(ET * LC * 4))
    x3 = place("x3", (P, ET, LC), F32, b2.take(ET * LC * 4))
    yacc3 = place("yacc3", (P, ET, LC), F32, b2.take(ET * LC * 4))
    a_big = [place(f"a_big{i}", (P, ET * SEG), F32, b2.take(ET * SEG * 4)) for i in range(2)]
    u_big = [place(f"u_big{i}", (P, ET * SEG), F32, b2.take(ET * SEG * 4)) for i in range(2)]
    bcn = [place(f"bcn{i}", (P, 2, LC), F32, b2.take(2 * LC * 4)) for i in range(2)]
    silu_ch = place("silu_ch", (P, ET, LC), BF16, b2.take(ET * LC * 2))
    sp1_off = b2.take(LC * 4)
    sp3_off = b2.take(LC * 4)
    sp3 = [place("sp3a", (P, LC), F32, sp1_off),
           place("sp3b", (P, LC), F32, sp3_off)]
    d3_off = None
    ygf = place("ygf", (P, LC), F32, delta3_off + 15 * LC * 4)
    yg1 = [place(f"yg1{i}", (P, LC), BF16, delta3_off + 14 * LC * 4 + i * (LC * 2))
           for i in range(2)]
    assert b2.o <= ARENA

    b3 = Bump()
    yffn3 = place("yffn3", (P, TTH, D), F32, b3.take(TTH * D * 4))
    h2T_off = b3.take(16 * TOKH * 2)
    h2T = place("h2T", (P, 16, TOKH), BF16, h2T_off)
    xnT3 = place("xnT3", (P, DT, TOKH), BF16, b3.take(DT * TOKH * 2))
    relu_st = place("relu_st", (P, 512), F32, b3.take(512 * 4))
    moeT_st = place("moeT_st", (P, TOKH), F32, b3.take(TOKH * 4))
    logits = place("logits", (P, TTH * 8), F32, b3.take(TTH * 8 * 4))
    pair28 = place("pair28", (P, 28), F32, b3.take(28 * 4))
    scale_sb = place("scale_sb", (P, TTH), F32, b3.take(TTH * 4))
    gate_f = place("gate_f", (P, DT * 8), F32, b3.take(DT * 8 * 4))
    gate_b = place("gate_b", (P, DT * 8), BF16, b3.take(DT * 8 * 2))
    assert b3.o <= ARENA
    bh = Bump(h2T_off)
    p3res = place("p3res", (P, D), F32, bh.take(D * 4))
    p3xn = place("p3xn", (P, D), F32, bh.take(D * 4))
    p3xnb = place("p3xnb", (P, D), BF16, bh.take(D * 2))
    junk3 = place("junk3", (P, D), F32, bh.take(D * 4))
    assert bh.o <= h2T_off + 16 * TOKH * 2

    b4 = Bump(TTH * D * 4)
    yfullA = place("yfullA", (P, 32, LC), BF16, b4.take(32 * LC * 2))
    yfullB = place("yfullB", (P, 32, LC), BF16, b4.take(32 * LC * 2))
    ysel = [place(f"ysel{i}", (P, 32, LC), BF16, b4.take(32 * LC * 2)) for i in range(2)]
    attnT = place("attnT", (P, 2, LC), F32, b4.take(2 * LC * 4))
    assert b4.o <= ARENA

    pb = [nc.alloc_psum_tensor(f"pb{i}", [P, 512], F32) for i in range(4)]
    pD = nc.alloc_psum_tensor("pD", [P, 2048], F32)
    pbv = [x.ap() for x in pb]
    pbbf = [x.ap().bitcast(BF16) for x in pb]
    pDv = pD.ap()

    B = Builder()
    SY = B.chan("SY"); GPD = B.chan("GPD"); SCD = B.chan("SCD")
    PE = B.chan("PE"); SC = B.chan("SC"); VE = B.chan("VE"); GP = B.chan("GP")
    CCT = B.chan("CCT")
    CCY = [B.chan(f"CCY{c}") for c in range(NCH)]

    def dma(out_ap, in_ap, waits=(), eng="sync"):
        ch = {"gpsimd": GPD, "scalar": SCD}.get(eng, SY)
        return B.emit(eng, lambda h, o=out_ap, i=in_ap: h.dma_start(out=o, in_=i),
                      waits=waits, inc=(ch, 16))

    def pe(fn, waits=(), inc=True):
        return B.emit("tensor", fn, waits=waits, inc=(PE, 1) if inc else None)

    def sc(fn, waits=()):
        return B.emit("scalar", fn, waits=waits, inc=(SC, 1))

    def ve(fn, waits=()):
        return B.emit("vector", fn, waits=waits, inc=(VE, 1))

    def gp(fn, waits=()):
        return B.emit("gpsimd", fn, waits=waits, inc=(GP, 1))

    def now(ch):
        return (ch, ch.n)

    def snapshot():
        return [now(PE), now(SC), now(VE), now(GP), now(SY), now(GPD)]

    dma(convw_sb.ap(), convw)
    dma(convb_sb.ap(), convb)
    dma(wd2b_sb.ap(), wd2b)
    dma(wdvec_sb.ap(), wdvec)
    dma(normw_sb.ap(), normw)
    dma(msel_sb.ap(), hsel)
    dma(a_neg.ap(), alog)
    dma(wstage[0].ap()[:, 0:ET * P].rearrange("p (e r) -> p e r", e=ET),
        wd1T.rearrange("(e p) r -> p e r", p=P))
    dma(wstage[1].ap()[:, 0:ET * P], wd2T)
    t_small = now(SY)

    t_mz1 = gp(lambda h: h.memset(id_f32.ap(), 0.0))
    t_af1 = gp(lambda h: h.affine_select(out=id_f32.ap(), in_=id_f32.ap(),
                                         compare_op=ALU.not_equal, fill=1.0, base=0,
                                         pattern=[[-1, P]], channel_multiplier=1),
               waits=[t_mz1])
    t_mz2 = gp(lambda h: h.memset(id_bf.ap(), 0.0), waits=[t_af1])
    tok_id = gp(lambda h: h.affine_select(out=id_bf.ap(), in_=id_bf.ap(),
                                          compare_op=ALU.not_equal, fill=1.0, base=0,
                                          pattern=[[-1, P]], channel_multiplier=1),
                waits=[t_mz2])
    tok_wd1c = sc(lambda h: h.activation(out=wd1T_bf.ap(), in_=wstage[0].ap()[:, 0:ET * P],
                                         func=AF.Copy), waits=[t_small])
    tok_wd2c = sc(lambda h: h.activation(out=wd2T_bf.ap(), in_=wstage[1].ap()[:, 0:ET * P],
                                         func=AF.Copy), waits=[t_small])
    t_lbc = dma(wstage[0].ap()[:, 0:ET * 32].rearrange("p (e m) -> p e m", e=ET),
                wbcT.rearrange("(e p) m -> p e m", p=P),
                waits=[tok_wd1c])
    tok_wbcc = sc(lambda h: h.activation(out=wbcT_bf.ap(), in_=wstage[0].ap()[:, 0:ET * 32],
                                         func=AF.Copy), waits=[t_lbc])
    tok_aexp = sc(lambda h: h.activation(out=a_neg.ap(), in_=a_neg.ap(), func=AF.Exp),
                  waits=[t_small])
    ve(lambda h: h.tensor_single_scalar(a_neg.ap(), a_neg.ap(), -1.0, ALU.mult),
       waits=[tok_aexp])
    tok_cz = ve(lambda h: h.memset(carries.ap(), 0.0))
    tok_tz = ve(lambda h: h.memset(t_acc.ap(), 0.0))
    tok_bz = ve(lambda h: h.memset(bc_acc.ap(), 0.0))

    inv_d = 1.0 / D
    ev_bank = {}
    t_rn_prev = None
    t_tr_prev = None
    for tt in range(NTT):
        par = tt % 2
        w = [now(VE), now(SC)] if tt >= 2 else []
        t_ld = dma(resid_t[par].ap(), resid[tt * P:(tt + 1) * P, :], waits=w)
        t_sum = ve(lambda h, par=par: h.reduce_sum(cb[0].ap(), resid_t[par].ap(), axis=AX.X),
                   waits=[t_ld])
        t_nm = ve(lambda h: h.tensor_single_scalar(cb[1].ap(), cb[0].ap(), -inv_d, ALU.mult),
                  waits=[t_sum])
        t_var = sc(lambda h, par=par: h.activation(out=junk.ap(), in_=resid_t[par].ap(),
                                                   func=AF.Square, bias=cb[1].ap(),
                                                   accum_out=cb[2].ap()),
                   waits=[t_nm])
        t_v = ve(lambda h: h.tensor_scalar(cb[3].ap(), cb[2].ap(), inv_d, EPS, ALU.mult, ALU.add),
                 waits=[t_var])
        t_l1 = sc(lambda h: h.activation(out=cb[4].ap(), in_=cb[3].ap(), func=AF.Ln), waits=[t_v])
        t_rstd = sc(lambda h: h.activation(out=cb[4].ap(), in_=cb[4].ap(), func=AF.Exp, scale=-0.5),
                    waits=[t_l1])
        t_rc = ve(lambda h: h.reciprocal(cb[5].ap(), cb[3].ap()), waits=[t_l1])
        t_vv = ve(lambda h: h.tensor_single_scalar(cb[6].ap(), cb[2].ap(), inv_d, ALU.mult),
                  waits=[t_rc])
        t_msq = ve(lambda h: h.tensor_tensor(cb[6].ap(), cb[6].ap(), cb[5].ap(), ALU.mult),
                   waits=[t_vv])
        t_t2 = ve(lambda h: h.tensor_single_scalar(cb[6].ap(), cb[6].ap(), EPS, ALU.add),
                  waits=[t_msq])
        t_l2 = sc(lambda h: h.activation(out=cb[7].ap(), in_=cb[6].ap(), func=AF.Ln), waits=[t_t2])
        t_s = sc(lambda h: h.activation(out=cb[7].ap(), in_=cb[7].ap(), func=AF.Exp, scale=-0.5),
                 waits=[t_l2])
        t_xn = ve(lambda h, par=par: h.tensor_scalar(xn_f32.ap(), resid_t[par].ap(),
                                                     cb[1].ap(), cb[4].ap(), ALU.add, ALU.mult),
                  waits=[t_rstd, t_rn_prev])
        t_rn = ve(lambda h: h.tensor_single_scalar(rnpre.ap(), xn_f32.ap(), cb[7].ap(), ALU.mult),
                  waits=[t_xn, t_s, t_tr_prev])
        t_rn_prev = t_rn
        for dt in range(DT):
            bank = dt % 2
            t_tr = pe(lambda h, dt=dt, bank=bank: h.transpose(
                          pbbf[bank][:, 0:P], rnpre.ap()[:, dt * P:(dt + 1) * P], id_bf.ap()),
                      waits=[t_rn, tok_id, ev_bank.get(bank)])
            ev_bank[bank] = sc(lambda h, dt=dt, tt=tt, bank=bank: h.activation(
                                   out=rnT.ap()[:, dt, tt * P:(tt + 1) * P],
                                   in_=pbbf[bank][:, 0:P],
                                   func=AF.Copy, scale=normw_sb.ap()[:, dt:dt + 1]),
                               waits=[t_tr])
            if dt == DT - 1:
                t_tr_prev = t_tr
    tok_p1a = now(SC)
    if debug_taps:
        for dt in range(DT):
            dma(taps["tap_rnT"][dt], rnT.ap()[:, dt, :], waits=[tok_p1a])

    castf = [junk, xn_f32]
    ps_state = {"cst": [None, None], "st": [None, None], "j": 0}

    def emit_prespill(njobs):
        for _ in range(njobs):
            j = ps_state["j"]
            if j >= 32:
                return
            ps_state["j"] = j + 1
            pj = j % 2
            wlf = [ps_state["cst"][pj]] if ps_state["cst"][pj] else [now(SC), now(VE), now(PE)]
            t_lf = dma(castf[pj].ap(), out_wT[j * P:(j + 1) * P, :], waits=wlf)
            t_cst = ve(lambda h, fv=castf[pj].ap(), bv=castb[pj].ap(): h.tensor_copy(bv, fv),
                       waits=[t_lf, ps_state["st"][pj]])
            ps_state["cst"][pj] = t_cst
            ps_state["st"][pj] = dma(ogb_spill[j], castb[pj].ap(), waits=[t_cst], eng="scalar")

    t_csk = None
    t_cin = None
    g1last = [None, None]
    g2last = [None, None]
    prev_mm5 = None
    t_e3 = None
    t_e5 = None
    t_spill_prev = None
    t_xsp_prev = None
    t_c = None
    t_zp = None
    for e in range(ET):
        par = e % 2
        t_lsk = dma(wstage[0].ap().rearrange("p (a b) -> p a b", a=DT),
                    skip_wT.rearrange("(dt p) e -> p dt e", p=P)[:, :, e * P:(e + 1) * P],
                    waits=[t_csk if e > 0 else tok_wbcc])
        t_csk = sc(lambda h, par=par: h.activation(out=wbf[par].ap(), in_=wstage[0].ap(),
                                                   func=AF.Copy),
                   waits=[t_lsk, g1last[par]])
        t_lin = dma(wstage[1].ap().rearrange("p (a b) -> p a b", a=DT),
                    in_wT.rearrange("(dt p) e -> p dt e", p=P)[:, :, e * P:(e + 1) * P],
                    waits=[t_cin if e > 0 else tok_wd2c])
        t_cin = sc(lambda h, par=par: h.activation(out=wbf[2 + par].ap(), in_=wstage[1].ap(),
                                                   func=AF.Copy),
                   waits=[t_lin, g2last[par]])

        for tc in range(4):
            bank = tc % 2
            wmm = [t_csk, tok_p1a, ev_bank.get(bank)]
            for dt in range(DT):
                t_mm = pe(lambda h, dt=dt, bank=bank, tc=tc, par=par: h.matmul(
                              pbv[bank], wbf[par].ap()[:, dt * P:(dt + 1) * P],
                              rnT.ap()[:, dt, tc * 512:(tc + 1) * 512],
                              start=(dt == 0), stop=(dt == DT - 1)),
                          waits=wmm if dt == 0 else (), inc=(dt == DT - 1))
            ev_bank[bank] = sc(lambda h, tc=tc, bank=bank: h.activation(
                                   out=silu_st.ap()[:, tc * 512:(tc + 1) * 512],
                                   in_=pbv[bank], func=AF.Silu),
                               waits=[t_mm, t_spill_prev if tc <= 1 else None])
        g1last[par] = t_mm
        t_spill_prev = dma(silu_spill[e], silu_st.ap(), waits=[ev_bank[0], ev_bank[1]])

        for tc in range(4):
            bank = 2 + tc % 2
            wmm = [t_cin, ev_bank.get(bank)]
            for dt in range(DT):
                t_mm = pe(lambda h, dt=dt, bank=bank, tc=tc, par=par: h.matmul(
                              pbv[bank], wbf[2 + par].ap()[:, dt * P:(dt + 1) * P],
                              rnT.ap()[:, dt, tc * 512:(tc + 1) * 512],
                              start=(dt == 0), stop=(dt == DT - 1)),
                          waits=wmm if dt == 0 else (), inc=(dt == DT - 1))
            ev_bank[bank] = sc(lambda h, tc=tc, bank=bank: h.activation(
                                   out=xin_st.ap()[:, 8 + tc * 512:8 + (tc + 1) * 512],
                                   in_=pbv[bank], func=AF.Copy),
                               waits=[t_mm, t_c if tc <= 1 else None])
        g2last[par] = t_mm
        if e == 0:
            t_zp = ve(lambda h: h.memset(xin_st.ap()[:, 0:8], 0.0))
        t_c = ve(lambda h, e=e: h.tensor_scalar(convacc.ap(), xin_st.ap()[:, 5:5 + D],
                                                convw_sb.ap()[:, 4 * e:4 * e + 1],
                                                convb_sb.ap()[:, e:e + 1], ALU.mult, ALU.add),
                 waits=[ev_bank[2], ev_bank[3], t_zp, t_xsp_prev])
        for k in range(1, 4):
            t_c = ve(lambda h, e=e, k=k: h.scalar_tensor_tensor(
                         convacc.ap(), xin_st.ap()[:, 5 + k:5 + k + D],
                         convw_sb.ap()[:, 4 * e + k:4 * e + k + 1], convacc.ap(),
                         ALU.mult, ALU.add),
                     waits=[t_c])
        t_x = sc(lambda h: h.activation(out=x_f32.ap(), in_=convacc.ap(), func=AF.Silu),
                 waits=[t_c, t_xsp_prev])
        t_xsp = dma(x_spill[e], x_f32.ap(), waits=[t_x])
        t_xb = sc(lambda h: h.activation(out=xb_bf.ap(), in_=x_f32.ap(), func=AF.Copy),
                  waits=[t_x, prev_mm5])
        t_xsp_prev = t_xsp

        wg3 = [t_xb, tok_wd1c, t_e5 or tok_tz]
        t_mm3 = None
        for tc in range(4):
            t_mm3 = pe(lambda h, e=e, tc=tc: h.matmul(
                           pDv[:, tc * 512:(tc + 1) * 512],
                           wd1T_bf.ap()[:, e * P:(e + 1) * P],
                           xb_bf.ap()[:, tc * 512:(tc + 1) * 512],
                           start=True, stop=True),
                       waits=wg3 if tc == 0 else (), inc=(tc == 3))
        t_e3 = ve(lambda h: h.tensor_tensor(t_acc.ap(), pDv, t_acc.ap(), ALU.add),
                  waits=[t_mm3, tok_tz])
        t_mm5 = None
        for tc in range(4):
            t_mm5 = pe(lambda h, e=e, tc=tc: h.matmul(
                           pDv[0:32, tc * 512:(tc + 1) * 512],
                           wbcT_bf.ap()[:, e * 32:(e + 1) * 32],
                           xb_bf.ap()[:, tc * 512:(tc + 1) * 512],
                           start=True, stop=True),
                       waits=[t_e3, tok_wbcc] if tc == 0 else (), inc=(tc == 3))
        t_e5 = ve(lambda h: h.tensor_tensor(bc_acc.ap(), pDv[0:32, :], bc_acc.ap(), ALU.add),
                  waits=[t_mm5, tok_bz])
        prev_mm5 = t_mm5
        emit_prespill(2)

    if debug_taps:
        dma(taps["tap_x"], x_spill, waits=[t_xsp_prev])

    emit_prespill(64)

    t_d1 = dma(cc_t_in[0:128, :], t_acc.ap(), waits=[t_e3, t_e5])
    t_d2 = dma(cc_t_in[128:160, :], bc_acc.ap())
    tok_cct = B.emit("gpsimd",
                     lambda h: h.collective_compute("AllReduce", ALU.add, replica_groups=PAIRS,
                                                    ins=[cc_t_in.opt()], outs=[cc_t_out.opt()]),
                     waits=[t_d1, t_d2], inc=(CCT, 1))
    t_tb = dma(wstage[0].ap(), cc_t_out[0:128, :], waits=[tok_cct, t_csk])
    tok_tcast = sc(lambda h: h.activation(out=t_bf.ap(), in_=wstage[0].ap(), func=AF.Copy),
                   waits=[t_tb])
    if debug_taps:
        dma(taps["tap_t"], cc_t_out[0:128, :], waits=[tok_cct])
        dma(taps["tap_bc"], cc_t_out[128:160, :], waits=[tok_cct])

    if stop_after == "p1":
        return _finalize(nc, B)

    B.barrier_into(snapshot())
    tok_az = ve(lambda h: (h.memset(a_big[0].ap(), 0.0), h.memset(a_big[1].ap(), 0.0))[1])
    tok_cz2 = ve(lambda h: h.memset(carries.ap(), 0.0))

    def bc_src(n, c0, c1):
        two = cc_t_out[128 + n:145 + n:16, c0:c1]
        return two.unsqueeze(0).broadcast_to([P, 2, c1 - c0])

    u3 = [u.ap().rearrange("p (e s) -> p e s", s=SEG) for u in u_big]
    a3 = [a.ap().rearrange("p (e s) -> p e s", s=SEG) for a in a_big]
    car3 = carries.ap().rearrange("p (e m) -> p e m", m=NST)
    t_gat = None
    t_dx = None
    hist_u = [None, None]
    hist_p = [None, None]
    hist_yadd = [None, None]
    hist_exp = [None, None]
    hist_tts = [None, None]
    t_yadd = None
    t_u = None
    t_p = None
    prev_ygd0 = prev_ygd1 = None
    for c in range(NCH):
        t_delp = [None, None]
        c0, c1 = c * LC, (c + 1) * LC
        wfree = [t_gat] if t_gat else []
        t_lx = dma(x3.ap(), x_spill.rearrange("e p l -> p e l")[:, :, c0:c1], waits=wfree)
        t_ls = dma(silu_ch.ap(), silu_spill.rearrange("e p l -> p e l")[:, :, c0:c1], waits=wfree)
        t_del = None
        for e in range(ET):
            bank = e % 2
            wg4 = [tok_tcast, tok_wd2c, ev_bank.get(bank)]
            if e < 2 and t_dx:
                wg4.append(t_dx)
            t_g4 = pe(lambda h, e=e, bank=bank, c0=c0, c1=c1: h.matmul(
                          pbv[bank][:, 0:LC], wd2T_bf.ap()[:, e * P:(e + 1) * P],
                          t_bf.ap()[:, c0:c1], start=True, stop=True),
                      waits=wg4, inc=True)
            sb3 = sp3[e % 2]
            wdel_guard = [t_delp[e % 2]] if t_delp[e % 2] else []
            t_s1 = sc(lambda h, e=e, bank=bank: h.activation(
                          out=delta3.ap()[:, e, :], in_=pbv[bank][:, 0:LC], func=AF.Relu,
                          bias=wd2b_sb.ap()[:, e:e + 1]),
                      waits=[t_g4] + ([now(SC)] if e == 0 else [])
                            + ([t_gat, prev_ygd0, prev_ygd1] if e == 14 else []))
            t_s2 = sc(lambda h, e=e, bank=bank, sb3=sb3: h.activation(
                          out=sb3.ap(), in_=pbv[bank][:, 0:LC], func=AF.Abs,
                          bias=wd2b_sb.ap()[:, e:e + 1]),
                      waits=[t_s1] + wdel_guard)
            ev_bank[bank] = t_s2
            t_s3 = sc(lambda h, sb3=sb3: h.activation(out=sb3.ap(), in_=sb3.ap(), func=AF.Exp,
                                                      scale=-1.0),
                      waits=[t_s2])
            t_s4 = sc(lambda h, sb3=sb3: h.activation(out=sb3.ap(), in_=sb3.ap(), func=AF.Ln,
                                                      bias=1.0),
                      waits=[t_s3])
            t_del = ve(lambda h, e=e, sb3=sb3: h.tensor_tensor(delta3.ap()[:, e, :],
                                                               delta3.ap()[:, e, :], sb3.ap(),
                                                               ALU.add),
                       waits=[t_s4])
            t_delp[e % 2] = t_del
        t_dly = t_del
        t_dx = ve(lambda h: h.tensor_tensor(
                      dx3.ap().rearrange("p e l -> p (e l)"),
                      delta3.ap().rearrange("p e l -> p (e l)"),
                      x3.ap().rearrange("p e l -> p (e l)"), ALU.mult),
                  waits=[t_dly, t_lx, t_u])
        t_yz = ve(lambda h: h.memset(yacc3.ap(), 0.0), waits=[t_gat])
        bc_tok = {0: dma(bcn[0].ap(), bc_src(0, c0, c1),
                         waits=[tok_cct, hist_u[0], hist_p[0]])}
        for n in range(NST):
            pn = n % 2
            if n + 1 < NST:
                bc_tok[n + 1] = dma(bcn[(n + 1) % 2].ap(), bc_src(n + 1, c0, c1),
                                    waits=[hist_u[(n + 1) % 2], hist_p[(n + 1) % 2]])
            t_a = None
            for e in range(ET):
                wexp = [t_dly]
                if e == 0:
                    wexp.append(hist_p[pn] or tok_az)
                t_a = sc(lambda h, e=e, n=n, pn=pn: h.activation(
                             out=a_big[pn].ap()[:, e * SEG + 1:e * SEG + 1 + LC],
                             in_=delta3.ap()[:, e, :], func=AF.Exp,
                             scale=a_neg.ap()[:, e * NST + n:e * NST + n + 1]),
                         waits=wexp)
            hist_exp[pn] = t_a
            t_u = ve(lambda h, pn=pn: h.tensor_tensor(
                         u3[pn][:, :, 1:SEG], dx3.ap(),
                         bcn[pn].ap()[:, 0:1, :].broadcast_to([P, ET, LC]), ALU.mult),
                     waits=[t_dx, bc_tok[n], hist_yadd[pn]])
            hist_u[pn] = t_u
            t_cc = ve(lambda h, n=n, pn=pn: h.tensor_copy(u3[pn][:, :, 0:1], car3[:, :, n:n + 1]),
                      waits=[t_u, tok_cz2])
            t_h = ve(lambda h, pn=pn: h.tensor_tensor_scan(a_big[pn].ap(), a_big[pn].ap(),
                                                           u_big[pn].ap(), 0.0,
                                                           ALU.mult, ALU.add),
                     waits=[t_cc, t_a, tok_az])
            hist_tts[pn] = t_h
            t_nc = ve(lambda h, n=n, pn=pn: h.tensor_copy(car3[:, :, n:n + 1],
                                                          a3[pn][:, :, SEG - 1:SEG]),
                      waits=[t_h])
            t_p = ve(lambda h, pn=pn: h.tensor_tensor(
                         u3[pn][:, :, 1:SEG], a3[pn][:, :, 1:SEG],
                         bcn[pn].ap()[:, 1:2, :].broadcast_to([P, ET, LC]), ALU.mult),
                     waits=[t_nc])
            hist_p[pn] = t_p
            t_yadd = ve(lambda h, pn=pn: h.tensor_tensor(yacc3.ap(), yacc3.ap(),
                                                         u3[pn][:, :, 1:SEG], ALU.add),
                        waits=[t_p, t_yz])
            hist_yadd[pn] = t_yadd
        t_g2 = [None, None]
        t_ygd = [None, None]
        for e in range(ET):
            pg = e % 2
            t_g1 = ve(lambda h, e=e: h.scalar_tensor_tensor(
                          ygf.ap(), x3.ap()[:, e, :], wdvec_sb.ap()[:, e:e + 1],
                          yacc3.ap()[:, e, :], ALU.mult, ALU.add),
                      waits=([t_ls, t_yadd] if e == 0 else []) + [t_ygd[pg]])
            t_g2[pg] = ve(lambda h, e=e, pg=pg: h.tensor_tensor(yg1[pg].ap(), ygf.ap(),
                                                                silu_ch.ap()[:, e, :], ALU.mult),
                          waits=[t_g1])
            t_ygd[pg] = dma(yg_in[c][e * P:(e + 1) * P, :], yg1[pg].ap(), waits=[t_g2[pg]])
        t_gat = t_g2[1]
        prev_ygd0, prev_ygd1 = t_ygd[0], t_ygd[1]
        B.emit("gpsimd",
               lambda h, c=c: h.collective_compute("AllGather", ALU.bypass, replica_groups=PAIRS,
                                                   ins=[yg_in[c].opt()], outs=[yg_out[c].opt()]),
               waits=[t_ygd[0], t_ygd[1]], inc=(CCY[c], 1))
        if debug_taps:
            dma(taps["tap_delta"][c], delta3.ap().rearrange("p e l -> p (e l)"), waits=[t_dx])

    if stop_after == "p2":
        return _finalize(nc, B)

    B.barrier_into(snapshot())
    ev_bank = {}
    ev2 = {}
    t_prev = None
    t_xb = None
    t_spx = None
    for tt in range(TTH):
        t_ld = dma(p3res.ap(), resid_ffn[tt * P:(tt + 1) * P, :],
                   waits=[t_prev, now(SC), now(VE)] if t_prev else [])
        t_sum = ve(lambda h: h.reduce_sum(cb[0].ap(), p3res.ap(), axis=AX.X), waits=[t_ld])
        t_nm = ve(lambda h: h.tensor_single_scalar(cb[1].ap(), cb[0].ap(), -inv_d, ALU.mult),
                  waits=[t_sum])
        t_var = sc(lambda h: h.activation(out=junk3.ap(), in_=p3res.ap(), func=AF.Square,
                                          bias=cb[1].ap(), accum_out=cb[2].ap()),
                   waits=[t_nm])
        t_v = ve(lambda h: h.tensor_scalar(cb[3].ap(), cb[2].ap(), inv_d, EPS, ALU.mult, ALU.add),
                 waits=[t_var])
        t_l1 = sc(lambda h: h.activation(out=cb[4].ap(), in_=cb[3].ap(), func=AF.Ln), waits=[t_v])
        t_rstd = sc(lambda h: h.activation(out=cb[4].ap(), in_=cb[4].ap(), func=AF.Exp, scale=-0.5),
                    waits=[t_l1])
        t_xn = ve(lambda h: h.tensor_scalar(p3xn.ap(), p3res.ap(), cb[1].ap(), cb[4].ap(),
                                            ALU.add, ALU.mult),
                  waits=[t_rstd, t_xb, t_spx])
        t_spx = dma(xn_spill[tt], p3xn.ap(), waits=[t_xn])
        t_xb = sc(lambda h: h.activation(out=p3xnb.ap(), in_=p3xn.ap(), func=AF.Copy),
                  waits=[t_xn])
        for dt in range(DT):
            bank = dt % 2
            t_tr = pe(lambda h, dt=dt, bank=bank: h.transpose(
                          pbbf[bank][:, 0:P], p3xnb.ap()[:, dt * P:(dt + 1) * P], id_bf.ap()),
                      waits=[t_xb, tok_id, ev_bank.get(bank)])
            ev_bank[bank] = sc(lambda h, dt=dt, tt=tt, bank=bank: h.activation(
                                   out=xnT3.ap()[:, dt, tt * P:(tt + 1) * P],
                                   in_=pbbf[bank][:, 0:P], func=AF.Copy),
                               waits=[t_tr])
        t_prev = ev_bank[1]
    tok_xnT = now(SC)

    t_lg = dma(gate_f.ap().rearrange("p (a b) -> p a b", a=DT),
               gate_w.rearrange("(dt p) m -> p dt m", p=P))
    tok_gc = sc(lambda h: h.activation(out=gate_b.ap(), in_=gate_f.ap(), func=AF.Copy),
                waits=[t_lg])
    ev0 = None
    t_sc = None
    for tt in range(TTH):
        t_mmg = None
        for dt in range(DT):
            t_mmg = pe(lambda h, dt=dt, tt=tt: h.matmul(
                           pbv[0][:, 0:8], xnT3.ap()[:, dt, tt * P:(tt + 1) * P],
                           gate_b.ap()[:, dt * 8:(dt + 1) * 8],
                           start=(dt == 0), stop=(dt == DT - 1)),
                       waits=[tok_gc, tok_xnT, ev0, ev_bank.get(0)] if dt == 0 else (),
                       inc=(dt == DT - 1))
        ev0 = ve(lambda h, tt=tt: h.tensor_copy(logits.ap()[:, tt * 8:(tt + 1) * 8],
                                                pbv[0][:, 0:8]),
                 waits=[t_mmg])
        tpair = ev0
        off = 0
        for k in range(1, 8):
            tpair = ve(lambda h, tt=tt, k=k, off=off: h.tensor_tensor(
                           pair28.ap()[:, off:off + 8 - k],
                           logits.ap()[:, tt * 8:tt * 8 + 8 - k],
                           logits.ap()[:, tt * 8 + k:tt * 8 + 8], ALU.add),
                       waits=[tpair])
            off += 8 - k
        t_sc = ve(lambda h, tt=tt: h.reduce_max(scale_sb.ap()[:, tt:tt + 1], pair28.ap(),
                                                axis=AX.X),
                  waits=[tpair])
    tok_scale = t_sc

    wcast = [None, None]
    ev_pD = None
    exmm = [None, None]
    tr_last = None
    for ot in range(DT):
        par = ot % 2
        t_lw = dma(wstage[par].ap().rearrange("p (a b) -> p a b", a=DT),
                   exp_w.rearrange("(dt p) o -> p dt o", p=P)[:, :, ot * P:(ot + 1) * P],
                   waits=[wcast[par]])
        t_wc = sc(lambda h, par=par: h.activation(out=wbf[par].ap(), in_=wstage[par].ap(),
                                                  func=AF.Copy),
                  waits=[t_lw, exmm[par]])
        wcast[par] = t_wc
        t_mme = None
        for tc in range(2):
            for dt in range(DT):
                t_mme = pe(lambda h, dt=dt, tc=tc, par=par: h.matmul(
                               pDv[:, tc * 512:(tc + 1) * 512],
                               wbf[par].ap()[:, dt * P:(dt + 1) * P],
                               xnT3.ap()[:, dt, tc * 512:(tc + 1) * 512],
                               start=(dt == 0), stop=(dt == DT - 1)),
                           waits=[t_wc, ev_pD] if (dt == 0 and tc == 0) else (),
                           inc=(dt == DT - 1))
        exmm[par] = t_mme
        ev_pD = sc(lambda h: h.activation(out=moeT_st.ap(), in_=pDv[:, 0:1024], func=AF.Copy),
                   waits=[t_mme, tr_last])
        for tt in range(TTH):
            bank = 2 + tt % 2
            t_tr = pe(lambda h, tt=tt, bank=bank: h.transpose(
                          pbv[bank][:, 0:P], moeT_st.ap()[:, tt * P:(tt + 1) * P], id_f32.ap()),
                      waits=[ev_pD, tok_id, ev2.get(bank)])
            ev2[bank] = sc(lambda h, tt=tt, ot=ot, bank=bank: h.activation(
                               out=yffn3.ap()[:, tt, ot * P:(ot + 1) * P],
                               in_=pbv[bank][:, 0:P], func=AF.Copy,
                               scale=scale_sb.ap()[:, tt:tt + 1]),
                           waits=[t_tr, tok_scale])
            tr_last = t_tr
    tok_moe = [now(SC), now(VE)]

    ev_dn = None
    t_h2 = None
    upmm = [None, None]
    dnmm = [None, None]
    for q in range(4):
        for icq in range(16):
            ic = q * 16 + icq
            par = ic % 2
            t_lu = dma(wstage[par].ap().rearrange("p (a b) -> p a b", a=DT),
                       up_w.rearrange("(dt p) i -> p dt i", p=P)[:, :, ic * P:(ic + 1) * P],
                       waits=[wcast[par]])
            t_wc = sc(lambda h, par=par: h.activation(out=wbf[par].ap(), in_=wstage[par].ap(),
                                                      func=AF.Copy),
                      waits=[t_lu, upmm[par] if upmm[par] else exmm[par]])
            wcast[par] = t_wc
            for tc in range(2):
                bank = tc
                wmm = [t_wc, ev_bank.get(bank)]
                if q == 0 and icq == 0 and tc == 0:
                    wmm += tok_moe
                t_mmu = None
                for dt in range(DT):
                    t_mmu = pe(lambda h, dt=dt, tc=tc, par=par, bank=bank: h.matmul(
                                   pbv[bank], wbf[par].ap()[:, dt * P:(dt + 1) * P],
                                   xnT3.ap()[:, dt, tc * 512:(tc + 1) * 512],
                                   start=(dt == 0), stop=(dt == DT - 1)),
                               waits=wmm if dt == 0 else (), inc=(dt == DT - 1))
                upmm[par] = t_mmu
                t_r = sc(lambda h, bank=bank: h.activation(out=relu_st.ap(), in_=pbv[bank],
                                                           func=AF.Relu),
                         waits=[t_mmu, t_h2])
                ev_bank[bank] = t_r
                t_h2 = sc(lambda h, icq=icq, tc=tc: h.activation(
                              out=h2T.ap()[:, icq, tc * 512:(tc + 1) * 512],
                              in_=relu_st.ap(), func=AF.Square),
                          waits=[t_r, ev_dn if (q > 0 and icq == 0 and tc == 0) else None])
        tok_up = t_h2
        for dto in range(DT):
            par = dto % 2
            t_ldw = dma(wstage[par].ap().rearrange("p (a b) -> p a b", a=16),
                        down_w.rearrange("(i p) d -> p i d", p=P)[:, q * 16:(q + 1) * 16,
                                                                 dto * P:(dto + 1) * P],
                        waits=[wcast[par]])
            t_dwc = sc(lambda h, par=par: h.activation(out=wbf[2 + par].ap(), in_=wstage[par].ap(),
                                                       func=AF.Copy),
                       waits=[t_ldw, dnmm[par]])
            wcast[par] = t_dwc
            t_mmd = None
            for icq in range(16):
                for tc in range(2):
                    t_mmd = pe(lambda h, icq=icq, tc=tc, par=par: h.matmul(
                                   pDv[:, tc * 512:(tc + 1) * 512],
                                   wbf[2 + par].ap()[:, icq * P:(icq + 1) * P],
                                   h2T.ap()[:, icq, tc * 512:(tc + 1) * 512],
                                   start=(icq == 0), stop=(icq == 15)),
                               waits=[t_dwc, tok_up, ev_dn] if (icq == 0 and tc == 0) else (),
                               inc=(icq == 15))
            dnmm[par] = t_mmd
            ev_dn = sc(lambda h: h.activation(out=moeT_st.ap(), in_=pDv[:, 0:1024], func=AF.Copy),
                       waits=[t_mmd, tr_last])
            for tt in range(TTH):
                bank = 2 + tt % 2
                t_tr = pe(lambda h, tt=tt, bank=bank: h.transpose(
                              pbv[bank][:, 0:P], moeT_st.ap()[:, tt * P:(tt + 1) * P],
                              id_f32.ap()),
                          waits=[ev_dn, ev2.get(bank)])
                ev2[bank] = ve(lambda h, tt=tt, dto=dto, bank=bank: h.tensor_tensor(
                                   yffn3.ap()[:, tt, dto * P:(dto + 1) * P],
                                   pbv[bank][:, 0:P],
                                   yffn3.ap()[:, tt, dto * P:(dto + 1) * P], ALU.add),
                               waits=[t_tr])
                tr_last = t_tr
    tok_down = [now(VE), now(SC), now(PE)]

    t_lgm = dma(wstage[0].ap(), ln_g.broadcast_to([P, D]), waits=[wcast[0], wcast[1]])
    t_lbt = dma(wstage[1].ap(), ln_b.broadcast_to([P, D]), waits=[wcast[0], wcast[1]])
    t_fin = None
    t_lr = None
    t_lxn = None
    for tt in range(TTH):
        t_sum = ve(lambda h, tt=tt: h.reduce_sum(cb[0].ap(), yffn3.ap()[:, tt, :], axis=AX.X),
                   waits=tok_down if tt == 0 else [])
        t_nm = ve(lambda h: h.tensor_single_scalar(cb[1].ap(), cb[0].ap(), -inv_d, ALU.mult),
                  waits=[t_sum])
        t_var = sc(lambda h, tt=tt: h.activation(out=junk3.ap(), in_=yffn3.ap()[:, tt, :],
                                                 func=AF.Square, bias=cb[1].ap(),
                                                 accum_out=cb[2].ap()),
                   waits=[t_nm] + (tok_down if tt == 0 else []))
        t_v = ve(lambda h: h.tensor_scalar(cb[3].ap(), cb[2].ap(), inv_d, EPS, ALU.mult, ALU.add),
                 waits=[t_var])
        t_l1 = sc(lambda h: h.activation(out=cb[4].ap(), in_=cb[3].ap(), func=AF.Ln), waits=[t_v])
        t_rstd = sc(lambda h: h.activation(out=cb[4].ap(), in_=cb[4].ap(), func=AF.Exp, scale=-0.5),
                    waits=[t_l1])
        t_app = ve(lambda h, tt=tt: h.tensor_scalar(yffn3.ap()[:, tt, :], yffn3.ap()[:, tt, :],
                                                    cb[1].ap(), cb[4].ap(), ALU.add, ALU.mult),
                   waits=[t_rstd])
        t_gm = ve(lambda h, tt=tt: h.tensor_tensor(yffn3.ap()[:, tt, :], yffn3.ap()[:, tt, :],
                                                   wstage[0].ap(), ALU.mult),
                  waits=[t_app, t_lgm])
        t_bt = ve(lambda h, tt=tt: h.tensor_tensor(yffn3.ap()[:, tt, :], yffn3.ap()[:, tt, :],
                                                   wstage[1].ap(), ALU.mult_op1 if False else ALU.add),
                  waits=[t_gm, t_lbt])
        t_lr = dma(p3res.ap(), resid_ffn[tt * P:(tt + 1) * P, :],
                   waits=[t_fin] + (tok_down if tt == 0 else []))
        t_lxn = dma(p3xn.ap(), xn_spill[tt], waits=[t_fin] if t_fin else [])
        t_a1 = ve(lambda h, tt=tt: h.tensor_tensor(yffn3.ap()[:, tt, :], yffn3.ap()[:, tt, :],
                                                   p3res.ap(), ALU.add),
                  waits=[t_bt, t_lr])
        t_fin = ve(lambda h, tt=tt: h.tensor_tensor(yffn3.ap()[:, tt, :], yffn3.ap()[:, tt, :],
                                                    p3xn.ap(), ALU.add),
                   waits=[t_a1, t_lxn])
    tok_base = t_fin
    if debug_taps:
        for tt in range(TTH):
            dma(taps["tap_ffn"][tt], yffn3.ap()[:, tt, :], waits=[tok_base])

    if stop_after == "p3":
        return _finalize(nc, B)

    B.barrier_into(snapshot())
    t_selp = [None, None]
    t_cast4 = [None, None]
    g6mm = [None, None]
    ev_pb0 = {}
    tr4_last = None
    t_add_last = None
    for ps in range(2):
        sel_tok = []
        for ci in range(2):
            ch = ps * 2 + ci
            t_la = dma(yfullA.ap(), yg_out[ch].rearrange("(e p) l -> p e l", p=P),
                       waits=[(CCY[ch], 1), t_selp[0], t_selp[1]])
            t_lb = dma(yfullB.ap(), yg_out[ch + 4].rearrange("(e p) l -> p e l", p=P),
                       waits=[(CCY[ch + 4], 1)])
            t_s1 = ve(lambda h, ci=ci: h.tensor_single_scalar(
                          ysel[ci].ap().rearrange("p e l -> p (e l)"),
                          yfullA.ap().rearrange("p e l -> p (e l)"),
                          msel_sb.ap()[:, 0:1], ALU.mult),
                      waits=[t_la, now(PE)])
            t_sel = ve(lambda h, ci=ci: h.scalar_tensor_tensor(
                           ysel[ci].ap().rearrange("p e l -> p (e l)"),
                           yfullB.ap().rearrange("p e l -> p (e l)"),
                           msel_sb.ap()[:, 1:2],
                           ysel[ci].ap().rearrange("p e l -> p (e l)"),
                           ALU.mult, ALU.add),
                       waits=[t_s1, t_lb])
            t_selp[ci] = t_sel
            sel_tok.append(t_sel)
        for dt in range(DT):
            for hb in range(2):
                par = (dt * 2 + hb) % 2
                t_lo = dma(wbf[par].ap().rearrange("p (a b) -> p a b", a=DT),
                           ogb_spill.rearrange("g p d -> p g d")[:, hb * 16:(hb + 1) * 16,
                                                                dt * P:(dt + 1) * P],
                           waits=[g6mm[par]], eng=("sync" if par == 0 else "scalar"))
                for ci in range(2):
                    for eti in range(16):
                        et = hb * 16 + eti
                        t_mmo = pe(lambda h, eti=eti, et=et, par=par, ci=ci: h.matmul(
                                       pbv[ci][:, 0:LC], wbf[par].ap()[:, eti * P:(eti + 1) * P],
                                       ysel[ci].ap()[:, et, :],
                                       start=(et == 0), stop=(et == 31)),
                                   waits=([t_lo, sel_tok[ci], ev_pb0.get(ci)]
                                          if eti == 0 else ()),
                                   inc=(eti == 15))
                    g6mm[par] = t_mmo
            for ci in range(2):
                ch = ps * 2 + ci
                ev_pb0[ci] = sc(lambda h, ci=ci: h.activation(
                                    out=attnT.ap()[:, ci, :], in_=pbv[ci][:, 0:LC], func=AF.Copy),
                                waits=[g6mm[(dt * 2 + 1) % 2], g6mm[dt * 2 % 2], tr4_last])
                for k in range(2):
                    bank = 2 + k
                    t_tr = pe(lambda h, ci=ci, k=k, bank=bank: h.transpose(
                                  pbv[bank][:, 0:P], attnT.ap()[:, ci, k * P:(k + 1) * P],
                                  id_f32.ap()),
                              waits=[ev_pb0[ci], ev2.get(bank)])
                    ev2[bank] = ve(lambda h, ch=ch, k=k, dt=dt, bank=bank: h.tensor_tensor(
                                       yffn3.ap()[:, ch * 2 + k, dt * P:(dt + 1) * P],
                                       pbv[bank][:, 0:P],
                                       yffn3.ap()[:, ch * 2 + k, dt * P:(dt + 1) * P], ALU.add),
                                   waits=[t_tr, tok_base])
                    tr4_last = t_tr
        for ci in range(2):
            ch = ps * 2 + ci
            for k in range(2):
                dma(out[(ch * 256 + k * P):(ch * 256 + (k + 1) * P), :],
                    yffn3.ap()[:, ch * 2 + k, :], waits=[ev2[2], ev2[3]])

    return _finalize(nc, B)


def _finalize(nc, B):
    B.emit("sync", lambda h: None, waits=[(c, c.n) for c in B.chans if c.n > 0])
    with contextlib.ExitStack() as st:
        for c in B.chans:
            c.h = st.enter_context(nc.semaphore(c.name))
        block = st.enter_context(nc.Block())
        for eng in ENGS:
            def mk(eng):
                def emit_all(h):
                    for f in B.ops[eng]:
                        f(h)
                return emit_all
            getattr(block, eng)(mk(eng))
    return nc


def _prep_core(inputs, b, hf):
    asc = np.ascontiguousarray
    eh = slice(hf * EH, (hf + 1) * EH)
    f32 = np.float32
    resid = np.asarray(inputs["L0_attn_resid"], f32)
    pmaj = lambda v: asc(np.asarray(v, f32).reshape(-1, P).T)
    m = {
        "resid": asc(resid[b]),
        "resid_ffn": asc(resid[b, hf * TOKH:(hf + 1) * TOKH]),
        "in_wT": asc(np.asarray(inputs["L0_attn_in_w"], f32)[eh].T),
        "skip_wT": asc(np.asarray(inputs["L0_attn_skip_w"], f32)[eh].T),
        "wd1T": asc(np.asarray(inputs["L0_attn_wd1"], f32)[:, eh].T),
        "wd2T": asc(np.asarray(inputs["L0_attn_wd2"], f32)[eh].T),
        "wbcT": asc(np.concatenate([np.asarray(inputs["L0_attn_wB"], f32)[:, eh],
                                    np.asarray(inputs["L0_attn_wC"], f32)[:, eh]], 0).T),
        "convw": asc(np.asarray(inputs["L0_attn_conv_w"], f32)[eh]
                     .reshape(ET, P, 4).transpose(1, 0, 2).reshape(P, ET * 4)),
        "convb": pmaj(np.asarray(inputs["L0_attn_conv_b"], f32)[eh]),
        "wd2b": pmaj(np.asarray(inputs["L0_attn_wd2_b"], f32)[eh]),
        "alog": asc(np.asarray(inputs["L0_attn_A_log"], f32)[eh]
                    .reshape(ET, P, NST).transpose(1, 0, 2).reshape(P, ET * NST)),
        "wdvec": pmaj(np.asarray(inputs["L0_attn_W_D"], f32)[eh]),
        "normw": pmaj(np.asarray(inputs["L0_attn_norm_w"], f32)),
        "out_wT": asc(np.asarray(inputs["L0_attn_out_w"], f32).T),
        "gate_w": asc(np.asarray(inputs["L0_ffn_gate_w"], f32)),
        "up_w": asc(np.asarray(inputs["L0_ffn_up_w"], f32)),
        "down_w": asc(np.asarray(inputs["L0_ffn_down_w"], f32)),
        "exp_w": asc(np.asarray(inputs["L0_ffn_expert_weight"], f32)),
        "ln_g": asc(np.asarray(inputs["L0_ffn_ln_gamma"], f32).reshape(1, D)),
        "ln_b": asc(np.asarray(inputs["L0_ffn_ln_beta"], f32).reshape(1, D)),
        "hsel": asc(np.broadcast_to(
            np.array([1.0, 0.0] if hf == 0 else [0.0, 1.0], f32), (P, 2)).copy()),
    }
    return m


def kernel(**inputs):
    nc = build_nc()
    in_maps = [_prep_core(inputs, c // 2, c % 2) for c in range(8)]
    res = run_bass_kernel_spmd(nc, in_maps, core_ids=list(range(8)))
    outp = np.zeros((4, L, D), np.float32)
    for c in range(8):
        b, hf = c // 2, c % 2
        outp[b, hf * TOKH:(hf + 1) * TOKH, :] = np.asarray(res.results[c]["out"])
    return outp


# version 18
# speedup vs baseline: 1.0367x; 1.0033x over previous
import contextlib
import numpy as np
import ml_dtypes
import concourse.bass as bass
import concourse.mybir as mybir
from concourse.bass_utils import run_bass_kernel_spmd

F32 = mybir.dt.float32
BF16 = mybir.dt.bfloat16
AF = mybir.ActivationFunctionType
ALU = mybir.AluOpType
AX = mybir.AxisListType

P = 128
D = 2048
EH = 2048
L = 2048
ET = 16
DT = 16
NTT = 16
LC = 256
NCH = 8
NST = 16
SEG = LC + 1
TOKH = 1024
TTH = 8
II = 8192
EPS = 1e-5
PAIRS = [[0, 1], [2, 3], [4, 5], [6, 7]]
ENGS = ("sync", "tensor", "scalar", "vector", "gpsimd")


class Chan:
    __slots__ = ("name", "n", "h")
    def __init__(self, name):
        self.name, self.n, self.h = name, 0, None


class Builder:
    def __init__(self):
        self.ops = {e: [] for e in ENGS}
        self.chans = []
        self.pending = {e: [] for e in ENGS}

    def chan(self, name):
        c = Chan(name)
        self.chans.append(c)
        return c

    def barrier_into(self, toks):
        for e in ENGS:
            self.pending[e].extend(toks)

    def emit(self, eng, fn, waits=(), inc=None):
        if inc is not None:
            inc[0].n += inc[1]
        tok = None if inc is None else (inc[0], inc[0].n)
        waits = tuple(self.pending[eng]) + tuple(w for w in waits if w is not None)
        self.pending[eng] = []

        def f(h, fn=fn, waits=waits, inc=inc):
            seen = {}
            for c, v in waits:
                if seen.get(id(c), -1) < v:
                    seen[id(c)] = v
                    h.wait_ge(c.h, v)
            inst = fn(h)
            if inc is not None and inst is not None:
                inst.then_inc(inc[0].h, inc[1])
        self.ops[eng].append(f)
        return tok


def build_nc(stop_after=None, debug_taps=False):
    nc = bass.Bass("TRN2", target_bir_lowering=False, debug=False)

    def dram_in(name, shape, dtype=F32):
        return nc.dram_tensor(name, list(shape), dtype, kind="ExternalInput").ap()

    resid = dram_in("resid", (L, D))
    resid_ffn = dram_in("resid_ffn", (TOKH, D))
    in_wT = dram_in("in_wT", (D, EH))
    skip_wT = dram_in("skip_wT", (D, EH))
    wd1T = dram_in("wd1T", (EH, P))
    wd2T = dram_in("wd2T", (P, EH))
    wbcT = dram_in("wbcT", (EH, 32))
    convw = dram_in("convw", (P, ET * 4))
    convb = dram_in("convb", (P, ET))
    wd2b = dram_in("wd2b", (P, ET))
    alog = dram_in("alog", (P, ET * NST))
    wdvec = dram_in("wdvec", (P, ET))
    normw = dram_in("normw", (P, DT))
    out_wT = dram_in("out_wT", (2 * EH, D))
    gate_w = dram_in("gate_w", (D, 8))
    up_w = dram_in("up_w", (D, II))
    down_w = dram_in("down_w", (II, D))
    exp_w = dram_in("exp_w", (D, D))
    ln_g = dram_in("ln_g", (1, D))
    ln_b = dram_in("ln_b", (1, D))
    hsel = dram_in("hsel", (P, 2))

    out = nc.dram_tensor("out", [TOKH, D], F32, kind="ExternalOutput").ap()

    x_spill = nc.dram_tensor("x_spill", [ET, P, L], F32).ap()
    ogb_spill = nc.dram_tensor("ogb_spill", [32, P, D], BF16).ap()
    silu_spill = nc.dram_tensor("silu_spill", [ET, P, L], BF16).ap()
    xn_spill = nc.dram_tensor("xn_spill", [TTH, P, D], F32).ap()
    cc_t_in = nc.dram_tensor("cc_t_in", [160, L], F32).ap()
    cc_t_out = nc.dram_tensor("cc_t_out", [160, L], F32).ap()
    yg_in = [nc.dram_tensor(f"yg_in{c}", [EH, LC], BF16).ap() for c in range(NCH)]
    yg_out = [nc.dram_tensor(f"yg_out{c}", [2 * EH, LC], BF16).ap() for c in range(NCH)]

    taps = {}
    if debug_taps:
        def tap(name, shape, dtype=F32):
            taps[name] = nc.dram_tensor(name, list(shape), dtype, kind="ExternalOutput").ap()
        tap("tap_rnT", (DT, P, L), BF16)
        tap("tap_t", (P, L))
        tap("tap_bc", (32, L))
        tap("tap_x", (ET, P, L))
        tap("tap_delta", (NCH, P, ET * LC))
        tap("tap_yg", (NCH, P, ET * LC), BF16)
        tap("tap_ffn", (TTH, P, D))

    def sbg(name, shape, dtype):
        return nc.alloc_sbuf_tensor(name, list(shape), dtype)

    t_bf = sbg("t_bf", (P, L), BF16)
    wd1T_bf = sbg("wd1T_bf", (P, ET * P), BF16)
    wd2T_bf = sbg("wd2T_bf", (P, ET * P), BF16)
    wbcT_bf = sbg("wbcT_bf", (P, ET * 32), BF16)
    a_neg = sbg("a_neg", (P, ET * NST), F32)
    convw_sb = sbg("convw_sb", (P, ET * 4), F32)
    convb_sb = sbg("convb_sb", (P, ET), F32)
    wd2b_sb = sbg("wd2b_sb", (P, ET), F32)
    wdvec_sb = sbg("wdvec_sb", (P, ET), F32)
    normw_sb = sbg("normw_sb", (P, DT), F32)
    msel_sb = sbg("msel_sb", (P, 2), F32)
    carries = sbg("carries", (P, ET * NST), F32)
    t_acc = sbg("t_acc", (P, L), F32)
    bc_acc = sbg("bc_acc", (32, L), F32)
    wstage = [sbg(f"wstage{i}", (P, D), F32) for i in range(2)]
    wbf = [sbg(f"wbf{i}", (P, D), BF16) for i in range(4)]
    id_f32 = sbg("id_f32", (P, P), F32)
    id_bf = sbg("id_bf", (P, P), BF16)
    cb = [sbg(f"cb{i}", (P, 1), F32) for i in range(8)]

    ARENA = (nc.sbuf_bytes_remaining - 64) & ~31
    arena = nc.alloc_sbuf_tensor("arena", [P, ARENA], mybir.dt.uint8)
    arena_base = nc.lookup_mloc(arena).addr

    def place(name, shape, dtype, offset):
        per_part = int(np.prod(shape[1:])) * mybir.dt.size(dtype)
        assert offset % 32 == 0 and offset + per_part <= ARENA, (name, offset, per_part)
        return nc.alloc_sbuf_tensor_at(name, list(shape), dtype,
                                       offset=arena_base + offset)

    class Bump:
        def __init__(self, start=0): self.o = start
        def take(self, nbytes):
            r = self.o
            self.o = (self.o + nbytes + 31) & ~31
            return r

    b1 = Bump()
    rnT = place("rnT", (P, DT, L), BF16, b1.take(DT * L * 2))
    rtoff = [b1.take(D * 4) for _ in range(2)]
    resid_t = [place(f"resid_t{i}", (P, D), F32, rtoff[i]) for i in range(2)]
    xn_f32 = place("xn_f32", (P, D), F32, b1.take(D * 4))
    rnpre = place("rnpre", (P, D), BF16, b1.take(D * 2))
    xin_st = place("xin_st", (P, D + 8), F32, b1.take((D + 8) * 4))
    convacc = place("convacc", (P, D), F32, b1.take(D * 4))
    x_f32 = place("x_f32", (P, D), F32, b1.take(D * 4))
    xb_bf = place("xb_bf", (P, D), BF16, b1.take(D * 2))
    silu_st = place("silu_st", (P, D), BF16, b1.take(D * 2))
    junk = place("junk", (P, D), F32, b1.take(D * 4))
    assert b1.o <= ARENA
    castb = [place(f"castb{i}", (P, D), BF16, roff) for i, roff in enumerate(rtoff)]

    b2 = Bump()
    delta3 = place("delta3", (P, ET, LC), F32, b2.take(ET * LC * 4))
    dx3 = place("dx3", (P, ET, LC), F32, b2.take(ET * LC * 4))
    x3 = place("x3", (P, ET, LC), F32, b2.take(ET * LC * 4))
    yacc3 = place("yacc3", (P, ET, LC), F32, b2.take(ET * LC * 4))
    a_big = [place(f"a_big{i}", (P, ET * SEG), F32, b2.take(ET * SEG * 4)) for i in range(2)]
    u_big = [place(f"u_big{i}", (P, ET * SEG), F32, b2.take(ET * SEG * 4)) for i in range(2)]
    bcn = [place(f"bcn{i}", (P, 2, LC), F32, b2.take(2 * LC * 4)) for i in range(2)]
    silu_ch = place("silu_ch", (P, ET, LC), BF16, b2.take(ET * LC * 2))
    sp1_off = b2.take(LC * 4)
    sp3_off = b2.take(LC * 4)
    sp3 = [place("sp3a", (P, LC), F32, sp1_off),
           place("sp3b", (P, LC), F32, sp3_off)]
    ygf = place("ygf", (P, LC), F32, sp3_off)
    yg1 = [place(f"yg1{i}", (P, LC), BF16, sp1_off + i * (LC * 2)) for i in range(2)]
    assert b2.o <= ARENA

    b3 = Bump()
    yffn3 = place("yffn3", (P, TTH, D), F32, b3.take(TTH * D * 4))
    h2T_off = b3.take(16 * TOKH * 2)
    h2T = place("h2T", (P, 16, TOKH), BF16, h2T_off)
    xnT3 = place("xnT3", (P, DT, TOKH), BF16, b3.take(DT * TOKH * 2))
    relu_st = place("relu_st", (P, 512), F32, b3.take(512 * 4))
    moeT_st = place("moeT_st", (P, TOKH), F32, b3.take(TOKH * 4))
    logits = place("logits", (P, TTH * 8), F32, b3.take(TTH * 8 * 4))
    pair28 = place("pair28", (P, 28), F32, b3.take(28 * 4))
    scale_sb = place("scale_sb", (P, TTH), F32, b3.take(TTH * 4))
    gate_f = place("gate_f", (P, DT * 8), F32, b3.take(DT * 8 * 4))
    gate_b = place("gate_b", (P, DT * 8), BF16, b3.take(DT * 8 * 2))
    assert b3.o <= ARENA
    bh = Bump(h2T_off)
    p3res = place("p3res", (P, D), F32, bh.take(D * 4))
    p3xn = place("p3xn", (P, D), F32, bh.take(D * 4))
    p3xnb = place("p3xnb", (P, D), BF16, bh.take(D * 2))
    junk3 = place("junk3", (P, D), F32, bh.take(D * 4))
    assert bh.o <= h2T_off + 16 * TOKH * 2

    b4 = Bump(TTH * D * 4)
    yfullA = place("yfullA", (P, 32, LC), BF16, b4.take(32 * LC * 2))
    yfullB = place("yfullB", (P, 32, LC), BF16, b4.take(32 * LC * 2))
    ysel = [place(f"ysel{i}", (P, 32, LC), BF16, b4.take(32 * LC * 2)) for i in range(2)]
    attnT = place("attnT", (P, 2, LC), F32, b4.take(2 * LC * 4))
    assert b4.o <= ARENA

    pb = [nc.alloc_psum_tensor(f"pb{i}", [P, 512], F32) for i in range(4)]
    pD = nc.alloc_psum_tensor("pD", [P, 2048], F32)
    pbv = [x.ap() for x in pb]
    pbbf = [x.ap().bitcast(BF16) for x in pb]
    pDv = pD.ap()

    B = Builder()
    SY = B.chan("SY"); GPD = B.chan("GPD"); SCD = B.chan("SCD")
    PE = B.chan("PE"); SC = B.chan("SC"); VE = B.chan("VE"); GP = B.chan("GP")
    CCT = B.chan("CCT")
    CCY = [B.chan(f"CCY{c}") for c in range(NCH)]

    def dma(out_ap, in_ap, waits=(), eng="sync"):
        ch = {"gpsimd": GPD, "scalar": SCD}.get(eng, SY)
        return B.emit(eng, lambda h, o=out_ap, i=in_ap: h.dma_start(out=o, in_=i),
                      waits=waits, inc=(ch, 16))

    def pe(fn, waits=(), inc=True):
        return B.emit("tensor", fn, waits=waits, inc=(PE, 1) if inc else None)

    def sc(fn, waits=()):
        return B.emit("scalar", fn, waits=waits, inc=(SC, 1))

    def ve(fn, waits=()):
        return B.emit("vector", fn, waits=waits, inc=(VE, 1))

    def gp(fn, waits=()):
        return B.emit("gpsimd", fn, waits=waits, inc=(GP, 1))

    def now(ch):
        return (ch, ch.n)

    def snapshot():
        return [now(PE), now(SC), now(VE), now(GP), now(SY), now(GPD)]

    dma(convw_sb.ap(), convw)
    dma(convb_sb.ap(), convb)
    dma(wd2b_sb.ap(), wd2b)
    dma(wdvec_sb.ap(), wdvec)
    dma(normw_sb.ap(), normw)
    dma(msel_sb.ap(), hsel)
    dma(a_neg.ap(), alog)
    dma(wstage[0].ap()[:, 0:ET * P].rearrange("p (e r) -> p e r", e=ET),
        wd1T.rearrange("(e p) r -> p e r", p=P))
    dma(wstage[1].ap()[:, 0:ET * P], wd2T)
    t_small = now(SY)

    t_mz1 = gp(lambda h: h.memset(id_f32.ap(), 0.0))
    t_af1 = gp(lambda h: h.affine_select(out=id_f32.ap(), in_=id_f32.ap(),
                                         compare_op=ALU.not_equal, fill=1.0, base=0,
                                         pattern=[[-1, P]], channel_multiplier=1),
               waits=[t_mz1])
    t_mz2 = gp(lambda h: h.memset(id_bf.ap(), 0.0), waits=[t_af1])
    tok_id = gp(lambda h: h.affine_select(out=id_bf.ap(), in_=id_bf.ap(),
                                          compare_op=ALU.not_equal, fill=1.0, base=0,
                                          pattern=[[-1, P]], channel_multiplier=1),
                waits=[t_mz2])
    tok_wd1c = sc(lambda h: h.activation(out=wd1T_bf.ap(), in_=wstage[0].ap()[:, 0:ET * P],
                                         func=AF.Copy), waits=[t_small])
    tok_wd2c = sc(lambda h: h.activation(out=wd2T_bf.ap(), in_=wstage[1].ap()[:, 0:ET * P],
                                         func=AF.Copy), waits=[t_small])
    t_lbc = dma(wstage[0].ap()[:, 0:ET * 32].rearrange("p (e m) -> p e m", e=ET),
                wbcT.rearrange("(e p) m -> p e m", p=P),
                waits=[tok_wd1c])
    tok_wbcc = sc(lambda h: h.activation(out=wbcT_bf.ap(), in_=wstage[0].ap()[:, 0:ET * 32],
                                         func=AF.Copy), waits=[t_lbc])
    tok_aexp = sc(lambda h: h.activation(out=a_neg.ap(), in_=a_neg.ap(), func=AF.Exp),
                  waits=[t_small])
    ve(lambda h: h.tensor_single_scalar(a_neg.ap(), a_neg.ap(), -1.0, ALU.mult),
       waits=[tok_aexp])
    tok_cz = ve(lambda h: h.memset(carries.ap(), 0.0))
    tok_tz = ve(lambda h: h.memset(t_acc.ap(), 0.0))
    tok_bz = ve(lambda h: h.memset(bc_acc.ap(), 0.0))

    inv_d = 1.0 / D
    ev_bank = {}
    t_rn_prev = None
    t_tr_prev = None
    for tt in range(NTT):
        par = tt % 2
        w = [now(VE), now(SC)] if tt >= 2 else []
        t_ld = dma(resid_t[par].ap(), resid[tt * P:(tt + 1) * P, :], waits=w)
        t_sum = ve(lambda h, par=par: h.reduce_sum(cb[0].ap(), resid_t[par].ap(), axis=AX.X),
                   waits=[t_ld])
        t_nm = ve(lambda h: h.tensor_single_scalar(cb[1].ap(), cb[0].ap(), -inv_d, ALU.mult),
                  waits=[t_sum])
        t_var = sc(lambda h, par=par: h.activation(out=junk.ap(), in_=resid_t[par].ap(),
                                                   func=AF.Square, bias=cb[1].ap(),
                                                   accum_out=cb[2].ap()),
                   waits=[t_nm])
        t_v = ve(lambda h: h.tensor_scalar(cb[3].ap(), cb[2].ap(), inv_d, EPS, ALU.mult, ALU.add),
                 waits=[t_var])
        t_l1 = sc(lambda h: h.activation(out=cb[4].ap(), in_=cb[3].ap(), func=AF.Ln), waits=[t_v])
        t_rstd = sc(lambda h: h.activation(out=cb[4].ap(), in_=cb[4].ap(), func=AF.Exp, scale=-0.5),
                    waits=[t_l1])
        t_rc = ve(lambda h: h.reciprocal(cb[5].ap(), cb[3].ap()), waits=[t_l1])
        t_vv = ve(lambda h: h.tensor_single_scalar(cb[6].ap(), cb[2].ap(), inv_d, ALU.mult),
                  waits=[t_rc])
        t_msq = ve(lambda h: h.tensor_tensor(cb[6].ap(), cb[6].ap(), cb[5].ap(), ALU.mult),
                   waits=[t_vv])
        t_t2 = ve(lambda h: h.tensor_single_scalar(cb[6].ap(), cb[6].ap(), EPS, ALU.add),
                  waits=[t_msq])
        t_l2 = sc(lambda h: h.activation(out=cb[7].ap(), in_=cb[6].ap(), func=AF.Ln), waits=[t_t2])
        t_s = sc(lambda h: h.activation(out=cb[7].ap(), in_=cb[7].ap(), func=AF.Exp, scale=-0.5),
                 waits=[t_l2])
        t_xn = ve(lambda h, par=par: h.tensor_scalar(xn_f32.ap(), resid_t[par].ap(),
                                                     cb[1].ap(), cb[4].ap(), ALU.add, ALU.mult),
                  waits=[t_rstd, t_rn_prev])
        t_rn = ve(lambda h: h.tensor_single_scalar(rnpre.ap(), xn_f32.ap(), cb[7].ap(), ALU.mult),
                  waits=[t_xn, t_s, t_tr_prev])
        t_rn_prev = t_rn
        for dt in range(DT):
            bank = dt % 2
            t_tr = pe(lambda h, dt=dt, bank=bank: h.transpose(
                          pbbf[bank][:, 0:P], rnpre.ap()[:, dt * P:(dt + 1) * P], id_bf.ap()),
                      waits=[t_rn, tok_id, ev_bank.get(bank)])
            ev_bank[bank] = sc(lambda h, dt=dt, tt=tt, bank=bank: h.activation(
                                   out=rnT.ap()[:, dt, tt * P:(tt + 1) * P],
                                   in_=pbbf[bank][:, 0:P],
                                   func=AF.Copy, scale=normw_sb.ap()[:, dt:dt + 1]),
                               waits=[t_tr])
            if dt == DT - 1:
                t_tr_prev = t_tr
    tok_p1a = now(SC)
    if debug_taps:
        for dt in range(DT):
            dma(taps["tap_rnT"][dt], rnT.ap()[:, dt, :], waits=[tok_p1a])

    castf = [junk, xn_f32]
    ps_state = {"cst": [None, None], "st": [None, None], "j": 0}

    def emit_prespill(njobs):
        for _ in range(njobs):
            j = ps_state["j"]
            if j >= 32:
                return
            ps_state["j"] = j + 1
            pj = j % 2
            wlf = [ps_state["cst"][pj]] if ps_state["cst"][pj] else [now(SC), now(VE), now(PE)]
            t_lf = dma(castf[pj].ap(), out_wT[j * P:(j + 1) * P, :], waits=wlf, eng="gpsimd")
            t_cst = ve(lambda h, fv=castf[pj].ap(), bv=castb[pj].ap(): h.tensor_copy(bv, fv),
                       waits=[t_lf, ps_state["st"][pj]])
            ps_state["cst"][pj] = t_cst
            ps_state["st"][pj] = dma(ogb_spill[j], castb[pj].ap(), waits=[t_cst], eng="gpsimd")

    t_csk = None
    t_cin = None
    g1last = [None, None]
    g2last = [None, None]
    prev_mm5 = None
    t_e3 = None
    t_e5 = None
    t_spill_prev = None
    t_xsp_prev = None
    t_c = None
    t_zp = None
    for e in range(ET):
        par = e % 2
        t_lsk = dma(wstage[0].ap().rearrange("p (a b) -> p a b", a=DT),
                    skip_wT.rearrange("(dt p) e -> p dt e", p=P)[:, :, e * P:(e + 1) * P],
                    waits=[t_csk if e > 0 else tok_wbcc])
        t_csk = sc(lambda h, par=par: h.activation(out=wbf[par].ap(), in_=wstage[0].ap(),
                                                   func=AF.Copy),
                   waits=[t_lsk, g1last[par]])
        t_lin = dma(wstage[1].ap().rearrange("p (a b) -> p a b", a=DT),
                    in_wT.rearrange("(dt p) e -> p dt e", p=P)[:, :, e * P:(e + 1) * P],
                    waits=[t_cin if e > 0 else tok_wd2c])
        t_cin = sc(lambda h, par=par: h.activation(out=wbf[2 + par].ap(), in_=wstage[1].ap(),
                                                   func=AF.Copy),
                   waits=[t_lin, g2last[par]])

        for tc in range(4):
            bank = tc % 2
            wmm = [t_csk, tok_p1a, ev_bank.get(bank)]
            for dt in range(DT):
                t_mm = pe(lambda h, dt=dt, bank=bank, tc=tc, par=par: h.matmul(
                              pbv[bank], wbf[par].ap()[:, dt * P:(dt + 1) * P],
                              rnT.ap()[:, dt, tc * 512:(tc + 1) * 512],
                              start=(dt == 0), stop=(dt == DT - 1)),
                          waits=wmm if dt == 0 else (), inc=(dt == DT - 1))
            ev_bank[bank] = sc(lambda h, tc=tc, bank=bank: h.activation(
                                   out=silu_st.ap()[:, tc * 512:(tc + 1) * 512],
                                   in_=pbv[bank], func=AF.Silu),
                               waits=[t_mm, t_spill_prev if tc <= 1 else None])
        g1last[par] = t_mm
        t_spill_prev = dma(silu_spill[e], silu_st.ap(), waits=[ev_bank[0], ev_bank[1]])

        for tc in range(4):
            bank = 2 + tc % 2
            wmm = [t_cin, ev_bank.get(bank)]
            for dt in range(DT):
                t_mm = pe(lambda h, dt=dt, bank=bank, tc=tc, par=par: h.matmul(
                              pbv[bank], wbf[2 + par].ap()[:, dt * P:(dt + 1) * P],
                              rnT.ap()[:, dt, tc * 512:(tc + 1) * 512],
                              start=(dt == 0), stop=(dt == DT - 1)),
                          waits=wmm if dt == 0 else (), inc=(dt == DT - 1))
            ev_bank[bank] = sc(lambda h, tc=tc, bank=bank: h.activation(
                                   out=xin_st.ap()[:, 8 + tc * 512:8 + (tc + 1) * 512],
                                   in_=pbv[bank], func=AF.Copy),
                               waits=[t_mm, t_c if tc <= 1 else None])
        g2last[par] = t_mm
        if e == 0:
            t_zp = ve(lambda h: h.memset(xin_st.ap()[:, 0:8], 0.0))
        t_c = ve(lambda h, e=e: h.tensor_scalar(convacc.ap(), xin_st.ap()[:, 5:5 + D],
                                                convw_sb.ap()[:, 4 * e:4 * e + 1],
                                                convb_sb.ap()[:, e:e + 1], ALU.mult, ALU.add),
                 waits=[ev_bank[2], ev_bank[3], t_zp, t_xsp_prev])
        for k in range(1, 4):
            t_c = ve(lambda h, e=e, k=k: h.scalar_tensor_tensor(
                         convacc.ap(), xin_st.ap()[:, 5 + k:5 + k + D],
                         convw_sb.ap()[:, 4 * e + k:4 * e + k + 1], convacc.ap(),
                         ALU.mult, ALU.add),
                     waits=[t_c])
        t_x = sc(lambda h: h.activation(out=x_f32.ap(), in_=convacc.ap(), func=AF.Silu),
                 waits=[t_c, t_xsp_prev])
        t_xsp = dma(x_spill[e], x_f32.ap(), waits=[t_x])
        t_xb = sc(lambda h: h.activation(out=xb_bf.ap(), in_=x_f32.ap(), func=AF.Copy),
                  waits=[t_x, prev_mm5])
        t_xsp_prev = t_xsp

        wg3 = [t_xb, tok_wd1c, t_e5 or tok_tz]
        t_mm3 = None
        for tc in range(4):
            t_mm3 = pe(lambda h, e=e, tc=tc: h.matmul(
                           pDv[:, tc * 512:(tc + 1) * 512],
                           wd1T_bf.ap()[:, e * P:(e + 1) * P],
                           xb_bf.ap()[:, tc * 512:(tc + 1) * 512],
                           start=True, stop=True),
                       waits=wg3 if tc == 0 else (), inc=(tc == 3))
        t_e3 = ve(lambda h: h.tensor_tensor(t_acc.ap(), pDv, t_acc.ap(), ALU.add),
                  waits=[t_mm3, tok_tz])
        t_mm5 = None
        for tc in range(4):
            t_mm5 = pe(lambda h, e=e, tc=tc: h.matmul(
                           pDv[0:32, tc * 512:(tc + 1) * 512],
                           wbcT_bf.ap()[:, e * 32:(e + 1) * 32],
                           xb_bf.ap()[:, tc * 512:(tc + 1) * 512],
                           start=True, stop=True),
                       waits=[t_e3, tok_wbcc] if tc == 0 else (), inc=(tc == 3))
        t_e5 = ve(lambda h: h.tensor_tensor(bc_acc.ap(), pDv[0:32, :], bc_acc.ap(), ALU.add),
                  waits=[t_mm5, tok_bz])
        prev_mm5 = t_mm5
        emit_prespill(2)

    if debug_taps:
        dma(taps["tap_x"], x_spill, waits=[t_xsp_prev])

    emit_prespill(64)

    t_d1 = dma(cc_t_in[0:128, :], t_acc.ap(), waits=[t_e3, t_e5])
    t_d2 = dma(cc_t_in[128:160, :], bc_acc.ap())
    tok_cct = B.emit("gpsimd",
                     lambda h: h.collective_compute("AllReduce", ALU.add, replica_groups=PAIRS,
                                                    ins=[cc_t_in.opt()], outs=[cc_t_out.opt()]),
                     waits=[t_d1, t_d2], inc=(CCT, 1))
    t_tb = dma(wstage[0].ap(), cc_t_out[0:128, :], waits=[tok_cct, t_csk])
    tok_tcast = sc(lambda h: h.activation(out=t_bf.ap(), in_=wstage[0].ap(), func=AF.Copy),
                   waits=[t_tb])
    if debug_taps:
        dma(taps["tap_t"], cc_t_out[0:128, :], waits=[tok_cct])
        dma(taps["tap_bc"], cc_t_out[128:160, :], waits=[tok_cct])

    if stop_after == "p1":
        return _finalize(nc, B)

    B.barrier_into(snapshot())
    tok_az = ve(lambda h: (h.memset(a_big[0].ap(), 0.0), h.memset(a_big[1].ap(), 0.0))[1])
    tok_cz2 = ve(lambda h: h.memset(carries.ap(), 0.0))

    def bc_src(n, c0, c1):
        two = cc_t_out[128 + n:145 + n:16, c0:c1]
        return two.unsqueeze(0).broadcast_to([P, 2, c1 - c0])

    u3 = [u.ap().rearrange("p (e s) -> p e s", s=SEG) for u in u_big]
    a3 = [a.ap().rearrange("p (e s) -> p e s", s=SEG) for a in a_big]
    car3 = carries.ap().rearrange("p (e m) -> p e m", m=NST)
    t_gat = None
    t_dx = None
    hist_u = [None, None]
    hist_p = [None, None]
    hist_yadd = [None, None]
    hist_exp = [None, None]
    hist_tts = [None, None]
    t_yadd = None
    t_u = None
    t_p = None
    prev_ygd0 = prev_ygd1 = None
    for c in range(NCH):
        t_delp = [None, None]
        c0, c1 = c * LC, (c + 1) * LC
        wfree = [t_gat] if t_gat else []
        t_lx = dma(x3.ap(), x_spill.rearrange("e p l -> p e l")[:, :, c0:c1], waits=wfree)
        t_ls = dma(silu_ch.ap(), silu_spill.rearrange("e p l -> p e l")[:, :, c0:c1], waits=wfree)
        t_del = None
        for e in range(ET):
            bank = e % 2
            wg4 = [tok_tcast, tok_wd2c, ev_bank.get(bank)]
            if e < 2 and t_dx:
                wg4.append(t_dx)
            t_g4 = pe(lambda h, e=e, bank=bank, c0=c0, c1=c1: h.matmul(
                          pbv[bank][:, 0:LC], wd2T_bf.ap()[:, e * P:(e + 1) * P],
                          t_bf.ap()[:, c0:c1], start=True, stop=True),
                      waits=wg4, inc=True)
            sb3 = sp3[e % 2]
            wdel_guard = ([t_delp[e % 2]] if t_delp[e % 2]
                          else [t_gat, prev_ygd0, prev_ygd1])
            t_s1 = sc(lambda h, e=e, bank=bank: h.activation(
                          out=delta3.ap()[:, e, :], in_=pbv[bank][:, 0:LC], func=AF.Relu,
                          bias=wd2b_sb.ap()[:, e:e + 1]),
                      waits=[t_g4] + ([t_gat, prev_ygd0, prev_ygd1, t_u, now(SC)]
                                      if e == 0 else []))
            t_s2 = sc(lambda h, e=e, bank=bank, sb3=sb3: h.activation(
                          out=sb3.ap(), in_=pbv[bank][:, 0:LC], func=AF.Abs,
                          bias=wd2b_sb.ap()[:, e:e + 1]),
                      waits=[t_s1] + wdel_guard)
            ev_bank[bank] = t_s2
            t_s3 = sc(lambda h, sb3=sb3: h.activation(out=sb3.ap(), in_=sb3.ap(), func=AF.Exp,
                                                      scale=-1.0),
                      waits=[t_s2])
            t_s4 = sc(lambda h, sb3=sb3: h.activation(out=sb3.ap(), in_=sb3.ap(), func=AF.Ln,
                                                      bias=1.0),
                      waits=[t_s3])
            t_del = ve(lambda h, e=e, sb3=sb3: h.tensor_tensor(delta3.ap()[:, e, :],
                                                               delta3.ap()[:, e, :], sb3.ap(),
                                                               ALU.add),
                       waits=[t_s4])
            t_delp[e % 2] = t_del
        t_dly = t_del
        t_dx = ve(lambda h: h.tensor_tensor(
                      dx3.ap().rearrange("p e l -> p (e l)"),
                      delta3.ap().rearrange("p e l -> p (e l)"),
                      x3.ap().rearrange("p e l -> p (e l)"), ALU.mult),
                  waits=[t_dly, t_lx, t_u])
        t_yz = ve(lambda h: h.memset(yacc3.ap(), 0.0), waits=[t_gat])
        bc_tok = {0: dma(bcn[0].ap(), bc_src(0, c0, c1),
                         waits=[tok_cct, hist_u[0], hist_p[0]])}
        for n in range(NST):
            pn = n % 2
            if n + 1 < NST:
                bc_tok[n + 1] = dma(bcn[(n + 1) % 2].ap(), bc_src(n + 1, c0, c1),
                                    waits=[hist_u[(n + 1) % 2], hist_p[(n + 1) % 2]])
            t_a = None
            for e in range(ET):
                wexp = [t_dly]
                if e == 0:
                    wexp.append(hist_p[pn] or tok_az)
                t_a = sc(lambda h, e=e, n=n, pn=pn: h.activation(
                             out=a_big[pn].ap()[:, e * SEG + 1:e * SEG + 1 + LC],
                             in_=delta3.ap()[:, e, :], func=AF.Exp,
                             scale=a_neg.ap()[:, e * NST + n:e * NST + n + 1]),
                         waits=wexp)
            hist_exp[pn] = t_a
            t_u = ve(lambda h, pn=pn: h.tensor_tensor(
                         u3[pn][:, :, 1:SEG], dx3.ap(),
                         bcn[pn].ap()[:, 0:1, :].broadcast_to([P, ET, LC]), ALU.mult),
                     waits=[t_dx, bc_tok[n], hist_yadd[pn]])
            hist_u[pn] = t_u
            t_cc = ve(lambda h, n=n, pn=pn: h.tensor_copy(u3[pn][:, :, 0:1], car3[:, :, n:n + 1]),
                      waits=[t_u, tok_cz2])
            t_h = ve(lambda h, pn=pn: h.tensor_tensor_scan(a_big[pn].ap(), a_big[pn].ap(),
                                                           u_big[pn].ap(), 0.0,
                                                           ALU.mult, ALU.add),
                     waits=[t_cc, t_a, tok_az])
            hist_tts[pn] = t_h
            t_nc = ve(lambda h, n=n, pn=pn: h.tensor_copy(car3[:, :, n:n + 1],
                                                          a3[pn][:, :, SEG - 1:SEG]),
                      waits=[t_h])
            t_p = ve(lambda h, pn=pn: h.tensor_tensor(
                         u3[pn][:, :, 1:SEG], a3[pn][:, :, 1:SEG],
                         bcn[pn].ap()[:, 1:2, :].broadcast_to([P, ET, LC]), ALU.mult),
                     waits=[t_nc])
            hist_p[pn] = t_p
            t_yadd = ve(lambda h, pn=pn: h.tensor_tensor(yacc3.ap(), yacc3.ap(),
                                                         u3[pn][:, :, 1:SEG], ALU.add),
                        waits=[t_p, t_yz])
            hist_yadd[pn] = t_yadd
        t_g2 = [None, None]
        t_ygd = [None, None]
        for e in range(ET):
            pg = e % 2
            t_g1 = ve(lambda h, e=e: h.scalar_tensor_tensor(
                          ygf.ap(), x3.ap()[:, e, :], wdvec_sb.ap()[:, e:e + 1],
                          yacc3.ap()[:, e, :], ALU.mult, ALU.add),
                      waits=([t_ls, t_yadd] if e == 0 else []) + [t_ygd[pg]])
            t_g2[pg] = ve(lambda h, e=e, pg=pg: h.tensor_tensor(yg1[pg].ap(), ygf.ap(),
                                                                silu_ch.ap()[:, e, :], ALU.mult),
                          waits=[t_g1])
            t_ygd[pg] = dma(yg_in[c][e * P:(e + 1) * P, :], yg1[pg].ap(), waits=[t_g2[pg]])
        t_gat = t_g2[1]
        prev_ygd0, prev_ygd1 = t_ygd[0], t_ygd[1]
        B.emit("gpsimd",
               lambda h, c=c: h.collective_compute("AllGather", ALU.bypass, replica_groups=PAIRS,
                                                   ins=[yg_in[c].opt()], outs=[yg_out[c].opt()]),
               waits=[t_ygd[0], t_ygd[1]], inc=(CCY[c], 1))
        if debug_taps:
            dma(taps["tap_delta"][c], delta3.ap().rearrange("p e l -> p (e l)"), waits=[t_dx])

    if stop_after == "p2":
        return _finalize(nc, B)

    B.barrier_into(snapshot())
    ev_bank = {}
    ev2 = {}
    t_prev = None
    t_xb = None
    t_spx = None
    for tt in range(TTH):
        t_ld = dma(p3res.ap(), resid_ffn[tt * P:(tt + 1) * P, :],
                   waits=[t_prev, now(SC), now(VE)] if t_prev else [])
        t_sum = ve(lambda h: h.reduce_sum(cb[0].ap(), p3res.ap(), axis=AX.X), waits=[t_ld])
        t_nm = ve(lambda h: h.tensor_single_scalar(cb[1].ap(), cb[0].ap(), -inv_d, ALU.mult),
                  waits=[t_sum])
        t_var = sc(lambda h: h.activation(out=junk3.ap(), in_=p3res.ap(), func=AF.Square,
                                          bias=cb[1].ap(), accum_out=cb[2].ap()),
                   waits=[t_nm])
        t_v = ve(lambda h: h.tensor_scalar(cb[3].ap(), cb[2].ap(), inv_d, EPS, ALU.mult, ALU.add),
                 waits=[t_var])
        t_l1 = sc(lambda h: h.activation(out=cb[4].ap(), in_=cb[3].ap(), func=AF.Ln), waits=[t_v])
        t_rstd = sc(lambda h: h.activation(out=cb[4].ap(), in_=cb[4].ap(), func=AF.Exp, scale=-0.5),
                    waits=[t_l1])
        t_xn = ve(lambda h: h.tensor_scalar(p3xn.ap(), p3res.ap(), cb[1].ap(), cb[4].ap(),
                                            ALU.add, ALU.mult),
                  waits=[t_rstd, t_xb, t_spx])
        t_spx = dma(xn_spill[tt], p3xn.ap(), waits=[t_xn])
        t_xb = sc(lambda h: h.activation(out=p3xnb.ap(), in_=p3xn.ap(), func=AF.Copy),
                  waits=[t_xn])
        for dt in range(DT):
            bank = dt % 2
            t_tr = pe(lambda h, dt=dt, bank=bank: h.transpose(
                          pbbf[bank][:, 0:P], p3xnb.ap()[:, dt * P:(dt + 1) * P], id_bf.ap()),
                      waits=[t_xb, tok_id, ev_bank.get(bank)])
            ev_bank[bank] = sc(lambda h, dt=dt, tt=tt, bank=bank: h.activation(
                                   out=xnT3.ap()[:, dt, tt * P:(tt + 1) * P],
                                   in_=pbbf[bank][:, 0:P], func=AF.Copy),
                               waits=[t_tr])
        t_prev = ev_bank[1]
    tok_xnT = now(SC)

    t_lg = dma(gate_f.ap().rearrange("p (a b) -> p a b", a=DT),
               gate_w.rearrange("(dt p) m -> p dt m", p=P))
    tok_gc = sc(lambda h: h.activation(out=gate_b.ap(), in_=gate_f.ap(), func=AF.Copy),
                waits=[t_lg])
    ev0 = None
    t_sc = None
    for tt in range(TTH):
        t_mmg = None
        for dt in range(DT):
            t_mmg = pe(lambda h, dt=dt, tt=tt: h.matmul(
                           pbv[0][:, 0:8], xnT3.ap()[:, dt, tt * P:(tt + 1) * P],
                           gate_b.ap()[:, dt * 8:(dt + 1) * 8],
                           start=(dt == 0), stop=(dt == DT - 1)),
                       waits=[tok_gc, tok_xnT, ev0, ev_bank.get(0)] if dt == 0 else (),
                       inc=(dt == DT - 1))
        ev0 = ve(lambda h, tt=tt: h.tensor_copy(logits.ap()[:, tt * 8:(tt + 1) * 8],
                                                pbv[0][:, 0:8]),
                 waits=[t_mmg])
        tpair = ev0
        off = 0
        for k in range(1, 8):
            tpair = ve(lambda h, tt=tt, k=k, off=off: h.tensor_tensor(
                           pair28.ap()[:, off:off + 8 - k],
                           logits.ap()[:, tt * 8:tt * 8 + 8 - k],
                           logits.ap()[:, tt * 8 + k:tt * 8 + 8], ALU.add),
                       waits=[tpair])
            off += 8 - k
        t_sc = ve(lambda h, tt=tt: h.reduce_max(scale_sb.ap()[:, tt:tt + 1], pair28.ap(),
                                                axis=AX.X),
                  waits=[tpair])
    tok_scale = t_sc

    wcast = [None, None]
    ev_pD = None
    exmm = [None, None]
    tr_last = None
    for ot in range(DT):
        par = ot % 2
        t_lw = dma(wstage[par].ap().rearrange("p (a b) -> p a b", a=DT),
                   exp_w.rearrange("(dt p) o -> p dt o", p=P)[:, :, ot * P:(ot + 1) * P],
                   waits=[wcast[par]])
        t_wc = sc(lambda h, par=par: h.activation(out=wbf[par].ap(), in_=wstage[par].ap(),
                                                  func=AF.Copy),
                  waits=[t_lw, exmm[par]])
        wcast[par] = t_wc
        t_mme = None
        for tc in range(2):
            for dt in range(DT):
                t_mme = pe(lambda h, dt=dt, tc=tc, par=par: h.matmul(
                               pDv[:, tc * 512:(tc + 1) * 512],
                               wbf[par].ap()[:, dt * P:(dt + 1) * P],
                               xnT3.ap()[:, dt, tc * 512:(tc + 1) * 512],
                               start=(dt == 0), stop=(dt == DT - 1)),
                           waits=[t_wc, ev_pD] if (dt == 0 and tc == 0) else (),
                           inc=(dt == DT - 1))
        exmm[par] = t_mme
        ev_pD = sc(lambda h: h.activation(out=moeT_st.ap(), in_=pDv[:, 0:1024], func=AF.Copy),
                   waits=[t_mme, tr_last])
        for tt in range(TTH):
            bank = 2 + tt % 2
            t_tr = pe(lambda h, tt=tt, bank=bank: h.transpose(
                          pbv[bank][:, 0:P], moeT_st.ap()[:, tt * P:(tt + 1) * P], id_f32.ap()),
                      waits=[ev_pD, tok_id, ev2.get(bank)])
            ev2[bank] = sc(lambda h, tt=tt, ot=ot, bank=bank: h.activation(
                               out=yffn3.ap()[:, tt, ot * P:(ot + 1) * P],
                               in_=pbv[bank][:, 0:P], func=AF.Copy,
                               scale=scale_sb.ap()[:, tt:tt + 1]),
                           waits=[t_tr, tok_scale])
            tr_last = t_tr
    tok_moe = [now(SC), now(VE)]

    ev_dn = None
    t_h2 = None
    upmm = [None, None]
    dnmm = [None, None]
    for q in range(4):
        for icq in range(16):
            ic = q * 16 + icq
            par = ic % 2
            t_lu = dma(wstage[par].ap().rearrange("p (a b) -> p a b", a=DT),
                       up_w.rearrange("(dt p) i -> p dt i", p=P)[:, :, ic * P:(ic + 1) * P],
                       waits=[wcast[par]])
            t_wc = sc(lambda h, par=par: h.activation(out=wbf[par].ap(), in_=wstage[par].ap(),
                                                      func=AF.Copy),
                      waits=[t_lu, upmm[par] if upmm[par] else exmm[par]])
            wcast[par] = t_wc
            for tc in range(2):
                bank = tc
                wmm = [t_wc, ev_bank.get(bank)]
                if q == 0 and icq == 0 and tc == 0:
                    wmm += tok_moe
                t_mmu = None
                for dt in range(DT):
                    t_mmu = pe(lambda h, dt=dt, tc=tc, par=par, bank=bank: h.matmul(
                                   pbv[bank], wbf[par].ap()[:, dt * P:(dt + 1) * P],
                                   xnT3.ap()[:, dt, tc * 512:(tc + 1) * 512],
                                   start=(dt == 0), stop=(dt == DT - 1)),
                               waits=wmm if dt == 0 else (), inc=(dt == DT - 1))
                upmm[par] = t_mmu
                t_r = sc(lambda h, bank=bank: h.activation(out=relu_st.ap(), in_=pbv[bank],
                                                           func=AF.Relu),
                         waits=[t_mmu, t_h2])
                ev_bank[bank] = t_r
                t_h2 = sc(lambda h, icq=icq, tc=tc: h.activation(
                              out=h2T.ap()[:, icq, tc * 512:(tc + 1) * 512],
                              in_=relu_st.ap(), func=AF.Square),
                          waits=[t_r, ev_dn if (q > 0 and icq == 0 and tc == 0) else None])
        tok_up = t_h2
        for dto in range(DT):
            par = dto % 2
            t_ldw = dma(wstage[par].ap().rearrange("p (a b) -> p a b", a=16),
                        down_w.rearrange("(i p) d -> p i d", p=P)[:, q * 16:(q + 1) * 16,
                                                                 dto * P:(dto + 1) * P],
                        waits=[wcast[par]])
            t_dwc = sc(lambda h, par=par: h.activation(out=wbf[2 + par].ap(), in_=wstage[par].ap(),
                                                       func=AF.Copy),
                       waits=[t_ldw, dnmm[par]])
            wcast[par] = t_dwc
            t_mmd = None
            for icq in range(16):
                for tc in range(2):
                    t_mmd = pe(lambda h, icq=icq, tc=tc, par=par: h.matmul(
                                   pDv[:, tc * 512:(tc + 1) * 512],
                                   wbf[2 + par].ap()[:, icq * P:(icq + 1) * P],
                                   h2T.ap()[:, icq, tc * 512:(tc + 1) * 512],
                                   start=(icq == 0), stop=(icq == 15)),
                               waits=[t_dwc, tok_up, ev_dn] if (icq == 0 and tc == 0) else (),
                               inc=(icq == 15))
            dnmm[par] = t_mmd
            ev_dn = sc(lambda h: h.activation(out=moeT_st.ap(), in_=pDv[:, 0:1024], func=AF.Copy),
                       waits=[t_mmd, tr_last])
            for tt in range(TTH):
                bank = 2 + tt % 2
                t_tr = pe(lambda h, tt=tt, bank=bank: h.transpose(
                              pbv[bank][:, 0:P], moeT_st.ap()[:, tt * P:(tt + 1) * P],
                              id_f32.ap()),
                          waits=[ev_dn, ev2.get(bank)])
                ev2[bank] = ve(lambda h, tt=tt, dto=dto, bank=bank: h.tensor_tensor(
                                   yffn3.ap()[:, tt, dto * P:(dto + 1) * P],
                                   pbv[bank][:, 0:P],
                                   yffn3.ap()[:, tt, dto * P:(dto + 1) * P], ALU.add),
                               waits=[t_tr])
                tr_last = t_tr
    tok_down = [now(VE), now(SC), now(PE)]

    t_lgm = dma(wstage[0].ap(), ln_g.broadcast_to([P, D]), waits=[wcast[0], wcast[1]])
    t_lbt = dma(wstage[1].ap(), ln_b.broadcast_to([P, D]), waits=[wcast[0], wcast[1]])
    t_fin = None
    t_lr = None
    t_lxn = None
    for tt in range(TTH):
        t_sum = ve(lambda h, tt=tt: h.reduce_sum(cb[0].ap(), yffn3.ap()[:, tt, :], axis=AX.X),
                   waits=tok_down if tt == 0 else [])
        t_nm = ve(lambda h: h.tensor_single_scalar(cb[1].ap(), cb[0].ap(), -inv_d, ALU.mult),
                  waits=[t_sum])
        t_var = sc(lambda h, tt=tt: h.activation(out=junk3.ap(), in_=yffn3.ap()[:, tt, :],
                                                 func=AF.Square, bias=cb[1].ap(),
                                                 accum_out=cb[2].ap()),
                   waits=[t_nm] + (tok_down if tt == 0 else []))
        t_v = ve(lambda h: h.tensor_scalar(cb[3].ap(), cb[2].ap(), inv_d, EPS, ALU.mult, ALU.add),
                 waits=[t_var])
        t_l1 = sc(lambda h: h.activation(out=cb[4].ap(), in_=cb[3].ap(), func=AF.Ln), waits=[t_v])
        t_rstd = sc(lambda h: h.activation(out=cb[4].ap(), in_=cb[4].ap(), func=AF.Exp, scale=-0.5),
                    waits=[t_l1])
        t_app = ve(lambda h, tt=tt: h.tensor_scalar(yffn3.ap()[:, tt, :], yffn3.ap()[:, tt, :],
                                                    cb[1].ap(), cb[4].ap(), ALU.add, ALU.mult),
                   waits=[t_rstd])
        t_gm = ve(lambda h, tt=tt: h.tensor_tensor(yffn3.ap()[:, tt, :], yffn3.ap()[:, tt, :],
                                                   wstage[0].ap(), ALU.mult),
                  waits=[t_app, t_lgm])
        t_bt = ve(lambda h, tt=tt: h.tensor_tensor(yffn3.ap()[:, tt, :], yffn3.ap()[:, tt, :],
                                                   wstage[1].ap(), ALU.mult_op1 if False else ALU.add),
                  waits=[t_gm, t_lbt])
        t_lr = dma(p3res.ap(), resid_ffn[tt * P:(tt + 1) * P, :],
                   waits=[t_fin] + (tok_down if tt == 0 else []))
        t_lxn = dma(p3xn.ap(), xn_spill[tt], waits=[t_fin] if t_fin else [])
        t_a1 = ve(lambda h, tt=tt: h.tensor_tensor(yffn3.ap()[:, tt, :], yffn3.ap()[:, tt, :],
                                                   p3res.ap(), ALU.add),
                  waits=[t_bt, t_lr])
        t_fin = ve(lambda h, tt=tt: h.tensor_tensor(yffn3.ap()[:, tt, :], yffn3.ap()[:, tt, :],
                                                    p3xn.ap(), ALU.add),
                   waits=[t_a1, t_lxn])
    tok_base = t_fin
    if debug_taps:
        for tt in range(TTH):
            dma(taps["tap_ffn"][tt], yffn3.ap()[:, tt, :], waits=[tok_base])

    if stop_after == "p3":
        return _finalize(nc, B)

    B.barrier_into(snapshot())
    t_selp = [None, None]
    t_cast4 = [None, None]
    g6mm = [None, None]
    ev_pb0 = {}
    tr4_last = None
    t_add_last = None
    for ps in range(2):
        sel_tok = []
        for ci in range(2):
            ch = ps * 2 + ci
            t_la = dma(yfullA.ap(), yg_out[ch].rearrange("(e p) l -> p e l", p=P),
                       waits=[(CCY[ch], 1), t_selp[0], t_selp[1]])
            t_lb = dma(yfullB.ap(), yg_out[ch + 4].rearrange("(e p) l -> p e l", p=P),
                       waits=[(CCY[ch + 4], 1)])
            t_s1 = ve(lambda h, ci=ci: h.tensor_single_scalar(
                          ysel[ci].ap().rearrange("p e l -> p (e l)"),
                          yfullA.ap().rearrange("p e l -> p (e l)"),
                          msel_sb.ap()[:, 0:1], ALU.mult),
                      waits=[t_la, now(PE)])
            t_sel = ve(lambda h, ci=ci: h.scalar_tensor_tensor(
                           ysel[ci].ap().rearrange("p e l -> p (e l)"),
                           yfullB.ap().rearrange("p e l -> p (e l)"),
                           msel_sb.ap()[:, 1:2],
                           ysel[ci].ap().rearrange("p e l -> p (e l)"),
                           ALU.mult, ALU.add),
                       waits=[t_s1, t_lb])
            t_selp[ci] = t_sel
            sel_tok.append(t_sel)
        for dt in range(DT):
            for hb in range(2):
                par = (dt * 2 + hb) % 2
                t_lo = dma(wbf[par].ap().rearrange("p (a b) -> p a b", a=DT),
                           ogb_spill.rearrange("g p d -> p g d")[:, hb * 16:(hb + 1) * 16,
                                                                dt * P:(dt + 1) * P],
                           waits=[g6mm[par]], eng=("sync" if par == 0 else "scalar"))
                for ci in range(2):
                    for eti in range(16):
                        et = hb * 16 + eti
                        t_mmo = pe(lambda h, eti=eti, et=et, par=par, ci=ci: h.matmul(
                                       pbv[ci][:, 0:LC], wbf[par].ap()[:, eti * P:(eti + 1) * P],
                                       ysel[ci].ap()[:, et, :],
                                       start=(et == 0), stop=(et == 31)),
                                   waits=([t_lo, sel_tok[ci], ev_pb0.get(ci)]
                                          if eti == 0 else ()),
                                   inc=(eti == 15))
                    g6mm[par] = t_mmo
            for ci in range(2):
                ch = ps * 2 + ci
                ev_pb0[ci] = sc(lambda h, ci=ci: h.activation(
                                    out=attnT.ap()[:, ci, :], in_=pbv[ci][:, 0:LC], func=AF.Copy),
                                waits=[g6mm[(dt * 2 + 1) % 2], g6mm[dt * 2 % 2], tr4_last])
                for k in range(2):
                    bank = 2 + k
                    t_tr = pe(lambda h, ci=ci, k=k, bank=bank: h.transpose(
                                  pbv[bank][:, 0:P], attnT.ap()[:, ci, k * P:(k + 1) * P],
                                  id_f32.ap()),
                              waits=[ev_pb0[ci], ev2.get(bank)])
                    ev2[bank] = ve(lambda h, ch=ch, k=k, dt=dt, bank=bank: h.tensor_tensor(
                                       yffn3.ap()[:, ch * 2 + k, dt * P:(dt + 1) * P],
                                       pbv[bank][:, 0:P],
                                       yffn3.ap()[:, ch * 2 + k, dt * P:(dt + 1) * P], ALU.add),
                                   waits=[t_tr, tok_base])
                    tr4_last = t_tr
        for ci in range(2):
            ch = ps * 2 + ci
            for k in range(2):
                dma(out[(ch * 256 + k * P):(ch * 256 + (k + 1) * P), :],
                    yffn3.ap()[:, ch * 2 + k, :], waits=[ev2[2], ev2[3]])

    return _finalize(nc, B)


def _finalize(nc, B):
    B.emit("sync", lambda h: None, waits=[(c, c.n) for c in B.chans if c.n > 0])
    with contextlib.ExitStack() as st:
        for c in B.chans:
            c.h = st.enter_context(nc.semaphore(c.name))
        block = st.enter_context(nc.Block())
        for eng in ENGS:
            def mk(eng):
                def emit_all(h):
                    for f in B.ops[eng]:
                        f(h)
                return emit_all
            getattr(block, eng)(mk(eng))
    return nc


def _prep_core(inputs, b, hf):
    asc = np.ascontiguousarray
    eh = slice(hf * EH, (hf + 1) * EH)
    f32 = np.float32
    resid = np.asarray(inputs["L0_attn_resid"], f32)
    pmaj = lambda v: asc(np.asarray(v, f32).reshape(-1, P).T)
    m = {
        "resid": asc(resid[b]),
        "resid_ffn": asc(resid[b, hf * TOKH:(hf + 1) * TOKH]),
        "in_wT": asc(np.asarray(inputs["L0_attn_in_w"], f32)[eh].T),
        "skip_wT": asc(np.asarray(inputs["L0_attn_skip_w"], f32)[eh].T),
        "wd1T": asc(np.asarray(inputs["L0_attn_wd1"], f32)[:, eh].T),
        "wd2T": asc(np.asarray(inputs["L0_attn_wd2"], f32)[eh].T),
        "wbcT": asc(np.concatenate([np.asarray(inputs["L0_attn_wB"], f32)[:, eh],
                                    np.asarray(inputs["L0_attn_wC"], f32)[:, eh]], 0).T),
        "convw": asc(np.asarray(inputs["L0_attn_conv_w"], f32)[eh]
                     .reshape(ET, P, 4).transpose(1, 0, 2).reshape(P, ET * 4)),
        "convb": pmaj(np.asarray(inputs["L0_attn_conv_b"], f32)[eh]),
        "wd2b": pmaj(np.asarray(inputs["L0_attn_wd2_b"], f32)[eh]),
        "alog": asc(np.asarray(inputs["L0_attn_A_log"], f32)[eh]
                    .reshape(ET, P, NST).transpose(1, 0, 2).reshape(P, ET * NST)),
        "wdvec": pmaj(np.asarray(inputs["L0_attn_W_D"], f32)[eh]),
        "normw": pmaj(np.asarray(inputs["L0_attn_norm_w"], f32)),
        "out_wT": asc(np.asarray(inputs["L0_attn_out_w"], f32).T),
        "gate_w": asc(np.asarray(inputs["L0_ffn_gate_w"], f32)),
        "up_w": asc(np.asarray(inputs["L0_ffn_up_w"], f32)),
        "down_w": asc(np.asarray(inputs["L0_ffn_down_w"], f32)),
        "exp_w": asc(np.asarray(inputs["L0_ffn_expert_weight"], f32)),
        "ln_g": asc(np.asarray(inputs["L0_ffn_ln_gamma"], f32).reshape(1, D)),
        "ln_b": asc(np.asarray(inputs["L0_ffn_ln_beta"], f32).reshape(1, D)),
        "hsel": asc(np.broadcast_to(
            np.array([1.0, 0.0] if hf == 0 else [0.0, 1.0], f32), (P, 2)).copy()),
    }
    return m


def kernel(**inputs):
    nc = build_nc()
    in_maps = [_prep_core(inputs, c // 2, c % 2) for c in range(8)]
    res = run_bass_kernel_spmd(nc, in_maps, core_ids=list(range(8)))
    outp = np.zeros((4, L, D), np.float32)
    for c in range(8):
        b, hf = c // 2, c % 2
        outp[b, hf * TOKH:(hf + 1) * TOKH, :] = np.asarray(res.results[c]["out"])
    return outp


# version 19
# speedup vs baseline: 1.0379x; 1.0012x over previous
import contextlib
import numpy as np
import ml_dtypes
import concourse.bass as bass
import concourse.mybir as mybir
from concourse.bass_utils import run_bass_kernel_spmd

F32 = mybir.dt.float32
BF16 = mybir.dt.bfloat16
AF = mybir.ActivationFunctionType
ALU = mybir.AluOpType
AX = mybir.AxisListType

P = 128
D = 2048
EH = 2048
L = 2048
ET = 16
DT = 16
NTT = 16
LC = 256
NCH = 8
NST = 16
SEG = LC + 1
TOKH = 1024
TTH = 8
II = 8192
EPS = 1e-5
PAIRS = [[0, 1], [2, 3], [4, 5], [6, 7]]
ENGS = ("sync", "tensor", "scalar", "vector", "gpsimd")


class Chan:
    __slots__ = ("name", "n", "h")
    def __init__(self, name):
        self.name, self.n, self.h = name, 0, None


class Builder:
    def __init__(self):
        self.ops = {e: [] for e in ENGS}
        self.chans = []
        self.pending = {e: [] for e in ENGS}

    def chan(self, name):
        c = Chan(name)
        self.chans.append(c)
        return c

    def barrier_into(self, toks):
        for e in ENGS:
            self.pending[e].extend(toks)

    def emit(self, eng, fn, waits=(), inc=None):
        if inc is not None:
            inc[0].n += inc[1]
        tok = None if inc is None else (inc[0], inc[0].n)
        waits = tuple(self.pending[eng]) + tuple(w for w in waits if w is not None)
        self.pending[eng] = []

        def f(h, fn=fn, waits=waits, inc=inc):
            seen = {}
            for c, v in waits:
                if seen.get(id(c), -1) < v:
                    seen[id(c)] = v
                    h.wait_ge(c.h, v)
            inst = fn(h)
            if inc is not None and inst is not None:
                inst.then_inc(inc[0].h, inc[1])
        self.ops[eng].append(f)
        return tok


def build_nc(stop_after=None, debug_taps=False):
    nc = bass.Bass("TRN2", target_bir_lowering=False, debug=False)

    def dram_in(name, shape, dtype=F32):
        return nc.dram_tensor(name, list(shape), dtype, kind="ExternalInput").ap()

    resid = dram_in("resid", (L, D))
    resid_ffn = dram_in("resid_ffn", (TOKH, D))
    in_wT = dram_in("in_wT", (D, EH))
    skip_wT = dram_in("skip_wT", (D, EH))
    wd1T = dram_in("wd1T", (EH, P))
    wd2T = dram_in("wd2T", (P, EH))
    wbcT = dram_in("wbcT", (EH, 32))
    convw = dram_in("convw", (P, ET * 4))
    convb = dram_in("convb", (P, ET))
    wd2b = dram_in("wd2b", (P, ET))
    alog = dram_in("alog", (P, ET * NST))
    wdvec = dram_in("wdvec", (P, ET))
    normw = dram_in("normw", (P, DT))
    out_wT = dram_in("out_wT", (2 * EH, D))
    gate_w = dram_in("gate_w", (D, 8))
    up_w = dram_in("up_w", (D, II))
    down_w = dram_in("down_w", (II, D))
    exp_w = dram_in("exp_w", (D, D))
    ln_g = dram_in("ln_g", (1, D))
    ln_b = dram_in("ln_b", (1, D))
    hsel = dram_in("hsel", (P, 2))

    out = nc.dram_tensor("out", [TOKH, D], F32, kind="ExternalOutput").ap()

    x_spill = nc.dram_tensor("x_spill", [ET, P, L], F32).ap()
    ogb_spill = nc.dram_tensor("ogb_spill", [32, P, D], BF16).ap()
    silu_spill = nc.dram_tensor("silu_spill", [ET, P, L], BF16).ap()
    xn_spill = nc.dram_tensor("xn_spill", [TTH, P, D], F32).ap()
    cc_t_in = nc.dram_tensor("cc_t_in", [160, L], F32).ap()
    cc_t_out = nc.dram_tensor("cc_t_out", [160, L], F32).ap()
    yg_in = [nc.dram_tensor(f"yg_in{c}", [EH, LC], BF16).ap() for c in range(NCH)]
    yg_out = [nc.dram_tensor(f"yg_out{c}", [2 * EH, LC], BF16).ap() for c in range(NCH)]

    taps = {}
    if debug_taps:
        def tap(name, shape, dtype=F32):
            taps[name] = nc.dram_tensor(name, list(shape), dtype, kind="ExternalOutput").ap()
        tap("tap_rnT", (DT, P, L), BF16)
        tap("tap_t", (P, L))
        tap("tap_bc", (32, L))
        tap("tap_x", (ET, P, L))
        tap("tap_delta", (NCH, P, ET * LC))
        tap("tap_yg", (NCH, P, ET * LC), BF16)
        tap("tap_ffn", (TTH, P, D))

    def sbg(name, shape, dtype):
        return nc.alloc_sbuf_tensor(name, list(shape), dtype)

    t_bf = sbg("t_bf", (P, L), BF16)
    wd1T_bf = sbg("wd1T_bf", (P, ET * P), BF16)
    wd2T_bf = sbg("wd2T_bf", (P, ET * P), BF16)
    wbcT_bf = sbg("wbcT_bf", (P, ET * 32), BF16)
    a_neg = sbg("a_neg", (P, ET * NST), F32)
    convw_sb = sbg("convw_sb", (P, ET * 4), F32)
    convb_sb = sbg("convb_sb", (P, ET), F32)
    wd2b_sb = sbg("wd2b_sb", (P, ET), F32)
    wdvec_sb = sbg("wdvec_sb", (P, ET), F32)
    normw_sb = sbg("normw_sb", (P, DT), F32)
    msel_sb = sbg("msel_sb", (P, 2), F32)
    carries = sbg("carries", (P, ET * NST), F32)
    t_acc = sbg("t_acc", (P, L), F32)
    bc_acc = sbg("bc_acc", (32, L), F32)
    wstage = [sbg(f"wstage{i}", (P, D), F32) for i in range(2)]
    wbf = [sbg(f"wbf{i}", (P, D), BF16) for i in range(4)]
    id_f32 = sbg("id_f32", (P, P), F32)
    id_bf = sbg("id_bf", (P, P), BF16)
    cb = [sbg(f"cb{i}", (P, 1), F32) for i in range(8)]

    ARENA = (nc.sbuf_bytes_remaining - 64) & ~31
    arena = nc.alloc_sbuf_tensor("arena", [P, ARENA], mybir.dt.uint8)
    arena_base = nc.lookup_mloc(arena).addr

    def place(name, shape, dtype, offset):
        per_part = int(np.prod(shape[1:])) * mybir.dt.size(dtype)
        assert offset % 32 == 0 and offset + per_part <= ARENA, (name, offset, per_part)
        return nc.alloc_sbuf_tensor_at(name, list(shape), dtype,
                                       offset=arena_base + offset)

    class Bump:
        def __init__(self, start=0): self.o = start
        def take(self, nbytes):
            r = self.o
            self.o = (self.o + nbytes + 31) & ~31
            return r

    b1 = Bump()
    rnT = place("rnT", (P, DT, L), BF16, b1.take(DT * L * 2))
    rtoff = [b1.take(D * 4) for _ in range(2)]
    resid_t = [place(f"resid_t{i}", (P, D), F32, rtoff[i]) for i in range(2)]
    xn_f32 = place("xn_f32", (P, D), F32, b1.take(D * 4))
    rnpre = place("rnpre", (P, D), BF16, b1.take(D * 2))
    xin_st = place("xin_st", (P, D + 8), F32, b1.take((D + 8) * 4))
    convacc = place("convacc", (P, D), F32, b1.take(D * 4))
    x_f32 = place("x_f32", (P, D), F32, b1.take(D * 4))
    xb_bf = place("xb_bf", (P, D), BF16, b1.take(D * 2))
    silu_st = place("silu_st", (P, D), BF16, b1.take(D * 2))
    junk = place("junk", (P, D), F32, b1.take(D * 4))
    assert b1.o <= ARENA
    castb = [place(f"castb{i}", (P, D), BF16, roff) for i, roff in enumerate(rtoff)]

    b2 = Bump()
    delta3 = place("delta3", (P, ET, LC), F32, b2.take(ET * LC * 4))
    dx3 = place("dx3", (P, ET, LC), F32, b2.take(ET * LC * 4))
    x3 = place("x3", (P, ET, LC), F32, b2.take(ET * LC * 4))
    yacc3 = place("yacc3", (P, ET, LC), F32, b2.take(ET * LC * 4))
    a_big = [place(f"a_big{i}", (P, ET * SEG), F32, b2.take(ET * SEG * 4)) for i in range(2)]
    u_big = [place(f"u_big{i}", (P, ET * SEG), F32, b2.take(ET * SEG * 4)) for i in range(2)]
    bcn = [place(f"bcn{i}", (P, 2, LC), F32, b2.take(2 * LC * 4)) for i in range(2)]
    silu_ch = place("silu_ch", (P, ET, LC), BF16, b2.take(ET * LC * 2))
    sp1_off = b2.take(LC * 4)
    sp3_off = b2.take(LC * 4)
    sp3 = [place("sp3a", (P, LC), F32, sp1_off),
           place("sp3b", (P, LC), F32, sp3_off)]
    ygf = place("ygf", (P, LC), F32, sp3_off)
    yg1 = [place(f"yg1{i}", (P, LC), BF16, sp1_off + i * (LC * 2)) for i in range(2)]
    assert b2.o <= ARENA

    b3 = Bump()
    yffn3 = place("yffn3", (P, TTH, D), F32, b3.take(TTH * D * 4))
    h2T_off = b3.take(16 * TOKH * 2)
    h2T = place("h2T", (P, 16, TOKH), BF16, h2T_off)
    xnT3 = place("xnT3", (P, DT, TOKH), BF16, b3.take(DT * TOKH * 2))
    relu_st = place("relu_st", (P, 512), F32, b3.take(512 * 4))
    moeT_st = place("moeT_st", (P, TOKH), F32, b3.take(TOKH * 4))
    logits = place("logits", (P, TTH * 8), F32, b3.take(TTH * 8 * 4))
    pair28 = place("pair28", (P, 28), F32, b3.take(28 * 4))
    scale_sb = place("scale_sb", (P, TTH), F32, b3.take(TTH * 4))
    gate_f = place("gate_f", (P, DT * 8), F32, b3.take(DT * 8 * 4))
    gate_b = place("gate_b", (P, DT * 8), BF16, b3.take(DT * 8 * 2))
    assert b3.o <= ARENA
    bh = Bump(h2T_off)
    p3res = place("p3res", (P, D), F32, bh.take(D * 4))
    p3xn = place("p3xn", (P, D), F32, bh.take(D * 4))
    p3xnb = place("p3xnb", (P, D), BF16, bh.take(D * 2))
    junk3 = place("junk3", (P, D), F32, bh.take(D * 4))
    assert bh.o <= h2T_off + 16 * TOKH * 2

    b4 = Bump(TTH * D * 4)
    yfullA = place("yfullA", (P, 32, LC), BF16, b4.take(32 * LC * 2))
    yfullB = place("yfullB", (P, 32, LC), BF16, b4.take(32 * LC * 2))
    ysel = [place(f"ysel{i}", (P, 32, LC), BF16, b4.take(32 * LC * 2)) for i in range(2)]
    attnT = place("attnT", (P, 2, LC), F32, b4.take(2 * LC * 4))
    assert b4.o <= ARENA

    pb = [nc.alloc_psum_tensor(f"pb{i}", [P, 512], F32) for i in range(4)]
    pD = nc.alloc_psum_tensor("pD", [P, 2048], F32)
    pbv = [x.ap() for x in pb]
    pbbf = [x.ap().bitcast(BF16) for x in pb]
    pDv = pD.ap()

    B = Builder()
    SY = B.chan("SY"); GPD = B.chan("GPD"); SCD = B.chan("SCD")
    PE = B.chan("PE"); SC = B.chan("SC"); VE = B.chan("VE"); GP = B.chan("GP")
    CCT = B.chan("CCT")
    CCY = [B.chan(f"CCY{c}") for c in range(NCH)]

    def dma(out_ap, in_ap, waits=(), eng="sync"):
        ch = {"gpsimd": GPD, "scalar": SCD}.get(eng, SY)
        return B.emit(eng, lambda h, o=out_ap, i=in_ap: h.dma_start(out=o, in_=i),
                      waits=waits, inc=(ch, 16))

    def pe(fn, waits=(), inc=True):
        return B.emit("tensor", fn, waits=waits, inc=(PE, 1) if inc else None)

    def sc(fn, waits=()):
        return B.emit("scalar", fn, waits=waits, inc=(SC, 1))

    def ve(fn, waits=()):
        return B.emit("vector", fn, waits=waits, inc=(VE, 1))

    def gp(fn, waits=()):
        return B.emit("gpsimd", fn, waits=waits, inc=(GP, 1))

    def now(ch):
        return (ch, ch.n)

    def snapshot():
        return [now(PE), now(SC), now(VE), now(GP), now(SY), now(GPD)]

    dma(convw_sb.ap(), convw)
    dma(convb_sb.ap(), convb)
    dma(wd2b_sb.ap(), wd2b)
    dma(wdvec_sb.ap(), wdvec)
    dma(normw_sb.ap(), normw)
    dma(msel_sb.ap(), hsel)
    dma(a_neg.ap(), alog)
    dma(wstage[0].ap()[:, 0:ET * P].rearrange("p (e r) -> p e r", e=ET),
        wd1T.rearrange("(e p) r -> p e r", p=P))
    dma(wstage[1].ap()[:, 0:ET * P], wd2T)
    t_small = now(SY)

    t_mz1 = gp(lambda h: h.memset(id_f32.ap(), 0.0))
    t_af1 = gp(lambda h: h.affine_select(out=id_f32.ap(), in_=id_f32.ap(),
                                         compare_op=ALU.not_equal, fill=1.0, base=0,
                                         pattern=[[-1, P]], channel_multiplier=1),
               waits=[t_mz1])
    t_mz2 = gp(lambda h: h.memset(id_bf.ap(), 0.0), waits=[t_af1])
    tok_id = gp(lambda h: h.affine_select(out=id_bf.ap(), in_=id_bf.ap(),
                                          compare_op=ALU.not_equal, fill=1.0, base=0,
                                          pattern=[[-1, P]], channel_multiplier=1),
                waits=[t_mz2])
    tok_wd1c = sc(lambda h: h.activation(out=wd1T_bf.ap(), in_=wstage[0].ap()[:, 0:ET * P],
                                         func=AF.Copy), waits=[t_small])
    tok_wd2c = sc(lambda h: h.activation(out=wd2T_bf.ap(), in_=wstage[1].ap()[:, 0:ET * P],
                                         func=AF.Copy), waits=[t_small])
    t_lbc = dma(wstage[0].ap()[:, 0:ET * 32].rearrange("p (e m) -> p e m", e=ET),
                wbcT.rearrange("(e p) m -> p e m", p=P),
                waits=[tok_wd1c])
    tok_wbcc = sc(lambda h: h.activation(out=wbcT_bf.ap(), in_=wstage[0].ap()[:, 0:ET * 32],
                                         func=AF.Copy), waits=[t_lbc])
    tok_aexp = sc(lambda h: h.activation(out=a_neg.ap(), in_=a_neg.ap(), func=AF.Exp),
                  waits=[t_small])
    ve(lambda h: h.tensor_single_scalar(a_neg.ap(), a_neg.ap(), -1.0, ALU.mult),
       waits=[tok_aexp])
    tok_cz = ve(lambda h: h.memset(carries.ap(), 0.0))
    tok_tz = ve(lambda h: h.memset(t_acc.ap(), 0.0))
    tok_bz = ve(lambda h: h.memset(bc_acc.ap(), 0.0))

    inv_d = 1.0 / D
    ev_bank = {}
    t_rn_prev = None
    t_tr_prev = None
    for tt in range(NTT):
        par = tt % 2
        w = [now(VE), now(SC)] if tt >= 2 else []
        t_ld = dma(resid_t[par].ap(), resid[tt * P:(tt + 1) * P, :], waits=w)
        t_sum = ve(lambda h, par=par: h.reduce_sum(cb[0].ap(), resid_t[par].ap(), axis=AX.X),
                   waits=[t_ld])
        t_nm = ve(lambda h: h.tensor_single_scalar(cb[1].ap(), cb[0].ap(), -inv_d, ALU.mult),
                  waits=[t_sum])
        t_var = sc(lambda h, par=par: h.activation(out=junk.ap(), in_=resid_t[par].ap(),
                                                   func=AF.Square, bias=cb[1].ap(),
                                                   accum_out=cb[2].ap()),
                   waits=[t_nm])
        t_v = ve(lambda h: h.tensor_scalar(cb[3].ap(), cb[2].ap(), inv_d, EPS, ALU.mult, ALU.add),
                 waits=[t_var])
        t_l1 = sc(lambda h: h.activation(out=cb[4].ap(), in_=cb[3].ap(), func=AF.Ln), waits=[t_v])
        t_rstd = sc(lambda h: h.activation(out=cb[4].ap(), in_=cb[4].ap(), func=AF.Exp, scale=-0.5),
                    waits=[t_l1])
        t_rc = ve(lambda h: h.reciprocal(cb[5].ap(), cb[3].ap()), waits=[t_l1])
        t_vv = ve(lambda h: h.tensor_single_scalar(cb[6].ap(), cb[2].ap(), inv_d, ALU.mult),
                  waits=[t_rc])
        t_msq = ve(lambda h: h.tensor_tensor(cb[6].ap(), cb[6].ap(), cb[5].ap(), ALU.mult),
                   waits=[t_vv])
        t_t2 = ve(lambda h: h.tensor_single_scalar(cb[6].ap(), cb[6].ap(), EPS, ALU.add),
                  waits=[t_msq])
        t_l2 = sc(lambda h: h.activation(out=cb[7].ap(), in_=cb[6].ap(), func=AF.Ln), waits=[t_t2])
        t_s = sc(lambda h: h.activation(out=cb[7].ap(), in_=cb[7].ap(), func=AF.Exp, scale=-0.5),
                 waits=[t_l2])
        t_xn = ve(lambda h, par=par: h.tensor_scalar(xn_f32.ap(), resid_t[par].ap(),
                                                     cb[1].ap(), cb[4].ap(), ALU.add, ALU.mult),
                  waits=[t_rstd, t_rn_prev])
        t_rn = ve(lambda h: h.tensor_single_scalar(rnpre.ap(), xn_f32.ap(), cb[7].ap(), ALU.mult),
                  waits=[t_xn, t_s, t_tr_prev])
        t_rn_prev = t_rn
        for dt in range(DT):
            bank = dt % 2
            t_tr = pe(lambda h, dt=dt, bank=bank: h.transpose(
                          pbbf[bank][:, 0:P], rnpre.ap()[:, dt * P:(dt + 1) * P], id_bf.ap()),
                      waits=[t_rn, tok_id, ev_bank.get(bank)])
            ev_bank[bank] = sc(lambda h, dt=dt, tt=tt, bank=bank: h.activation(
                                   out=rnT.ap()[:, dt, tt * P:(tt + 1) * P],
                                   in_=pbbf[bank][:, 0:P],
                                   func=AF.Copy, scale=normw_sb.ap()[:, dt:dt + 1]),
                               waits=[t_tr])
            if dt == DT - 1:
                t_tr_prev = t_tr
    tok_p1a = now(SC)
    if debug_taps:
        for dt in range(DT):
            dma(taps["tap_rnT"][dt], rnT.ap()[:, dt, :], waits=[tok_p1a])

    castf = [junk, xn_f32]
    ps_state = {"cst": [None, None], "st": [None, None], "j": 0}

    def emit_prespill(njobs):
        for _ in range(njobs):
            j = ps_state["j"]
            if j >= 32:
                return
            ps_state["j"] = j + 1
            pj = j % 2
            wlf = [ps_state["cst"][pj]] if ps_state["cst"][pj] else [now(SC), now(VE), now(PE)]
            t_lf = dma(castf[pj].ap(), out_wT[j * P:(j + 1) * P, :], waits=wlf, eng="gpsimd")
            t_cst = ve(lambda h, fv=castf[pj].ap(), bv=castb[pj].ap(): h.tensor_copy(bv, fv),
                       waits=[t_lf, ps_state["st"][pj]])
            ps_state["cst"][pj] = t_cst
            ps_state["st"][pj] = dma(ogb_spill[j], castb[pj].ap(), waits=[t_cst], eng="gpsimd")

    t_csk = None
    t_cin = None
    g1last = [None, None]
    g2last = [None, None]
    prev_mm5 = None
    t_e3 = None
    t_e5 = None
    t_spill_prev = None
    t_xsp_prev = None
    t_c = None
    t_zp = None
    for e in range(ET):
        par = e % 2
        t_lsk = dma(wstage[0].ap().rearrange("p (a b) -> p a b", a=DT),
                    skip_wT.rearrange("(dt p) e -> p dt e", p=P)[:, :, e * P:(e + 1) * P],
                    waits=[t_csk if e > 0 else tok_wbcc])
        t_csk = sc(lambda h, par=par: h.activation(out=wbf[par].ap(), in_=wstage[0].ap(),
                                                   func=AF.Copy),
                   waits=[t_lsk, g1last[par]])
        t_lin = dma(wstage[1].ap().rearrange("p (a b) -> p a b", a=DT),
                    in_wT.rearrange("(dt p) e -> p dt e", p=P)[:, :, e * P:(e + 1) * P],
                    waits=[t_cin if e > 0 else tok_wd2c])
        t_cin = sc(lambda h, par=par: h.activation(out=wbf[2 + par].ap(), in_=wstage[1].ap(),
                                                   func=AF.Copy),
                   waits=[t_lin, g2last[par]])

        for tc in range(4):
            bank = tc % 2
            wmm = [t_csk, tok_p1a, ev_bank.get(bank)]
            for dt in range(DT):
                t_mm = pe(lambda h, dt=dt, bank=bank, tc=tc, par=par: h.matmul(
                              pbv[bank], wbf[par].ap()[:, dt * P:(dt + 1) * P],
                              rnT.ap()[:, dt, tc * 512:(tc + 1) * 512],
                              start=(dt == 0), stop=(dt == DT - 1)),
                          waits=wmm if dt == 0 else (), inc=(dt == DT - 1))
            ev_bank[bank] = sc(lambda h, tc=tc, bank=bank: h.activation(
                                   out=silu_st.ap()[:, tc * 512:(tc + 1) * 512],
                                   in_=pbv[bank], func=AF.Silu),
                               waits=[t_mm, t_spill_prev if tc <= 1 else None])
        g1last[par] = t_mm
        t_spill_prev = dma(silu_spill[e], silu_st.ap(), waits=[ev_bank[0], ev_bank[1]],
                           eng="scalar")

        for tc in range(4):
            bank = 2 + tc % 2
            wmm = [t_cin, ev_bank.get(bank)]
            for dt in range(DT):
                t_mm = pe(lambda h, dt=dt, bank=bank, tc=tc, par=par: h.matmul(
                              pbv[bank], wbf[2 + par].ap()[:, dt * P:(dt + 1) * P],
                              rnT.ap()[:, dt, tc * 512:(tc + 1) * 512],
                              start=(dt == 0), stop=(dt == DT - 1)),
                          waits=wmm if dt == 0 else (), inc=(dt == DT - 1))
            ev_bank[bank] = sc(lambda h, tc=tc, bank=bank: h.activation(
                                   out=xin_st.ap()[:, 8 + tc * 512:8 + (tc + 1) * 512],
                                   in_=pbv[bank], func=AF.Copy),
                               waits=[t_mm, t_c if tc <= 1 else None])
        g2last[par] = t_mm
        if e == 0:
            t_zp = ve(lambda h: h.memset(xin_st.ap()[:, 0:8], 0.0))
        t_c = ve(lambda h, e=e: h.tensor_scalar(convacc.ap(), xin_st.ap()[:, 5:5 + D],
                                                convw_sb.ap()[:, 4 * e:4 * e + 1],
                                                convb_sb.ap()[:, e:e + 1], ALU.mult, ALU.add),
                 waits=[ev_bank[2], ev_bank[3], t_zp, t_xsp_prev])
        for k in range(1, 4):
            t_c = ve(lambda h, e=e, k=k: h.scalar_tensor_tensor(
                         convacc.ap(), xin_st.ap()[:, 5 + k:5 + k + D],
                         convw_sb.ap()[:, 4 * e + k:4 * e + k + 1], convacc.ap(),
                         ALU.mult, ALU.add),
                     waits=[t_c])
        t_x = sc(lambda h: h.activation(out=x_f32.ap(), in_=convacc.ap(), func=AF.Silu),
                 waits=[t_c, t_xsp_prev])
        t_xsp = dma(x_spill[e], x_f32.ap(), waits=[t_x], eng="scalar")
        t_xb = sc(lambda h: h.activation(out=xb_bf.ap(), in_=x_f32.ap(), func=AF.Copy),
                  waits=[t_x, prev_mm5])
        t_xsp_prev = t_xsp

        wg3 = [t_xb, tok_wd1c, t_e5 or tok_tz]
        t_mm3 = None
        for tc in range(4):
            t_mm3 = pe(lambda h, e=e, tc=tc: h.matmul(
                           pDv[:, tc * 512:(tc + 1) * 512],
                           wd1T_bf.ap()[:, e * P:(e + 1) * P],
                           xb_bf.ap()[:, tc * 512:(tc + 1) * 512],
                           start=True, stop=True),
                       waits=wg3 if tc == 0 else (), inc=(tc == 3))
        t_e3 = ve(lambda h: h.tensor_tensor(t_acc.ap(), pDv, t_acc.ap(), ALU.add),
                  waits=[t_mm3, tok_tz])
        t_mm5 = None
        for tc in range(4):
            t_mm5 = pe(lambda h, e=e, tc=tc: h.matmul(
                           pDv[0:32, tc * 512:(tc + 1) * 512],
                           wbcT_bf.ap()[:, e * 32:(e + 1) * 32],
                           xb_bf.ap()[:, tc * 512:(tc + 1) * 512],
                           start=True, stop=True),
                       waits=[t_e3, tok_wbcc] if tc == 0 else (), inc=(tc == 3))
        t_e5 = ve(lambda h: h.tensor_tensor(bc_acc.ap(), pDv[0:32, :], bc_acc.ap(), ALU.add),
                  waits=[t_mm5, tok_bz])
        prev_mm5 = t_mm5
        emit_prespill(2)

    if debug_taps:
        dma(taps["tap_x"], x_spill, waits=[t_xsp_prev])

    emit_prespill(64)

    t_d1 = dma(cc_t_in[0:128, :], t_acc.ap(), waits=[t_e3, t_e5])
    t_d2 = dma(cc_t_in[128:160, :], bc_acc.ap())
    tok_cct = B.emit("gpsimd",
                     lambda h: h.collective_compute("AllReduce", ALU.add, replica_groups=PAIRS,
                                                    ins=[cc_t_in.opt()], outs=[cc_t_out.opt()]),
                     waits=[t_d1, t_d2], inc=(CCT, 1))
    t_tb = dma(wstage[0].ap(), cc_t_out[0:128, :], waits=[tok_cct, t_csk])
    tok_tcast = sc(lambda h: h.activation(out=t_bf.ap(), in_=wstage[0].ap(), func=AF.Copy),
                   waits=[t_tb])
    if debug_taps:
        dma(taps["tap_t"], cc_t_out[0:128, :], waits=[tok_cct])
        dma(taps["tap_bc"], cc_t_out[128:160, :], waits=[tok_cct])

    if stop_after == "p1":
        return _finalize(nc, B)

    B.barrier_into(snapshot())
    tok_az = ve(lambda h: (h.memset(a_big[0].ap(), 0.0), h.memset(a_big[1].ap(), 0.0))[1])
    tok_cz2 = ve(lambda h: h.memset(carries.ap(), 0.0))

    def bc_src(n, c0, c1):
        two = cc_t_out[128 + n:145 + n:16, c0:c1]
        return two.unsqueeze(0).broadcast_to([P, 2, c1 - c0])

    u3 = [u.ap().rearrange("p (e s) -> p e s", s=SEG) for u in u_big]
    a3 = [a.ap().rearrange("p (e s) -> p e s", s=SEG) for a in a_big]
    car3 = carries.ap().rearrange("p (e m) -> p e m", m=NST)
    t_gat = None
    t_dx = None
    hist_u = [None, None]
    hist_p = [None, None]
    hist_yadd = [None, None]
    hist_exp = [None, None]
    hist_tts = [None, None]
    t_yadd = None
    t_u = None
    t_p = None
    prev_ygd0 = prev_ygd1 = None
    for c in range(NCH):
        t_delp = [None, None]
        c0, c1 = c * LC, (c + 1) * LC
        wfree = [t_gat] if t_gat else []
        t_lx = dma(x3.ap(), x_spill.rearrange("e p l -> p e l")[:, :, c0:c1], waits=wfree)
        t_ls = dma(silu_ch.ap(), silu_spill.rearrange("e p l -> p e l")[:, :, c0:c1], waits=wfree)
        t_del = None
        for e in range(ET):
            bank = e % 2
            wg4 = [tok_tcast, tok_wd2c, ev_bank.get(bank)]
            if e < 2 and t_dx:
                wg4.append(t_dx)
            t_g4 = pe(lambda h, e=e, bank=bank, c0=c0, c1=c1: h.matmul(
                          pbv[bank][:, 0:LC], wd2T_bf.ap()[:, e * P:(e + 1) * P],
                          t_bf.ap()[:, c0:c1], start=True, stop=True),
                      waits=wg4, inc=True)
            sb3 = sp3[e % 2]
            wdel_guard = ([t_delp[e % 2]] if t_delp[e % 2]
                          else [t_gat, prev_ygd0, prev_ygd1])
            t_s1 = sc(lambda h, e=e, bank=bank: h.activation(
                          out=delta3.ap()[:, e, :], in_=pbv[bank][:, 0:LC], func=AF.Relu,
                          bias=wd2b_sb.ap()[:, e:e + 1]),
                      waits=[t_g4] + ([t_gat, prev_ygd0, prev_ygd1, t_u, now(SC)]
                                      if e == 0 else []))
            t_s2 = sc(lambda h, e=e, bank=bank, sb3=sb3: h.activation(
                          out=sb3.ap(), in_=pbv[bank][:, 0:LC], func=AF.Abs,
                          bias=wd2b_sb.ap()[:, e:e + 1]),
                      waits=[t_s1] + wdel_guard)
            ev_bank[bank] = t_s2
            t_s3 = sc(lambda h, sb3=sb3: h.activation(out=sb3.ap(), in_=sb3.ap(), func=AF.Exp,
                                                      scale=-1.0),
                      waits=[t_s2])
            t_s4 = sc(lambda h, sb3=sb3: h.activation(out=sb3.ap(), in_=sb3.ap(), func=AF.Ln,
                                                      bias=1.0),
                      waits=[t_s3])
            t_del = ve(lambda h, e=e, sb3=sb3: h.tensor_tensor(delta3.ap()[:, e, :],
                                                               delta3.ap()[:, e, :], sb3.ap(),
                                                               ALU.add),
                       waits=[t_s4])
            t_delp[e % 2] = t_del
        t_dly = t_del
        t_dx = ve(lambda h: h.tensor_tensor(
                      dx3.ap().rearrange("p e l -> p (e l)"),
                      delta3.ap().rearrange("p e l -> p (e l)"),
                      x3.ap().rearrange("p e l -> p (e l)"), ALU.mult),
                  waits=[t_dly, t_lx, t_u])
        t_yz = ve(lambda h: h.memset(yacc3.ap(), 0.0), waits=[t_gat])
        bc_tok = {0: dma(bcn[0].ap(), bc_src(0, c0, c1),
                         waits=[tok_cct, hist_u[0], hist_p[0]])}
        for n in range(NST):
            pn = n % 2
            if n + 1 < NST:
                bc_tok[n + 1] = dma(bcn[(n + 1) % 2].ap(), bc_src(n + 1, c0, c1),
                                    waits=[hist_u[(n + 1) % 2], hist_p[(n + 1) % 2]])
            t_a = None
            for e in range(ET):
                wexp = [t_dly]
                if e == 0:
                    wexp.append(hist_p[pn] or tok_az)
                t_a = sc(lambda h, e=e, n=n, pn=pn: h.activation(
                             out=a_big[pn].ap()[:, e * SEG + 1:e * SEG + 1 + LC],
                             in_=delta3.ap()[:, e, :], func=AF.Exp,
                             scale=a_neg.ap()[:, e * NST + n:e * NST + n + 1]),
                         waits=wexp)
            hist_exp[pn] = t_a
            t_u = ve(lambda h, pn=pn: h.tensor_tensor(
                         u3[pn][:, :, 1:SEG], dx3.ap(),
                         bcn[pn].ap()[:, 0:1, :].broadcast_to([P, ET, LC]), ALU.mult),
                     waits=[t_dx, bc_tok[n], hist_yadd[pn]])
            hist_u[pn] = t_u
            t_cc = ve(lambda h, n=n, pn=pn: h.tensor_copy(u3[pn][:, :, 0:1], car3[:, :, n:n + 1]),
                      waits=[t_u, tok_cz2])
            t_h = ve(lambda h, pn=pn: h.tensor_tensor_scan(a_big[pn].ap(), a_big[pn].ap(),
                                                           u_big[pn].ap(), 0.0,
                                                           ALU.mult, ALU.add),
                     waits=[t_cc, t_a, tok_az])
            hist_tts[pn] = t_h
            t_nc = ve(lambda h, n=n, pn=pn: h.tensor_copy(car3[:, :, n:n + 1],
                                                          a3[pn][:, :, SEG - 1:SEG]),
                      waits=[t_h])
            t_p = ve(lambda h, pn=pn: h.tensor_tensor(
                         u3[pn][:, :, 1:SEG], a3[pn][:, :, 1:SEG],
                         bcn[pn].ap()[:, 1:2, :].broadcast_to([P, ET, LC]), ALU.mult),
                     waits=[t_nc])
            hist_p[pn] = t_p
            t_yadd = ve(lambda h, pn=pn: h.tensor_tensor(yacc3.ap(), yacc3.ap(),
                                                         u3[pn][:, :, 1:SEG], ALU.add),
                        waits=[t_p, t_yz])
            hist_yadd[pn] = t_yadd
        t_g2 = [None, None]
        t_ygd = [None, None]
        for e in range(ET):
            pg = e % 2
            t_g1 = ve(lambda h, e=e: h.scalar_tensor_tensor(
                          ygf.ap(), x3.ap()[:, e, :], wdvec_sb.ap()[:, e:e + 1],
                          yacc3.ap()[:, e, :], ALU.mult, ALU.add),
                      waits=([t_ls, t_yadd] if e == 0 else []) + [t_ygd[pg]])
            t_g2[pg] = ve(lambda h, e=e, pg=pg: h.tensor_tensor(yg1[pg].ap(), ygf.ap(),
                                                                silu_ch.ap()[:, e, :], ALU.mult),
                          waits=[t_g1])
            t_ygd[pg] = dma(yg_in[c][e * P:(e + 1) * P, :], yg1[pg].ap(), waits=[t_g2[pg]])
        t_gat = t_g2[1]
        prev_ygd0, prev_ygd1 = t_ygd[0], t_ygd[1]
        B.emit("gpsimd",
               lambda h, c=c: h.collective_compute("AllGather", ALU.bypass, replica_groups=PAIRS,
                                                   ins=[yg_in[c].opt()], outs=[yg_out[c].opt()]),
               waits=[t_ygd[0], t_ygd[1]], inc=(CCY[c], 1))
        if debug_taps:
            dma(taps["tap_delta"][c], delta3.ap().rearrange("p e l -> p (e l)"), waits=[t_dx])

    if stop_after == "p2":
        return _finalize(nc, B)

    B.barrier_into(snapshot())
    ev_bank = {}
    ev2 = {}
    t_prev = None
    t_xb = None
    t_spx = None
    for tt in range(TTH):
        t_ld = dma(p3res.ap(), resid_ffn[tt * P:(tt + 1) * P, :],
                   waits=[t_prev, now(SC), now(VE)] if t_prev else [])
        t_sum = ve(lambda h: h.reduce_sum(cb[0].ap(), p3res.ap(), axis=AX.X), waits=[t_ld])
        t_nm = ve(lambda h: h.tensor_single_scalar(cb[1].ap(), cb[0].ap(), -inv_d, ALU.mult),
                  waits=[t_sum])
        t_var = sc(lambda h: h.activation(out=junk3.ap(), in_=p3res.ap(), func=AF.Square,
                                          bias=cb[1].ap(), accum_out=cb[2].ap()),
                   waits=[t_nm])
        t_v = ve(lambda h: h.tensor_scalar(cb[3].ap(), cb[2].ap(), inv_d, EPS, ALU.mult, ALU.add),
                 waits=[t_var])
        t_l1 = sc(lambda h: h.activation(out=cb[4].ap(), in_=cb[3].ap(), func=AF.Ln), waits=[t_v])
        t_rstd = sc(lambda h: h.activation(out=cb[4].ap(), in_=cb[4].ap(), func=AF.Exp, scale=-0.5),
                    waits=[t_l1])
        t_xn = ve(lambda h: h.tensor_scalar(p3xn.ap(), p3res.ap(), cb[1].ap(), cb[4].ap(),
                                            ALU.add, ALU.mult),
                  waits=[t_rstd, t_xb, t_spx])
        t_spx = dma(xn_spill[tt], p3xn.ap(), waits=[t_xn])
        t_xb = sc(lambda h: h.activation(out=p3xnb.ap(), in_=p3xn.ap(), func=AF.Copy),
                  waits=[t_xn])
        for dt in range(DT):
            bank = dt % 2
            t_tr = pe(lambda h, dt=dt, bank=bank: h.transpose(
                          pbbf[bank][:, 0:P], p3xnb.ap()[:, dt * P:(dt + 1) * P], id_bf.ap()),
                      waits=[t_xb, tok_id, ev_bank.get(bank)])
            ev_bank[bank] = sc(lambda h, dt=dt, tt=tt, bank=bank: h.activation(
                                   out=xnT3.ap()[:, dt, tt * P:(tt + 1) * P],
                                   in_=pbbf[bank][:, 0:P], func=AF.Copy),
                               waits=[t_tr])
        t_prev = ev_bank[1]
    tok_xnT = now(SC)

    t_lg = dma(gate_f.ap().rearrange("p (a b) -> p a b", a=DT),
               gate_w.rearrange("(dt p) m -> p dt m", p=P))
    tok_gc = sc(lambda h: h.activation(out=gate_b.ap(), in_=gate_f.ap(), func=AF.Copy),
                waits=[t_lg])
    ev0 = None
    t_sc = None
    for tt in range(TTH):
        t_mmg = None
        for dt in range(DT):
            t_mmg = pe(lambda h, dt=dt, tt=tt: h.matmul(
                           pbv[0][:, 0:8], xnT3.ap()[:, dt, tt * P:(tt + 1) * P],
                           gate_b.ap()[:, dt * 8:(dt + 1) * 8],
                           start=(dt == 0), stop=(dt == DT - 1)),
                       waits=[tok_gc, tok_xnT, ev0, ev_bank.get(0)] if dt == 0 else (),
                       inc=(dt == DT - 1))
        ev0 = ve(lambda h, tt=tt: h.tensor_copy(logits.ap()[:, tt * 8:(tt + 1) * 8],
                                                pbv[0][:, 0:8]),
                 waits=[t_mmg])
        tpair = ev0
        off = 0
        for k in range(1, 8):
            tpair = ve(lambda h, tt=tt, k=k, off=off: h.tensor_tensor(
                           pair28.ap()[:, off:off + 8 - k],
                           logits.ap()[:, tt * 8:tt * 8 + 8 - k],
                           logits.ap()[:, tt * 8 + k:tt * 8 + 8], ALU.add),
                       waits=[tpair])
            off += 8 - k
        t_sc = ve(lambda h, tt=tt: h.reduce_max(scale_sb.ap()[:, tt:tt + 1], pair28.ap(),
                                                axis=AX.X),
                  waits=[tpair])
    tok_scale = t_sc

    wcast = [None, None]
    ev_pD = None
    exmm = [None, None]
    tr_last = None
    for ot in range(DT):
        par = ot % 2
        t_lw = dma(wstage[par].ap().rearrange("p (a b) -> p a b", a=DT),
                   exp_w.rearrange("(dt p) o -> p dt o", p=P)[:, :, ot * P:(ot + 1) * P],
                   waits=[wcast[par]])
        t_wc = sc(lambda h, par=par: h.activation(out=wbf[par].ap(), in_=wstage[par].ap(),
                                                  func=AF.Copy),
                  waits=[t_lw, exmm[par]])
        wcast[par] = t_wc
        t_mme = None
        for tc in range(2):
            for dt in range(DT):
                t_mme = pe(lambda h, dt=dt, tc=tc, par=par: h.matmul(
                               pDv[:, tc * 512:(tc + 1) * 512],
                               wbf[par].ap()[:, dt * P:(dt + 1) * P],
                               xnT3.ap()[:, dt, tc * 512:(tc + 1) * 512],
                               start=(dt == 0), stop=(dt == DT - 1)),
                           waits=[t_wc, ev_pD] if (dt == 0 and tc == 0) else (),
                           inc=(dt == DT - 1))
        exmm[par] = t_mme
        ev_pD = sc(lambda h: h.activation(out=moeT_st.ap(), in_=pDv[:, 0:1024], func=AF.Copy),
                   waits=[t_mme, tr_last])
        for tt in range(TTH):
            bank = 2 + tt % 2
            t_tr = pe(lambda h, tt=tt, bank=bank: h.transpose(
                          pbv[bank][:, 0:P], moeT_st.ap()[:, tt * P:(tt + 1) * P], id_f32.ap()),
                      waits=[ev_pD, tok_id, ev2.get(bank)])
            ev2[bank] = sc(lambda h, tt=tt, ot=ot, bank=bank: h.activation(
                               out=yffn3.ap()[:, tt, ot * P:(ot + 1) * P],
                               in_=pbv[bank][:, 0:P], func=AF.Copy,
                               scale=scale_sb.ap()[:, tt:tt + 1]),
                           waits=[t_tr, tok_scale])
            tr_last = t_tr
    tok_moe = [now(SC), now(VE)]

    ev_dn = None
    t_h2 = None
    upmm = [None, None]
    dnmm = [None, None]
    for q in range(4):
        for icq in range(16):
            ic = q * 16 + icq
            par = ic % 2
            t_lu = dma(wstage[par].ap().rearrange("p (a b) -> p a b", a=DT),
                       up_w.rearrange("(dt p) i -> p dt i", p=P)[:, :, ic * P:(ic + 1) * P],
                       waits=[wcast[par]])
            t_wc = sc(lambda h, par=par: h.activation(out=wbf[par].ap(), in_=wstage[par].ap(),
                                                      func=AF.Copy),
                      waits=[t_lu, upmm[par] if upmm[par] else exmm[par]])
            wcast[par] = t_wc
            for tc in range(2):
                bank = tc
                wmm = [t_wc, ev_bank.get(bank)]
                if q == 0 and icq == 0 and tc == 0:
                    wmm += tok_moe
                t_mmu = None
                for dt in range(DT):
                    t_mmu = pe(lambda h, dt=dt, tc=tc, par=par, bank=bank: h.matmul(
                                   pbv[bank], wbf[par].ap()[:, dt * P:(dt + 1) * P],
                                   xnT3.ap()[:, dt, tc * 512:(tc + 1) * 512],
                                   start=(dt == 0), stop=(dt == DT - 1)),
                               waits=wmm if dt == 0 else (), inc=(dt == DT - 1))
                upmm[par] = t_mmu
                t_r = sc(lambda h, bank=bank: h.activation(out=relu_st.ap(), in_=pbv[bank],
                                                           func=AF.Relu),
                         waits=[t_mmu, t_h2])
                ev_bank[bank] = t_r
                t_h2 = sc(lambda h, icq=icq, tc=tc: h.activation(
                              out=h2T.ap()[:, icq, tc * 512:(tc + 1) * 512],
                              in_=relu_st.ap(), func=AF.Square),
                          waits=[t_r, ev_dn if (q > 0 and icq == 0 and tc == 0) else None])
        tok_up = t_h2
        for dto in range(DT):
            par = dto % 2
            t_ldw = dma(wstage[par].ap().rearrange("p (a b) -> p a b", a=16),
                        down_w.rearrange("(i p) d -> p i d", p=P)[:, q * 16:(q + 1) * 16,
                                                                 dto * P:(dto + 1) * P],
                        waits=[wcast[par]])
            t_dwc = sc(lambda h, par=par: h.activation(out=wbf[2 + par].ap(), in_=wstage[par].ap(),
                                                       func=AF.Copy),
                       waits=[t_ldw, dnmm[par]])
            wcast[par] = t_dwc
            t_mmd = None
            for icq in range(16):
                for tc in range(2):
                    t_mmd = pe(lambda h, icq=icq, tc=tc, par=par: h.matmul(
                                   pDv[:, tc * 512:(tc + 1) * 512],
                                   wbf[2 + par].ap()[:, icq * P:(icq + 1) * P],
                                   h2T.ap()[:, icq, tc * 512:(tc + 1) * 512],
                                   start=(icq == 0), stop=(icq == 15)),
                               waits=[t_dwc, tok_up, ev_dn] if (icq == 0 and tc == 0) else (),
                               inc=(icq == 15))
            dnmm[par] = t_mmd
            ev_dn = sc(lambda h: h.activation(out=moeT_st.ap(), in_=pDv[:, 0:1024], func=AF.Copy),
                       waits=[t_mmd, tr_last])
            for tt in range(TTH):
                bank = 2 + tt % 2
                t_tr = pe(lambda h, tt=tt, bank=bank: h.transpose(
                              pbv[bank][:, 0:P], moeT_st.ap()[:, tt * P:(tt + 1) * P],
                              id_f32.ap()),
                          waits=[ev_dn, ev2.get(bank)])
                ev2[bank] = ve(lambda h, tt=tt, dto=dto, bank=bank: h.tensor_tensor(
                                   yffn3.ap()[:, tt, dto * P:(dto + 1) * P],
                                   pbv[bank][:, 0:P],
                                   yffn3.ap()[:, tt, dto * P:(dto + 1) * P], ALU.add),
                               waits=[t_tr])
                tr_last = t_tr
    tok_down = [now(VE), now(SC), now(PE)]

    t_lgm = dma(wstage[0].ap(), ln_g.broadcast_to([P, D]), waits=[wcast[0], wcast[1]])
    t_lbt = dma(wstage[1].ap(), ln_b.broadcast_to([P, D]), waits=[wcast[0], wcast[1]])
    t_fin = None
    t_lr = None
    t_lxn = None
    for tt in range(TTH):
        t_sum = ve(lambda h, tt=tt: h.reduce_sum(cb[0].ap(), yffn3.ap()[:, tt, :], axis=AX.X),
                   waits=tok_down if tt == 0 else [])
        t_nm = ve(lambda h: h.tensor_single_scalar(cb[1].ap(), cb[0].ap(), -inv_d, ALU.mult),
                  waits=[t_sum])
        t_var = sc(lambda h, tt=tt: h.activation(out=junk3.ap(), in_=yffn3.ap()[:, tt, :],
                                                 func=AF.Square, bias=cb[1].ap(),
                                                 accum_out=cb[2].ap()),
                   waits=[t_nm] + (tok_down if tt == 0 else []))
        t_v = ve(lambda h: h.tensor_scalar(cb[3].ap(), cb[2].ap(), inv_d, EPS, ALU.mult, ALU.add),
                 waits=[t_var])
        t_l1 = sc(lambda h: h.activation(out=cb[4].ap(), in_=cb[3].ap(), func=AF.Ln), waits=[t_v])
        t_rstd = sc(lambda h: h.activation(out=cb[4].ap(), in_=cb[4].ap(), func=AF.Exp, scale=-0.5),
                    waits=[t_l1])
        t_app = ve(lambda h, tt=tt: h.tensor_scalar(yffn3.ap()[:, tt, :], yffn3.ap()[:, tt, :],
                                                    cb[1].ap(), cb[4].ap(), ALU.add, ALU.mult),
                   waits=[t_rstd])
        t_gm = ve(lambda h, tt=tt: h.tensor_tensor(yffn3.ap()[:, tt, :], yffn3.ap()[:, tt, :],
                                                   wstage[0].ap(), ALU.mult),
                  waits=[t_app, t_lgm])
        t_bt = ve(lambda h, tt=tt: h.tensor_tensor(yffn3.ap()[:, tt, :], yffn3.ap()[:, tt, :],
                                                   wstage[1].ap(), ALU.mult_op1 if False else ALU.add),
                  waits=[t_gm, t_lbt])
        t_lr = dma(p3res.ap(), resid_ffn[tt * P:(tt + 1) * P, :],
                   waits=[t_fin] + (tok_down if tt == 0 else []))
        t_lxn = dma(p3xn.ap(), xn_spill[tt], waits=[t_fin] if t_fin else [])
        t_a1 = ve(lambda h, tt=tt: h.tensor_tensor(yffn3.ap()[:, tt, :], yffn3.ap()[:, tt, :],
                                                   p3res.ap(), ALU.add),
                  waits=[t_bt, t_lr])
        t_fin = ve(lambda h, tt=tt: h.tensor_tensor(yffn3.ap()[:, tt, :], yffn3.ap()[:, tt, :],
                                                    p3xn.ap(), ALU.add),
                   waits=[t_a1, t_lxn])
    tok_base = t_fin
    if debug_taps:
        for tt in range(TTH):
            dma(taps["tap_ffn"][tt], yffn3.ap()[:, tt, :], waits=[tok_base])

    if stop_after == "p3":
        return _finalize(nc, B)

    B.barrier_into(snapshot())
    t_selp = [None, None]
    t_cast4 = [None, None]
    g6mm = [None, None]
    ev_pb0 = {}
    tr4_last = None
    t_add_last = None
    for ps in range(2):
        sel_tok = []
        for ci in range(2):
            ch = ps * 2 + ci
            t_la = dma(yfullA.ap(), yg_out[ch].rearrange("(e p) l -> p e l", p=P),
                       waits=[(CCY[ch], 1), t_selp[0], t_selp[1]])
            t_lb = dma(yfullB.ap(), yg_out[ch + 4].rearrange("(e p) l -> p e l", p=P),
                       waits=[(CCY[ch + 4], 1)])
            t_s1 = ve(lambda h, ci=ci: h.tensor_single_scalar(
                          ysel[ci].ap().rearrange("p e l -> p (e l)"),
                          yfullA.ap().rearrange("p e l -> p (e l)"),
                          msel_sb.ap()[:, 0:1], ALU.mult),
                      waits=[t_la, now(PE)])
            t_sel = ve(lambda h, ci=ci: h.scalar_tensor_tensor(
                           ysel[ci].ap().rearrange("p e l -> p (e l)"),
                           yfullB.ap().rearrange("p e l -> p (e l)"),
                           msel_sb.ap()[:, 1:2],
                           ysel[ci].ap().rearrange("p e l -> p (e l)"),
                           ALU.mult, ALU.add),
                       waits=[t_s1, t_lb])
            t_selp[ci] = t_sel
            sel_tok.append(t_sel)
        for dt in range(DT):
            for hb in range(2):
                par = (dt * 2 + hb) % 2
                t_lo = dma(wbf[par].ap().rearrange("p (a b) -> p a b", a=DT),
                           ogb_spill.rearrange("g p d -> p g d")[:, hb * 16:(hb + 1) * 16,
                                                                dt * P:(dt + 1) * P],
                           waits=[g6mm[par]], eng=("sync" if par == 0 else "scalar"))
                for ci in range(2):
                    for eti in range(16):
                        et = hb * 16 + eti
                        t_mmo = pe(lambda h, eti=eti, et=et, par=par, ci=ci: h.matmul(
                                       pbv[ci][:, 0:LC], wbf[par].ap()[:, eti * P:(eti + 1) * P],
                                       ysel[ci].ap()[:, et, :],
                                       start=(et == 0), stop=(et == 31)),
                                   waits=([t_lo, sel_tok[ci], ev_pb0.get(ci)]
                                          if eti == 0 else ()),
                                   inc=(eti == 15))
                    g6mm[par] = t_mmo
            for ci in range(2):
                ch = ps * 2 + ci
                ev_pb0[ci] = sc(lambda h, ci=ci: h.activation(
                                    out=attnT.ap()[:, ci, :], in_=pbv[ci][:, 0:LC], func=AF.Copy),
                                waits=[g6mm[(dt * 2 + 1) % 2], g6mm[dt * 2 % 2], tr4_last])
                for k in range(2):
                    bank = 2 + k
                    t_tr = pe(lambda h, ci=ci, k=k, bank=bank: h.transpose(
                                  pbv[bank][:, 0:P], attnT.ap()[:, ci, k * P:(k + 1) * P],
                                  id_f32.ap()),
                              waits=[ev_pb0[ci], ev2.get(bank)])
                    ev2[bank] = ve(lambda h, ch=ch, k=k, dt=dt, bank=bank: h.tensor_tensor(
                                       yffn3.ap()[:, ch * 2 + k, dt * P:(dt + 1) * P],
                                       pbv[bank][:, 0:P],
                                       yffn3.ap()[:, ch * 2 + k, dt * P:(dt + 1) * P], ALU.add),
                                   waits=[t_tr, tok_base])
                    tr4_last = t_tr
        for ci in range(2):
            ch = ps * 2 + ci
            for k in range(2):
                dma(out[(ch * 256 + k * P):(ch * 256 + (k + 1) * P), :],
                    yffn3.ap()[:, ch * 2 + k, :], waits=[ev2[2], ev2[3]])

    return _finalize(nc, B)


def _finalize(nc, B):
    B.emit("sync", lambda h: None, waits=[(c, c.n) for c in B.chans if c.n > 0])
    with contextlib.ExitStack() as st:
        for c in B.chans:
            c.h = st.enter_context(nc.semaphore(c.name))
        block = st.enter_context(nc.Block())
        for eng in ENGS:
            def mk(eng):
                def emit_all(h):
                    for f in B.ops[eng]:
                        f(h)
                return emit_all
            getattr(block, eng)(mk(eng))
    return nc


def _prep_core(inputs, b, hf):
    asc = np.ascontiguousarray
    eh = slice(hf * EH, (hf + 1) * EH)
    f32 = np.float32
    resid = np.asarray(inputs["L0_attn_resid"], f32)
    pmaj = lambda v: asc(np.asarray(v, f32).reshape(-1, P).T)
    m = {
        "resid": asc(resid[b]),
        "resid_ffn": asc(resid[b, hf * TOKH:(hf + 1) * TOKH]),
        "in_wT": asc(np.asarray(inputs["L0_attn_in_w"], f32)[eh].T),
        "skip_wT": asc(np.asarray(inputs["L0_attn_skip_w"], f32)[eh].T),
        "wd1T": asc(np.asarray(inputs["L0_attn_wd1"], f32)[:, eh].T),
        "wd2T": asc(np.asarray(inputs["L0_attn_wd2"], f32)[eh].T),
        "wbcT": asc(np.concatenate([np.asarray(inputs["L0_attn_wB"], f32)[:, eh],
                                    np.asarray(inputs["L0_attn_wC"], f32)[:, eh]], 0).T),
        "convw": asc(np.asarray(inputs["L0_attn_conv_w"], f32)[eh]
                     .reshape(ET, P, 4).transpose(1, 0, 2).reshape(P, ET * 4)),
        "convb": pmaj(np.asarray(inputs["L0_attn_conv_b"], f32)[eh]),
        "wd2b": pmaj(np.asarray(inputs["L0_attn_wd2_b"], f32)[eh]),
        "alog": asc(np.asarray(inputs["L0_attn_A_log"], f32)[eh]
                    .reshape(ET, P, NST).transpose(1, 0, 2).reshape(P, ET * NST)),
        "wdvec": pmaj(np.asarray(inputs["L0_attn_W_D"], f32)[eh]),
        "normw": pmaj(np.asarray(inputs["L0_attn_norm_w"], f32)),
        "out_wT": asc(np.asarray(inputs["L0_attn_out_w"], f32).T),
        "gate_w": asc(np.asarray(inputs["L0_ffn_gate_w"], f32)),
        "up_w": asc(np.asarray(inputs["L0_ffn_up_w"], f32)),
        "down_w": asc(np.asarray(inputs["L0_ffn_down_w"], f32)),
        "exp_w": asc(np.asarray(inputs["L0_ffn_expert_weight"], f32)),
        "ln_g": asc(np.asarray(inputs["L0_ffn_ln_gamma"], f32).reshape(1, D)),
        "ln_b": asc(np.asarray(inputs["L0_ffn_ln_beta"], f32).reshape(1, D)),
        "hsel": asc(np.broadcast_to(
            np.array([1.0, 0.0] if hf == 0 else [0.0, 1.0], f32), (P, 2)).copy()),
    }
    return m


def kernel(**inputs):
    nc = build_nc()
    in_maps = [_prep_core(inputs, c // 2, c % 2) for c in range(8)]
    res = run_bass_kernel_spmd(nc, in_maps, core_ids=list(range(8)))
    outp = np.zeros((4, L, D), np.float32)
    for c in range(8):
        b, hf = c // 2, c % 2
        outp[b, hf * TOKH:(hf + 1) * TOKH, :] = np.asarray(res.results[c]["out"])
    return outp
